# version 16
# baseline (speedup 1.0000x reference)
import math
from contextlib import ExitStack

import numpy as np
import ml_dtypes

import concourse.bass as bass
import concourse.mybir as mybir
from concourse.bass_utils import run_bass_kernel_spmd

F32 = mybir.dt.float32
BF16 = mybir.dt.bfloat16
I32 = mybir.dt.int32
AF = mybir.ActivationFunctionType
ALU = mybir.AluOpType
AX = mybir.AxisListType

D = 2048
NSTR = 16
HD = 128
NH = 16
H2 = 5632
NJ = H2 // 128
NE = 16
TC = 512
NT = TC // 128
RS = 1.0 / math.sqrt(128.0)
LN_EPS = 1e-5

MODE = "single"


class Sem:
    def __init__(self, h):
        self.h = h
        self.n = 0

    def inc(self, inst, k=1):
        inst.then_inc(self.h, k)
        self.n += k
        return inst

    def bump(self, eng, k=1):
        eng.nop(nofuse=True).then_inc(self.h, k)
        self.n += k


class Env:
    def __init__(self, nc):
        self.nc = nc
        self.es = ExitStack()
        self._sems = {}

    def sem(self, name):
        if name not in self._sems:
            self._sems[name] = Sem(self.es.enter_context(self.nc.semaphore(name)))
        return self._sems[name]


class Phase:

    def __init__(self, env, name):
        self.env = env
        self.nc = env.nc
        self.name = name
        self.es = ExitStack()
        self._bases = {}

    def sbuf(self, name, shape, dt):
        return self.es.enter_context(self.nc.sbuf_tensor(f"{self.name}_{name}", shape, dt))

    def psum(self, name, shape, dt=F32):
        return self.es.enter_context(self.nc.psum_tensor(f"{self.name}_{name}", shape, dt))

    def sem(self, name):
        s = self.env.sem(name)
        if name not in self._bases:
            self._bases[name] = s.n
        return s

    def base(self, name):
        self.sem(name)
        return self._bases[name]

    def __enter__(self):
        return self

    def __exit__(self, *a):
        self.es.close()
        return False


def xchunk(xT, t0, tc):
    return xT[:, t0:t0 + tc].rearrange("(s p) c -> p s c", p=128)


def sb3(t, nstr):
    return t[:].rearrange("p (s c) -> p s c", s=nstr)


def stripes_ap(w):
    return w[:].rearrange("(s p) c -> p s c", p=128)


def phase_tables(env, T, dram):
    nc = env.nc
    with Phase(env, "p0") as ph:
        pos_i = ph.sbuf("pos_i", [64, T], I32)
        fid_i = ph.sbuf("fid_i", [64, 1], I32)
        fid_f = ph.sbuf("fid_f", [64, 1], F32)
        freq = ph.sbuf("freq", [64, 1], F32)
        ang = ph.sbuf("ang", [64, T], F32)
        ang2 = ph.sbuf("ang2", [64, T], F32)
        r = ph.sbuf("r", [64, T], F32)
        ki = ph.sbuf("ki", [64, T], I32)
        kf = ph.sbuf("kf", [64, T], F32)
        m1 = ph.sbuf("m1", [64, T], F32)
        m2 = ph.sbuf("m2", [64, T], F32)
        trig = ph.sbuf("trig", [64, T], F32)
        cosF = ph.sbuf("cosF", [128, T], F32)
        sinF = ph.sbuf("sinF", [128, T], F32)
        sg = ph.sem("G")
        sv = ph.sem("V")
        ss = ph.sem("S")
        so = ph.sem("O")

        def emit_reduce(v, src):
            sv.inc(v.tensor_scalar(ki[:], src[:], 1.0 / (2 * math.pi), None, ALU.mult))
            v.wait_ge(sv.h, sv.n)
            sv.inc(v.tensor_copy(kf[:], ki[:]))
            v.wait_ge(sv.h, sv.n)
            sv.inc(v.scalar_tensor_tensor(r[:], kf[:], -2 * math.pi, src[:], ALU.mult, ALU.add))
            v.wait_ge(sv.h, sv.n)
            sv.inc(v.tensor_scalar(m1[:], r[:], math.pi, None, ALU.is_gt))
            sv.inc(v.tensor_scalar(m2[:], r[:], -math.pi, None, ALU.is_lt))
            v.wait_ge(sv.h, sv.n)
            sv.inc(v.tensor_sub(m1[:], m1[:], m2[:]))
            v.wait_ge(sv.h, sv.n)
            sv.inc(v.scalar_tensor_tensor(r[:], m1[:], -2 * math.pi, r[:], ALU.mult, ALU.add))

        with nc.Block() as b:
            @b.gpsimd
            def _(g):
                sg.inc(g.iota(pos_i[:], pattern=[[1, T]], channel_multiplier=0))
                sg.inc(g.iota(fid_i[:], pattern=[[0, 1]], channel_multiplier=1))

            @b.vector
            def _(v):
                v.wait_ge(sg.h, sg.n)
                sv.inc(v.tensor_copy(fid_f[:], fid_i[:]))
                sv.inc(v.tensor_copy(ang[:], pos_i[:]))

            @b.scalar
            def _(sc):
                sc.wait_ge(sv.h, sv.n)
                ss.inc(sc.activation(freq[:], fid_f[:], AF.Exp,
                                     scale=-2.0 * math.log(10000.0) / 128.0))

            v_base = sv.n

            @b.vector
            def _(v):
                v.wait_ge(ss.h, ss.n)
                sv.inc(v.tensor_scalar(ang[:], ang[:], freq[:, 0:1], None, ALU.mult))
                v.wait_ge(sv.h, sv.n)
                sv.inc(v.tensor_scalar(ang2[:], ang[:], math.pi / 2, None, ALU.add))
                emit_reduce(v, ang)

            @b.scalar
            def _(sc):
                sc.wait_ge(sv.h, sv.n)
                ss.inc(sc.activation(trig[:], r[:], AF.Sin))

            @b.vector
            def _(v):
                v.wait_ge(ss.h, ss.n)
                sv.inc(v.tensor_scalar(sinF[0:64, :], trig[:], -1.0, None, ALU.mult))
                sv.inc(v.tensor_copy(sinF[64:128, :], trig[:]))
                emit_reduce(v, ang2)

            @b.scalar
            def _(sc):
                sc.wait_ge(sv.h, sv.n)
                ss.inc(sc.activation(trig[:], r[:], AF.Sin))

            @b.vector
            def _(v):
                v.wait_ge(ss.h, ss.n)
                sv.inc(v.tensor_copy(cosF[0:64, :], trig[:]))
                sv.inc(v.tensor_copy(cosF[64:128, :], trig[:]))

            @b.sync
            def _(s):
                s.wait_ge(sv.h, sv.n)
                so.inc(s.dma_start(dram["cosF"][:], cosF[:]), 16)
                so.inc(s.dma_start(dram["sinF"][:], sinF[:]), 16)
                s.wait_ge(so.h, so.n)


def phase_kv(env, T, dram):
    nc = env.nc
    nck = T // TC
    with Phase(env, "p2a") as ph:
        wk = ph.sbuf("wk", [128, NSTR * HD], BF16)
        wv = ph.sbuf("wv", [128, NSTR * HD], BF16)
        bkc = ph.sbuf("bkc", [128, 1], F32)
        bvr = ph.sbuf("bvr", [1, HD], F32)
        bvb = ph.sbuf("bvb", [128, HD], F32)
        ones_row = ph.sbuf("ones_row", [1, 128], F32)
        cosT = ph.sbuf("cosT", [128, T], BF16)
        sinT = ph.sbuf("sinT", [128, T], BF16)
        xb = [ph.sbuf(f"xb{i}", [128, NSTR * TC], BF16) for i in range(2)]
        kf = [ph.sbuf(f"kf{i}", [128, TC], F32) for i in range(2)]
        ksw = [ph.sbuf(f"ksw{i}", [128, TC], F32) for i in range(2)]
        kt1 = [ph.sbuf(f"kt1{i}", [128, TC], F32) for i in range(2)]
        krb = [ph.sbuf(f"krb{i}", [128, TC], BF16) for i in range(2)]
        vsb = [ph.sbuf(f"vsb{i}", [128, NT * HD], BF16) for i in range(2)]
        pk = [ph.psum(f"pk{i}", [128, TC]) for i in range(2)]
        pv = [ph.psum(f"pv{i}", [128, HD]) for i in range(4)]
        pb = ph.psum("pb", [128, HD])

        LDp = [ph.sem("LD0"), ph.sem("LD1")]
        WD = ph.sem("WD")
        MS = ph.sem("MS")
        TE = ph.sem("TE")
        SE = ph.sem("SE")
        VE = ph.sem("VE")
        SW = ph.sem("SW")
        ST = ph.sem("ST")
        CD = ph.sem("CD")
        bTE, bSE, bVE, bSW, bST, bCD = (ph.base(x) for x in
                                        ("TE", "SE", "VE", "SW", "ST", "CD"))
        bLD = [ph.base("LD0"), ph.base("LD1")]

        def cdw(eng, i):
            if i >= 1:
                eng.wait_ge(CD.h, bCD + i)

        with nc.Block() as b:
            @b.gpsimd
            def _(g):
                WD.inc(g.dma_start(sb3(wk, NSTR), stripes_ap(dram["Wk"])), 16)
                WD.inc(g.dma_start(sb3(wv, NSTR), stripes_ap(dram["Wv"])), 16)
                WD.inc(g.dma_start(bkc[:], dram["bk"][:]), 16)
                WD.inc(g.dma_start(bvr[:], dram["bv"][:]), 16)
                WD.inc(g.dma_start(cosT[:], dram["cosF"][:]), 16)
                WD.inc(g.dma_start(sinT[:], dram["sinF"][:]), 16)
                MS.inc(g.memset(ones_row[:], 1.0))
                for i in range(nck):
                    cdw(g, i - 1)
                    LDp[i % 2].inc(g.dma_start(sb3(xb[i % 2], NSTR),
                                               xchunk(dram["xT"], i * TC, TC)), 16)

            @b.tensor
            def _(t):
                t.wait_ge(WD.h, WD.n)
                t.wait_ge(MS.h, MS.n)
                TE.inc(t.matmul(pb[:], ones_row[:, :], bvr[:], start=True, stop=True))
                for i in range(nck):
                    cdw(t, i)
                    t.wait_ge(LDp[i % 2].h, bLD[i % 2] + 16 * (i // 2 + 1))
                    xv = sb3(xb[i % 2], NSTR)
                    for s in range(NSTR):
                        mm = t.matmul(pk[i % 2][:], wk[:, s * HD:(s + 1) * HD], xv[:, s, :],
                                      start=(s == 0), stop=(s == NSTR - 1))
                    TE.inc(mm)
                    for j in range(NT):
                        for s in range(NSTR):
                            mm = t.matmul(pv[j][:], xv[:, s, j * 128:(j + 1) * 128],
                                          wv[:, s * HD:(s + 1) * HD],
                                          start=(s == 0), stop=(s == NSTR - 1))
                        TE.inc(mm)

            @b.scalar
            def _(sc):
                for i in range(nck):
                    cdw(sc, i)
                    sc.wait_ge(TE.h, bTE + 1 + i * 5 + 1)
                    SE.inc(sc.activation(kf[i % 2][:], pk[i % 2][:], AF.Identity,
                                         bias=bkc[:, 0:1]))
                    sc.wait_ge(SE.h, bSE + i + 1)
                    SW.inc(sc.dma_start(ksw[i % 2][0:64, :], kf[i % 2][64:128, :]), 16)
                    SW.inc(sc.dma_start(ksw[i % 2][64:128, :], kf[i % 2][0:64, :]), 16)

            @b.vector
            def _(v):
                v.wait_ge(TE.h, bTE + 1)
                VE.inc(v.tensor_copy(bvb[:], pb[:]))
                v.wait_ge(VE.h, VE.n)
                for i in range(nck):
                    cdw(v, i)
                    cs = slice(i * TC, (i + 1) * TC)
                    v.wait_ge(SE.h, bSE + i + 1)
                    VE.inc(v.tensor_mul(kt1[i % 2][:], kf[i % 2][:], cosT[:, cs]))
                    v.wait_ge(SW.h, bSW + 32 * (i + 1))
                    VE.inc(v.tensor_mul(ksw[i % 2][:], ksw[i % 2][:], sinT[:, cs]))
                    v.wait_ge(VE.h, VE.n)
                    VE.inc(v.tensor_add(krb[i % 2][:], kt1[i % 2][:], ksw[i % 2][:]))
                    for j in range(NT):
                        v.wait_ge(TE.h, bTE + 1 + i * 5 + 1 + j + 1)
                        VE.inc(v.tensor_add(vsb[i % 2][:, j * HD:(j + 1) * HD],
                                            pv[j][:], bvb[:]))

            @b.sync
            def _(s):
                for i in range(nck):
                    s.wait_ge(VE.h, bVE + 1 + i * 7 + 3)
                    ST.inc(s.dma_start(dram["KR_bf"][:, i * TC:(i + 1) * TC],
                                       krb[i % 2][:]), 16)
                    s.wait_ge(VE.h, bVE + 1 + i * 7 + 7)
                    ST.inc(s.dma_start(
                        dram["Vn_bf"][i * TC:(i + 1) * TC, :].rearrange("(j p) c -> p j c", p=128),
                        vsb[i % 2][:].rearrange("p (j c) -> p j c", j=NT)), 16)
                    s.wait_ge(ST.h, bST + 32 * (i + 1))
                    CD.bump(s)



def phase_qctx(env, T, dram):
    nc = env.nc
    nck = T // TC
    NKT = T // 128
    VHN = NH
    with Phase(env, "p2b") as ph:
        wq = ph.sbuf("wq", [128, NSTR * D], BF16)
        bqc = ph.sbuf("bqc", [128, NSTR], F32)
        cosT = ph.sbuf("cosT", [128, T], BF16)
        sinT = ph.sbuf("sinT", [128, T], BF16)
        krT = ph.sbuf("krT", [128, T], BF16)
        vna = ph.sbuf("vna", [128, NKT * HD], BF16)
        ones_col = ph.sbuf("ones_col", [128, 1], F32)
        xb = [ph.sbuf(f"xb{i}", [128, NSTR * TC], BF16) for i in range(2)]
        qf = ph.sbuf("qf", [128, TC], F32)
        qsw = ph.sbuf("qsw", [128, TC], F32)
        qt1 = ph.sbuf("qt1", [128, TC], F32)
        qrb = ph.sbuf("qrb", [128, NH * TC], BF16)
        NEB = 4
        expb = [ph.sbuf(f"expb{i}", [128, TC], BF16) for i in range(NEB)]
        ctxo = ph.sbuf("ctxo", [128, NH * TC], BF16)
        denr = [ph.sbuf(f"denr{i}", [1, TC], F32) for i in range(NH)]
        dav = ph.sbuf("dav", [128, TC], F32)
        dag = ph.sbuf("dag", [128, TC], F32)
        pq = [ph.psum(f"pq{i}", [128, TC]) for i in range(2)]
        psc = [ph.psum(f"psc{i}", [128, TC]) for i in range(4)]
        pctx = ph.psum("pctx", [128, TC])
        pden = ph.psum("pden", [1, TC])

        LDp = [ph.sem("LD0"), ph.sem("LD1")]
        WD = ph.sem("WD")
        MS = ph.sem("MS")
        TE = ph.sem("TE")
        SE = ph.sem("SE")
        VE = ph.sem("VE")
        GP = ph.sem("GP")
        SW = ph.sem("SW")
        ST = ph.sem("ST")
        CD = ph.sem("CD")
        bTE, bSE, bVE, bGP, bSW, bST, bCD = (ph.base(x) for x in
                                             ("TE", "SE", "VE", "GP", "SW", "ST", "CD"))
        bLD = [ph.base("LD0"), ph.base("LD1")]

        TEC = NH + NH * (NKT + 2)
        SEC = NH + NH * (NKT + 2)
        VEC = NH * 3 + VHN * (NKT + 1)
        GPC = (NH - VHN) * NKT
        SWC = 32 * NH
        STC = 16 * (NH + 1)

        def cdw(eng, i):
            if i >= 1:
                eng.wait_ge(CD.h, bCD + i)

        def dacc_idx(i, h, j):
            return bVE + i * VEC + NH * 3 + h * (NKT + 1) + j + 1

        with nc.Block() as b:
            @b.gpsimd
            def _(g):
                WD.inc(g.dma_start(sb3(wq, NSTR), stripes_ap(dram["Wq"])), 16)
                WD.inc(g.dma_start(bqc[:], dram["bq2"][:]), 16)
                WD.inc(g.dma_start(cosT[:], dram["cosF"][:]), 16)
                WD.inc(g.dma_start(sinT[:], dram["sinF"][:]), 16)
                WD.inc(g.dma_start(krT[:], dram["KR_bf"][:]), 16)
                WD.inc(g.dma_start(vna[:].rearrange("p (j c) -> p j c", j=NKT),
                                   dram["Vn_bf"][:].rearrange("(j p) c -> p j c", p=128)), 16)
                MS.inc(g.memset(ones_col[:], 1.0))
                for i in range(nck):
                    cdw(g, i - 1)
                    LDp[i % 2].inc(g.dma_start(sb3(xb[i % 2], NSTR),
                                               xchunk(dram["xT"], i * TC, TC)), 16)


            @b.tensor
            def _(t):
                t.wait_ge(WD.h, WD.n)
                t.wait_ge(MS.h, MS.n)
                for i in range(nck):
                    cdw(t, i)
                    t.wait_ge(LDp[i % 2].h, bLD[i % 2] + 16 * (i // 2 + 1))
                    xv = sb3(xb[i % 2], NSTR)
                    tec = bTE + i * TEC
                    sec = bSE + i * SEC
                    vec = bVE + i * VEC
                    for o in range(NH):
                        if o >= 2:
                            t.wait_ge(SE.h, sec + o - 1)
                        for s in range(NSTR):
                            mm = t.matmul(pq[o % 2][:],
                                          wq[:, s * D + o * HD:s * D + (o + 1) * HD],
                                          xv[:, s, :], start=(s == 0), stop=(s == NSTR - 1))
                        TE.inc(mm)
                    for h in range(NH):
                        t.wait_ge(VE.h, vec + 3 * (h + 1))
                        seh = sec + NH + h * (NKT + 2)
                        teh = tec + NH + h * (NKT + 2)
                        for j in range(NKT):
                            if j >= 4:
                                t.wait_ge(SE.h, seh + j - 3)
                            TE.inc(t.matmul(psc[j % 4][:], krT[:, j * 128:(j + 1) * 128],
                                            qrb[:, h * TC:(h + 1) * TC],
                                            start=True, stop=True))
                            if j >= 1:
                                t.wait_ge(SE.h, seh + j)
                                t.matmul(pctx[:], vna[:, (j - 1) * HD:j * HD],
                                         expb[(j - 1) % NEB][:],
                                         start=(j == 1), stop=False)
                        t.wait_ge(SE.h, seh + NKT)
                        TE.inc(t.matmul(pctx[:], vna[:, (NKT - 1) * HD:NKT * HD],
                                        expb[(NKT - 1) % NEB][:], start=False, stop=True))
                        if h >= 1:
                            t.wait_ge(SE.h, seh)
                        t.wait_ge(VE.h, dacc_idx(i, h, NKT))
                        TE.inc(t.matmul(pden[:], ones_col[:, :], dav[:],
                                        start=True, stop=True))

            @b.scalar
            def _(sc):
                for i in range(nck):
                    cdw(sc, i)
                    tec = bTE + i * TEC
                    sec = bSE + i * SEC
                    vec = bVE + i * VEC
                    for o in range(NH):
                        sc.wait_ge(TE.h, tec + o + 1)
                        if o >= 1:
                            sc.wait_ge(VE.h, vec + 3 * o)
                            sc.wait_ge(SW.h, bSW + i * SWC + 32 * o)
                        SE.inc(sc.activation(qf[:], pq[o % 2][:], AF.Identity,
                                             bias=bqc[:, o:o + 1]))
                        sc.wait_ge(SE.h, SE.n)
                        SW.inc(sc.dma_start(qsw[0:64, :], qf[64:128, :]), 16)
                        SW.inc(sc.dma_start(qsw[64:128, :], qf[0:64, :]), 16)
                    for h in range(NH):
                        seh = sec + NH + h * (NKT + 2)
                        teh = tec + NH + h * (NKT + 2)
                        for j in range(NKT):
                            sc.wait_ge(TE.h, teh + j + 1)
                            if j >= NEB:
                                sc.wait_ge(VE.h, dacc_idx(i, h, j - NEB))
                            SE.inc(sc.activation(expb[j % NEB][:], psc[j % 4][:],
                                                 AF.Exp, scale=RS))
                        sc.wait_ge(TE.h, teh + NKT + 1)
                        SE.inc(sc.copy(ctxo[:, h * TC:(h + 1) * TC], pctx[:]))
                        sc.wait_ge(TE.h, teh + NKT + 2)
                        SE.inc(sc.copy(denr[h][:], pden[:]))

            @b.vector
            def _(v):
                for i in range(nck):
                    cdw(v, i)
                    cs = slice(i * TC, (i + 1) * TC)
                    sec = bSE + i * SEC
                    vec = bVE + i * VEC
                    swc = bSW + i * SWC
                    for o in range(NH):
                        if o >= 1:
                            v.wait_ge(VE.h, vec + 3 * o)
                        v.wait_ge(SE.h, sec + o + 1)
                        v.wait_ge(SW.h, swc + 32 * (o + 1))
                        VE.inc(v.tensor_mul(qt1[:], qf[:], cosT[:, cs]))
                        VE.inc(v.tensor_mul(qsw[:], qsw[:], sinT[:, cs]))
                        v.wait_ge(VE.h, VE.n)
                        VE.inc(v.tensor_add(qrb[:, o * TC:(o + 1) * TC], qt1[:], qsw[:]))
                    for h in range(VHN):
                        seh = sec + NH + h * (NKT + 2)
                        if h > 0:
                            v.wait_ge(TE.h, bTE + i * TEC + NH + h * (NKT + 2))
                        elif i >= 1:
                            v.wait_ge(TE.h, bTE + (i - 1) * TEC + NH + NH * (NKT + 2))
                        for j in range(NKT):
                            acc = dav if j % 2 == 0 else dag
                            v.wait_ge(SE.h, seh + j + 1)
                            if j < 2:
                                VE.inc(v.tensor_copy(acc[:], expb[j % NEB][:]))
                            else:
                                v.wait_ge(VE.h, VE.n - 1)
                                VE.inc(v.tensor_add(acc[:], acc[:], expb[j % NEB][:]))
                        v.wait_ge(VE.h, VE.n)
                        VE.inc(v.tensor_add(dav[:], dav[:], dag[:]))

            @b.sync
            def _(s):
                for i in range(nck):
                    sec = bSE + i * SEC
                    for h in range(NH):
                        s.wait_ge(SE.h, sec + NH + h * (NKT + 2) + NKT + 2)
                        ST.inc(s.dma_start(dram["den"][h:h + 1, i * TC:(i + 1) * TC],
                                           denr[h][:]), 16)
                    s.wait_ge(SE.h, sec + SEC)
                    ST.inc(s.dma_start(dram["ctxT_bf"][:, i * TC:(i + 1) * TC]
                                       .rearrange("(h p) c -> p h c", p=128),
                                       ctxo[:].rearrange("p (h c) -> p h c", h=NH)), 16)
                    s.wait_ge(ST.h, bST + STC * (i + 1))
                    CD.bump(s)


def phase_attnout(env, T, dram):
    nc = env.nc
    nck = T // TC
    with Phase(env, "p2c") as ph:
        wo = ph.sbuf("wo", [128, NSTR * D], BF16)
        boc = ph.sbuf("boc", [128, NSTR], F32)
        ones_row = ph.sbuf("ones_row", [1, 128], F32)
        ones_col = ph.sbuf("ones_col", [128, 1], F32)
        cb = ph.sbuf("cb0", [128, NH * TC], BF16)
        denb = [ph.sbuf(f"denb{h}", [1, TC], F32) for h in range(NH)]
        xf = ph.sbuf("xf0", [128, NSTR * TC], F32)
        rbf = [ph.sbuf(f"rbf{i}", [128, TC], F32) for i in range(2)]
        y = ph.sbuf("y", [128, NSTR * TC], F32)
        mrow = ph.sbuf("mrow", [1, TC], F32)
        qrow = ph.sbuf("qrow", [1, TC], F32)
        m2row = ph.sbuf("m2row", [1, TC], F32)
        varrow = ph.sbuf("varrow", [1, TC], F32)
        rrow = ph.sbuf("rrow", [1, TC], F32)
        lnrow = m2row
        srow = qrow
        rbv = ph.sbuf("rbv", [128, TC], F32)
        shb = ph.sbuf("shb", [128, TC], F32)
        tmp = ph.sbuf("tmp", [128, TC], F32)
        sq = tmp
        po = [ph.psum(f"po{i}", [128, TC]) for i in range(2)]
        prb = [ph.psum(f"prb{i}", [128, TC]) for i in range(2)]
        ps1 = ph.psum("ps1", [1, TC])
        ps2 = ph.psum("ps2", [1, TC])
        pr = ph.psum("pr", [128, TC])
        psh = ph.psum("psh", [128, TC])

        LD = ph.sem("LD0")
        WD = ph.sem("WD")
        MS = ph.sem("MS")
        TE = ph.sem("TE")
        SE = ph.sem("SE")
        VE = ph.sem("VE")
        ST = ph.sem("ST")
        CD = ph.sem("CD")
        bTE, bSE, bVE, bST, bCD = (ph.base(x) for x in ("TE", "SE", "VE", "ST", "CD"))
        bLD = ph.base("LD0")

        TEC = NH + NSTR + NSTR + 2
        SEC = 2 * NH + NSTR + NSTR + 2 + 2
        VEC = NH + NSTR + 5 + 1 + 2 * NSTR
        LDC = 16 * (2 + NH)

        def cdw(eng, i):
            if i >= 1:
                eng.wait_ge(CD.h, bCD + i)

        with nc.Block() as b:
            @b.gpsimd
            def _(g):
                WD.inc(g.dma_start(sb3(wo, NSTR), stripes_ap(dram["Wo"])), 16)
                WD.inc(g.dma_start(boc[:], dram["bo2"][:]), 16)
                g.memset(ones_row[:], 1.0)
                MS.inc(g.memset(ones_col[:], 1.0))
                for i in range(nck):
                    if i >= 1:
                        g.wait_ge(TE.h, bTE + (i - 1) * TEC + NH + NSTR)
                        g.wait_ge(VE.h, bVE + (i - 1) * VEC + NH + NSTR)
                    cs = slice(i * TC, (i + 1) * TC)
                    LD.inc(g.dma_start(
                        cb[:].rearrange("p (h c) -> p h c", h=NH),
                        dram["ctxT_bf"][:, cs].rearrange("(h p) c -> p h c", p=128)), 16)
                    LD.inc(g.dma_start(sb3(xf, NSTR), xchunk(dram["xT"], i * TC, TC)), 16)
                    for h in range(NH):
                        LD.inc(g.dma_start(denb[h][:], dram["den"][h:h + 1, cs]), 16)

            @b.tensor
            def _(t):
                t.wait_ge(WD.h, WD.n)
                t.wait_ge(MS.h, MS.n)
                for i in range(nck):
                    cdw(t, i)
                    tec = bTE + i * TEC
                    sec = bSE + i * SEC
                    vec = bVE + i * VEC
                    t.wait_ge(LD.h, bLD + LDC * (i + 1))
                    for h in range(NH):
                        if h >= 2:
                            t.wait_ge(SE.h, sec + 2 * h - 3)
                        TE.inc(t.matmul(prb[h % 2][:], ones_row[:, :], denb[h][:],
                                        start=True, stop=True))
                    t.wait_ge(VE.h, vec + NH)
                    for o in range(NSTR):
                        if o >= 2:
                            t.wait_ge(SE.h, sec + 2 * NH + o - 1)
                        for s in range(NSTR):
                            mm = t.matmul(po[o % 2][:],
                                          wo[:, s * D + o * HD:s * D + (o + 1) * HD],
                                          cb[:, s * TC:(s + 1) * TC],
                                          start=(s == 0), stop=(s == NSTR - 1))
                        TE.inc(mm)
                    for s in range(NSTR):
                        t.wait_ge(VE.h, vec + NH + s + 1)
                        t.matmul(ps1[:], ones_col[:, :], y[:, s * TC:(s + 1) * TC],
                                 start=(s == 0), stop=(s == NSTR - 1))
                        t.wait_ge(SE.h, sec + 2 * NH + NSTR + s + 1)
                        TE.inc(t.matmul(ps2[:], ones_col[:, :], sq[:],
                                        start=(s == 0), stop=(s == NSTR - 1)))
                    t.wait_ge(SE.h, sec + 2 * NH + NSTR + NSTR + 2)
                    TE.inc(t.matmul(pr[:], ones_row[:, :], rrow[:], start=True, stop=True))
                    t.wait_ge(VE.h, vec + NH + NSTR + 6)
                    TE.inc(t.matmul(psh[:], ones_row[:, :], srow[:], start=True, stop=True))

            @b.scalar
            def _(sc):
                for i in range(nck):
                    cdw(sc, i)
                    tec = bTE + i * TEC
                    sec = bSE + i * SEC
                    vec = bVE + i * VEC
                    for h in range(NH):
                        sc.wait_ge(TE.h, tec + h + 1)
                        if h >= 2:
                            sc.wait_ge(VE.h, vec + h - 1)
                        SE.inc(sc.activation(rbf[h % 2][:], prb[h % 2][:], AF.Ln))
                        sc.wait_ge(SE.h, SE.n)
                        SE.inc(sc.activation(rbf[h % 2][:], rbf[h % 2][:],
                                             AF.Exp, scale=-1.0))
                    for o in range(NSTR):
                        sc.wait_ge(TE.h, tec + NH + o + 1)
                        SE.inc(sc.activation(y[:, o * TC:(o + 1) * TC], po[o % 2][:],
                                             AF.Identity, bias=boc[:, o:o + 1]))
                    for s in range(NSTR):
                        sc.wait_ge(VE.h, vec + NH + s + 1)
                        if s >= 1:
                            sc.wait_ge(TE.h, tec + NH + NSTR + s)
                        SE.inc(sc.activation(sq[:], y[:, s * TC:(s + 1) * TC], AF.Square))
                    sc.wait_ge(VE.h, vec + NH + NSTR + 5)
                    SE.inc(sc.activation(lnrow[:], varrow[:], AF.Ln))
                    sc.wait_ge(SE.h, SE.n)
                    SE.inc(sc.activation(rrow[:], lnrow[:], AF.Exp, scale=-0.5))
                    sc.wait_ge(TE.h, tec + NH + NSTR + NSTR + 1)
                    SE.inc(sc.copy(rbv[:], pr[:]))
                    sc.wait_ge(TE.h, tec + NH + NSTR + NSTR + 2)
                    SE.inc(sc.copy(shb[:], psh[:]))

            @b.vector
            def _(v):
                for i in range(nck):
                    cdw(v, i)
                    tec = bTE + i * TEC
                    sec = bSE + i * SEC
                    vec = bVE + i * VEC
                    for h in range(NH):
                        v.wait_ge(SE.h, sec + 2 * h + 2)
                        VE.inc(v.tensor_mul(cb[:, h * TC:(h + 1) * TC],
                                            cb[:, h * TC:(h + 1) * TC],
                                            rbf[h % 2][:]))
                    for s in range(NSTR):
                        v.wait_ge(SE.h, sec + 2 * NH + s + 1)
                        VE.inc(v.tensor_add(y[:, s * TC:(s + 1) * TC],
                                            y[:, s * TC:(s + 1) * TC],
                                            sb3(xf, NSTR)[:, s, :]))
                    v.wait_ge(TE.h, tec + NH + NSTR + NSTR)
                    VE.inc(v.tensor_scalar(mrow[:], ps1[:], 1.0 / D, None, ALU.mult))
                    VE.inc(v.tensor_scalar(qrow[:], ps2[:], 1.0 / D, None, ALU.mult))
                    v.wait_ge(VE.h, VE.n)
                    VE.inc(v.tensor_mul(m2row[:], mrow[:], mrow[:]))
                    v.wait_ge(VE.h, VE.n)
                    VE.inc(v.tensor_sub(varrow[:], qrow[:], m2row[:]))
                    v.wait_ge(VE.h, VE.n)
                    VE.inc(v.tensor_scalar(varrow[:], varrow[:], LN_EPS, None, ALU.add))
                    v.wait_ge(SE.h, sec + 2 * NH + NSTR + NSTR + 2)
                    v.wait_ge(VE.h, VE.n)
                    VE.inc(v.tensor_mul(srow[:], mrow[:], rrow[:]))
                    v.wait_ge(SE.h, sec + SEC)
                    for s in range(NSTR):
                        VE.inc(v.tensor_mul(tmp[:], y[:, s * TC:(s + 1) * TC], rbv[:]))
                        v.wait_ge(VE.h, VE.n)
                        VE.inc(v.tensor_sub(y[:, s * TC:(s + 1) * TC], tmp[:], shb[:]))
                        v.wait_ge(VE.h, VE.n)

            @b.sync
            def _(s):
                for i in range(nck):
                    s.wait_ge(VE.h, bVE + (i + 1) * VEC)
                    ST.inc(s.dma_start(xchunk(dram["x1"], i * TC, TC),
                                       sb3(y, NSTR)), 16)
                    s.wait_ge(ST.h, bST + 16 * (i + 1))
                    CD.bump(s)



def phase_gamma(env, T, dram):
    nc = env.nc
    mats = [("w1T", D * H2), ("w2T", D * H2), ("w3T", H2 * D)]
    FB = 4096
    with Phase(env, "p3a") as ph:
        wst = [ph.sbuf(f"wst{i}", [128, FB], F32) for i in range(2)]
        acc = ph.sbuf("acc", [128, 1], F32)
        part = ph.sbuf("part", [128, 1], F32)
        grow = ph.sbuf("grow", [1, 1], F32)
        ones_col = ph.sbuf("ones_col", [128, 1], F32)
        ones_row = ph.sbuf("ones_row", [1, 128], F32)
        thr = ph.sbuf("thr", [128, 8], F32)
        pg = ph.psum("pg", [1, 1])
        pbc = ph.psum("pbc", [128, 1])
        LDp = [ph.sem("LD0"), ph.sem("LD1")]
        MS = ph.sem("MS")
        TE = ph.sem("TE")
        VE = ph.sem("VE")
        ST = ph.sem("ST")
        bTE, bVE, bST = (ph.base(x) for x in ("TE", "VE", "ST"))
        bLD = [ph.base("LD0"), ph.base("LD1")]

        nchunks = [n // (128 * FB) for _, n in mats]
        flat_m = [m_ for m_, (_, n) in enumerate(mats) for _i in range(nchunks[m_])]

        def ve_after_acc(g):
            return bVE + 2 * (g + 1) + 4 * flat_m[g]

        with nc.Block() as b:
            @b.gpsimd
            def _(g):
                g.memset(ones_col[:], 1.0)
                g.memset(thr[:], 0.0)
                MS.inc(g.memset(ones_row[:], 1.0))
                tot = 0
                for m_, (wname, n) in enumerate(mats):
                    for i in range(nchunks[m_]):
                        if tot >= 2:
                            g.wait_ge(VE.h, ve_after_acc(tot - 2))
                        LDp[tot % 2].inc(g.dma_start(
                            wst[tot % 2][:],
                            dram[wname][:].rearrange("r c -> (r c)")
                            .rearrange("(p k) -> p k", p=128)[:, i * FB:(i + 1) * FB]), 16)
                        tot += 1

            @b.vector
            def _(v):
                tot = 0
                for m_, (wname, n) in enumerate(mats):
                    for i in range(nchunks[m_]):
                        v.wait_ge(LDp[tot % 2].h, bLD[tot % 2] + 16 * (tot // 2 + 1))
                        VE.inc(v.reduce_sum(part[:], wst[tot % 2][:], axis=AX.X,
                                            apply_absolute_value=True))
                        v.wait_ge(VE.h, VE.n)
                        if i == 0:
                            VE.inc(v.tensor_copy(acc[:], part[:]))
                        else:
                            VE.inc(v.tensor_add(acc[:], acc[:], part[:]))
                        v.wait_ge(VE.h, VE.n)
                        tot += 1
                    v.wait_ge(TE.h, bTE + 2 * m_ + 1)
                    VE.inc(v.tensor_scalar(grow[:], pg[:], 0.5 / n, None, ALU.mult))
                    v.wait_ge(VE.h, VE.n)
                    VE.inc(v.tensor_scalar(grow[:], grow[:], 0.5e-8, None, ALU.add))
                    v.wait_ge(TE.h, bTE + 2 * m_ + 2)
                    VE.inc(v.tensor_copy(thr[:, 2 * m_:2 * m_ + 1], pbc[:]))
                    VE.inc(v.tensor_scalar(thr[:, 2 * m_ + 1:2 * m_ + 2], pbc[:], -1.0,
                                           None, ALU.mult))

            @b.tensor
            def _(t):
                t.wait_ge(MS.h, MS.n)
                for m_, (wname, n) in enumerate(mats):
                    upto = sum(nchunks[:m_ + 1])
                    t.wait_ge(VE.h, ve_after_acc(upto - 1))
                    TE.inc(t.matmul(pg[:], ones_col[:, :], acc[:], start=True, stop=True))
                    t.wait_ge(VE.h, ve_after_acc(upto - 1) + 2)
                    TE.inc(t.matmul(pbc[:], ones_row[:, :], grow[:], start=True, stop=True))

            @b.sync
            def _(s):
                s.wait_ge(VE.h, VE.n)
                ST.inc(s.dma_start(dram["thr"][:], thr[:]), 16)
                s.wait_ge(ST.h, ST.n)

def phase_ffn1(env, T, dram):
    nc = env.nc
    nck = T // TC
    NQ = 4
    JQ = NJ // NQ
    CQ = JQ * 128
    QVC = 3 * 2 * NSTR
    with Phase(env, "p3b") as ph:
        q1 = ph.sbuf("q1", [128, NSTR * CQ], BF16)
        q2 = ph.sbuf("q2", [128, NSTR * CQ], BF16)
        wstg = [ph.sbuf(f"wstg{i}", [128, CQ], F32) for i in range(2)]
        tge = ph.sbuf("tge", [128, CQ], F32)
        tle = ph.sbuf("tle", [128, CQ], F32)
        thrc = ph.sbuf("thrc", [128, 8], F32)
        xb = [ph.sbuf(f"xb{i}", [128, NSTR * TC], BF16) for i in range(2)]
        sgm = ph.sbuf("sgm", [128, TC], F32)
        hq2 = [ph.sbuf(f"hq{i}", [128, JQ * TC], BF16) for i in range(2)]
        psA = [ph.psum(f"psA{i}", [128, TC]) for i in range(4)]
        psB = [ph.psum(f"psB{i}", [128, TC]) for i in range(4)]
        WLp = [ph.sem("WL0"), ph.sem("WL1")]
        LDp = [ph.sem("LD0"), ph.sem("LD1")]
        TE = ph.sem("TE")
        SE = ph.sem("SE")
        VE = ph.sem("VE")
        ST = ph.sem("ST")
        CD = ph.sem("CD")
        WD = ph.sem("WD")
        bTE, bSE, bVE, bST, bCD = (ph.base(x) for x in
                                   ("TE", "SE", "VE", "ST", "CD"))
        bWL = [ph.base("WL0"), ph.base("WL1")]
        bLD = [ph.base("LD0"), ph.base("LD1")]

        NIT = NQ * nck

        def itq(it):
            return it // nck, it % nck

        def vef(it):
            q, i = itq(it)
            return bVE + QVC * (q + 1) + JQ * it

        def mul_done(gj):
            return vef(gj // JQ) + gj % JQ + 1

        with nc.Block() as b:
            @b.gpsimd
            def _(g):
                WD.inc(g.dma_start(thrc[:], dram["thr"][:]), 16)
                for it in range(NIT):
                    q, i = itq(it)
                    if i == 0:
                        if q >= 1:
                            g.wait_ge(CD.h, bCD + q * nck)
                        for m, wname in enumerate(("w1T", "w2T")):
                            for s in range(NSTR):
                                k = m * NSTR + s
                                if k >= 2:
                                    g.wait_ge(VE.h, bVE + QVC * q + JQ * (q * nck)
                                              + 3 * (k - 1))
                                fk = q * 2 * NSTR + k
                                WLp[fk % 2].inc(g.dma_start(
                                    wstg[k % 2][:],
                                    dram[wname][:, q * CQ:(q + 1) * CQ]
                                    .rearrange("(s p) c -> p s c", p=128)[:, s, :]), 16)
                    if it >= 2:
                        g.wait_ge(CD.h, bCD + it - 1)
                    LDp[it % 2].inc(g.dma_start(sb3(xb[it % 2], NSTR),
                                                xchunk(dram["x1"], i * TC, TC)), 16)

            @b.vector
            def _(v):
                v.wait_ge(WD.h, WD.n)
                for it in range(NIT):
                    q, i = itq(it)
                    if i == 0:
                        for k in range(2 * NSTR):
                            m, s = k // NSTR, k % NSTR
                            qd = (q1, q2)[m]
                            fk = q * 2 * NSTR + k
                            v.wait_ge(WLp[fk % 2].h, bWL[fk % 2] + 16 * (fk // 2 + 1))
                            v.wait_ge(VE.h, VE.n)
                            tc_ = 2 * m
                            VE.inc(v.tensor_scalar(tge[:], wstg[k % 2][:],
                                                   thrc[:, tc_:tc_ + 1], None, ALU.is_ge))
                            VE.inc(v.tensor_scalar(tle[:], wstg[k % 2][:],
                                                   thrc[:, tc_ + 1:tc_ + 2], None, ALU.is_le))
                            v.wait_ge(VE.h, VE.n)
                            VE.inc(v.tensor_sub(qd[:, s * CQ:(s + 1) * CQ], tge[:], tle[:]))
                    if it >= 2:
                        v.wait_ge(CD.h, bCD + it - 1)
                    for j in range(JQ):
                        gj = it * JQ + j
                        v.wait_ge(SE.h, bSE + it * JQ + j + 1)
                        v.wait_ge(TE.h, bTE + it * 2 * JQ + 2 * j + 2)
                        VE.inc(v.tensor_mul(hq2[it % 2][:, j * TC:(j + 1) * TC], sgm[:],
                                            psB[gj % 4][:]))

            @b.tensor
            def _(t):
                t.wait_ge(WD.h, WD.n)
                for it in range(NIT):
                    q, i = itq(it)
                    t.wait_ge(LDp[it % 2].h, bLD[it % 2] + 16 * (it // 2 + 1))
                    if i == 0:
                        t.wait_ge(VE.h, bVE + QVC * (q + 1) + JQ * it)
                    xv = sb3(xb[it % 2], NSTR)
                    for j in range(JQ):
                        gj = it * JQ + j
                        if gj >= 4:
                            t.wait_ge(SE.h, bSE + gj - 3)
                            t.wait_ge(VE.h, mul_done(gj - 4))
                        for s in range(NSTR):
                            mmA = t.matmul(psA[gj % 4][:],
                                           q1[:, s * CQ + j * 128:s * CQ + (j + 1) * 128],
                                           xv[:, s, :], start=(s == 0), stop=(s == NSTR - 1))
                        TE.inc(mmA)
                        for s in range(NSTR):
                            mmB = t.matmul(psB[gj % 4][:],
                                           q2[:, s * CQ + j * 128:s * CQ + (j + 1) * 128],
                                           xv[:, s, :], start=(s == 0), stop=(s == NSTR - 1))
                        TE.inc(mmB)

            @b.scalar
            def _(sc):
                for it in range(NIT):
                    q, i = itq(it)
                    for j in range(JQ):
                        gj = it * JQ + j
                        sc.wait_ge(TE.h, bTE + it * 2 * JQ + 2 * j + 1)
                        if gj >= 1:
                            sc.wait_ge(VE.h, mul_done(gj - 1))
                        SE.inc(sc.activation(sgm[:], psA[gj % 4][:], AF.Sigmoid))

            @b.sync
            def _(s):
                for it in range(NIT):
                    q, i = itq(it)
                    s.wait_ge(VE.h, vef(it) + JQ)
                    ST.inc(s.dma_start(
                        dram["hid_bf"][q * CQ:(q + 1) * CQ, i * TC:(i + 1) * TC]
                        .rearrange("(j p) c -> p j c", p=128),
                        hq2[it % 2][:].rearrange("p (j c) -> p j c", j=JQ)), 16)
                    s.wait_ge(ST.h, bST + 16 * (it + 1))
                    CD.bump(s)


def phase_ffn2(env, T, dram):
    nc = env.nc
    nck = T // TC
    NHF = 2
    CH = D // NHF
    OT = CH // 128
    QVC = 3 * NJ
    with Phase(env, "p3c") as ph:
        q3 = ph.sbuf("q3", [128, NJ * CH], BF16)
        wstg = [ph.sbuf(f"wstg{i}", [128, CH], F32) for i in range(2)]
        tge = ph.sbuf("tge", [128, CH], F32)
        tle = ph.sbuf("tle", [128, CH], F32)
        thrc = ph.sbuf("thrc", [128, 8], F32)
        hb = ph.sbuf("hb", [128, NJ * TC], BF16)
        xf = ph.sbuf("xf", [128, OT * TC], F32)
        y = ph.sbuf("y", [128, OT * TC], F32)
        po = [ph.psum(f"po{i}", [128, TC]) for i in range(4)]
        WLp = [ph.sem("WL0"), ph.sem("WL1")]
        LDp2 = [ph.sem("LD0"), ph.sem("LD1")]
        TE = ph.sem("TE")
        SE = ph.sem("SE")
        VE = ph.sem("VE")
        ST = ph.sem("ST")
        CD = ph.sem("CD")
        WD = ph.sem("WD")
        bTE, bSE, bVE, bST, bCD = (ph.base(x) for x in
                                   ("TE", "SE", "VE", "ST", "CD"))
        bWL = [ph.base("WL0"), ph.base("WL1")]
        bLD2 = [ph.base("LD0"), ph.base("LD1")]

        NIT = NHF * nck

        def itq(it):
            return it // nck, it % nck

        def vef(it):
            half, i = itq(it)
            return bVE + QVC * (half + 1) + OT * it

        with nc.Block() as b:
            @b.gpsimd
            def _(g):
                WD.inc(g.dma_start(thrc[:], dram["thr"][:]), 16)
                for it in range(NIT):
                    half, i = itq(it)
                    if i == 0:
                        if half >= 1:
                            g.wait_ge(CD.h, bCD + half * nck)
                        for s in range(NJ):
                            if s >= 2:
                                g.wait_ge(VE.h, bVE + QVC * half + OT * (half * nck)
                                          + 3 * (s - 1))
                            fk = half * NJ + s
                            WLp[fk % 2].inc(g.dma_start(
                                wstg[s % 2][:],
                                dram["w3T"][:, half * CH:(half + 1) * CH]
                                .rearrange("(s p) c -> p s c", p=128)[:, s, :]), 16)
                    if it >= 1:
                        g.wait_ge(TE.h, bTE + it * OT)
                        g.wait_ge(VE.h, vef(it - 1) + OT)
                    LDp2[0].inc(g.dma_start(hb[:].rearrange("p (s c) -> p s c", s=NJ),
                                            dram["hid_bf"][:, i * TC:(i + 1) * TC]
                                            .rearrange("(s p) c -> p s c", p=128)), 16)
                    LDp2[1].inc(g.dma_start(
                        xf[:].rearrange("p (s c) -> p s c", s=OT),
                        dram["x1"][half * CH:(half + 1) * CH, i * TC:(i + 1) * TC]
                        .rearrange("(s p) c -> p s c", p=128)), 16)

            @b.vector
            def _(v):
                v.wait_ge(WD.h, WD.n)
                for it in range(NIT):
                    half, i = itq(it)
                    if i == 0:
                        for s in range(NJ):
                            fk = half * NJ + s
                            v.wait_ge(WLp[fk % 2].h, bWL[fk % 2] + 16 * (fk // 2 + 1))
                            v.wait_ge(VE.h, VE.n)
                            VE.inc(v.tensor_scalar(tge[:], wstg[s % 2][:],
                                                   thrc[:, 4:5], None, ALU.is_ge))
                            VE.inc(v.tensor_scalar(tle[:], wstg[s % 2][:],
                                                   thrc[:, 5:6], None, ALU.is_le))
                            v.wait_ge(VE.h, VE.n)
                            VE.inc(v.tensor_sub(q3[:, s * CH:(s + 1) * CH], tge[:], tle[:]))
                    if it >= 1:
                        v.wait_ge(CD.h, bCD + it)
                    v.wait_ge(LDp2[1].h, bLD2[1] + 16 * (it + 1))
                    for o in range(OT):
                        v.wait_ge(SE.h, bSE + it * OT + o + 1)
                        VE.inc(v.tensor_add(y[:, o * TC:(o + 1) * TC],
                                            y[:, o * TC:(o + 1) * TC],
                                            xf[:].rearrange("p (s c) -> p s c", s=OT)[:, o, :]))

            @b.tensor
            def _(t):
                t.wait_ge(WD.h, WD.n)
                for it in range(NIT):
                    half, i = itq(it)
                    t.wait_ge(LDp2[0].h, bLD2[0] + 16 * (it + 1))
                    if i == 0:
                        t.wait_ge(VE.h, bVE + QVC * (half + 1) + OT * it)
                    hv = hb[:].rearrange("p (s c) -> p s c", s=NJ)
                    for o in range(OT):
                        go = it * OT + o
                        if go >= 4:
                            t.wait_ge(SE.h, bSE + go - 3)
                        for s in range(NJ):
                            mm = t.matmul(po[go % 4][:],
                                          q3[:, s * CH + o * 128:s * CH + (o + 1) * 128],
                                          hv[:, s, :], start=(s == 0), stop=(s == NJ - 1))
                        TE.inc(mm)

            @b.scalar
            def _(sc):
                for it in range(NIT):
                    half, i = itq(it)
                    if it >= 1:
                        sc.wait_ge(CD.h, bCD + it)
                    for o in range(OT):
                        go = it * OT + o
                        sc.wait_ge(TE.h, bTE + go + 1)
                        if o >= 1:
                            sc.wait_ge(VE.h, vef(it) + o)
                        SE.inc(sc.copy(y[:, o * TC:(o + 1) * TC], po[go % 4][:]))

            @b.sync
            def _(s):
                for it in range(NIT):
                    half, i = itq(it)
                    s.wait_ge(VE.h, vef(it) + OT)
                    ST.inc(s.dma_start(
                        dram["y2"][half * CH:(half + 1) * CH, i * TC:(i + 1) * TC]
                        .rearrange("(s p) c -> p s c", p=128),
                        y[:].rearrange("p (s c) -> p s c", s=OT)), 16)
                    s.wait_ge(ST.h, bST + 16 * (it + 1))
                    CD.bump(s)


def make_phase_ln(src_name, dst_name, tag):
    def phase_ln(env, T, dram):
        nc = env.nc
        nck = T // TC
        with Phase(env, tag) as ph:
            ones_row = ph.sbuf("ones_row", [1, 128], F32)
            ones_col = ph.sbuf("ones_col", [128, 1], F32)
            y = ph.sbuf("y", [128, NSTR * TC], F32)
            tmp = ph.sbuf("tmp", [128, TC], F32)
            sq = tmp
            mrow = ph.sbuf("mrow", [1, TC], F32)
            qrow = ph.sbuf("qrow", [1, TC], F32)
            m2row = ph.sbuf("m2row", [1, TC], F32)
            varrow = ph.sbuf("varrow", [1, TC], F32)
            rrow = ph.sbuf("rrow", [1, TC], F32)
            lnrow = m2row
            srow = qrow
            rbv = ph.sbuf("rbv", [128, TC], F32)
            shb = ph.sbuf("shb", [128, TC], F32)
            ps1 = ph.psum("ps1", [1, TC])
            ps2 = ph.psum("ps2", [1, TC])
            pr = ph.psum("pr", [128, TC])
            psh = ph.psum("psh", [128, TC])
            LD = ph.sem("LD0")
            MS = ph.sem("MS")
            TE = ph.sem("TE")
            SE = ph.sem("SE")
            VE = ph.sem("VE")
            ST = ph.sem("ST")
            CD = ph.sem("CD")
            bTE, bSE, bVE, bST, bCD = (ph.base(x) for x in
                                       ("TE", "SE", "VE", "ST", "CD"))
            bLD = ph.base("LD0")

            TEC = NSTR + 2
            SEC = NSTR + 2 + 2
            VEC = 5 + 1 + 2 * NSTR

            with nc.Block() as b:
                @b.gpsimd
                def _(g):
                    g.memset(ones_row[:], 1.0)
                    MS.inc(g.memset(ones_col[:], 1.0))
                    for i in range(nck):
                        if i >= 1:
                            g.wait_ge(CD.h, bCD + i)
                        LD.inc(g.dma_start(sb3(y, NSTR),
                                           xchunk(dram[src_name], i * TC, TC)), 16)

                @b.tensor
                def _(t):
                    t.wait_ge(MS.h, MS.n)
                    for i in range(nck):
                        if i >= 1:
                            t.wait_ge(CD.h, bCD + i)
                        tec = bTE + i * TEC
                        sec = bSE + i * SEC
                        vec = bVE + i * VEC
                        t.wait_ge(LD.h, bLD + 16 * (i + 1))
                        for s in range(NSTR):
                            t.matmul(ps1[:], ones_col[:, :], y[:, s * TC:(s + 1) * TC],
                                     start=(s == 0), stop=(s == NSTR - 1))
                            t.wait_ge(SE.h, sec + s + 1)
                            TE.inc(t.matmul(ps2[:], ones_col[:, :], sq[:],
                                            start=(s == 0), stop=(s == NSTR - 1)))
                        t.wait_ge(SE.h, sec + NSTR + 2)
                        TE.inc(t.matmul(pr[:], ones_row[:, :], rrow[:],
                                        start=True, stop=True))
                        t.wait_ge(VE.h, vec + 6)
                        TE.inc(t.matmul(psh[:], ones_row[:, :], srow[:],
                                        start=True, stop=True))

                @b.scalar
                def _(sc):
                    for i in range(nck):
                        if i >= 1:
                            sc.wait_ge(CD.h, bCD + i)
                        tec = bTE + i * TEC
                        sec = bSE + i * SEC
                        vec = bVE + i * VEC
                        sc.wait_ge(LD.h, bLD + 16 * (i + 1))
                        for s in range(NSTR):
                            if s >= 1:
                                sc.wait_ge(TE.h, tec + s)
                            SE.inc(sc.activation(sq[:], y[:, s * TC:(s + 1) * TC],
                                                 AF.Square))
                        sc.wait_ge(VE.h, vec + 5)
                        SE.inc(sc.activation(lnrow[:], varrow[:], AF.Ln))
                        sc.wait_ge(SE.h, SE.n)
                        SE.inc(sc.activation(rrow[:], lnrow[:], AF.Exp, scale=-0.5))
                        sc.wait_ge(TE.h, tec + NSTR + 1)
                        SE.inc(sc.copy(rbv[:], pr[:]))
                        sc.wait_ge(TE.h, tec + NSTR + 2)
                        SE.inc(sc.copy(shb[:], psh[:]))

                @b.vector
                def _(v):
                    for i in range(nck):
                        if i >= 1:
                            v.wait_ge(CD.h, bCD + i)
                        tec = bTE + i * TEC
                        sec = bSE + i * SEC
                        vec = bVE + i * VEC
                        v.wait_ge(TE.h, tec + NSTR)
                        VE.inc(v.tensor_scalar(mrow[:], ps1[:], 1.0 / D, None, ALU.mult))
                        VE.inc(v.tensor_scalar(qrow[:], ps2[:], 1.0 / D, None, ALU.mult))
                        v.wait_ge(VE.h, VE.n)
                        VE.inc(v.tensor_mul(m2row[:], mrow[:], mrow[:]))
                        v.wait_ge(VE.h, VE.n)
                        VE.inc(v.tensor_sub(varrow[:], qrow[:], m2row[:]))
                        v.wait_ge(VE.h, VE.n)
                        VE.inc(v.tensor_scalar(varrow[:], varrow[:], LN_EPS, None, ALU.add))
                        v.wait_ge(SE.h, sec + NSTR + 2)
                        v.wait_ge(VE.h, VE.n)
                        VE.inc(v.tensor_mul(srow[:], mrow[:], rrow[:]))
                        v.wait_ge(SE.h, sec + SEC)
                        for s in range(NSTR):
                            VE.inc(v.tensor_mul(tmp[:], y[:, s * TC:(s + 1) * TC], rbv[:]))
                            v.wait_ge(VE.h, VE.n)
                            VE.inc(v.tensor_sub(y[:, s * TC:(s + 1) * TC], tmp[:], shb[:]))
                            v.wait_ge(VE.h, VE.n)

                @b.sync
                def _(s):
                    for i in range(nck):
                        s.wait_ge(VE.h, bVE + (i + 1) * VEC)
                        ST.inc(s.dma_start(xchunk(dram[dst_name], i * TC, TC),
                                           sb3(y, NSTR)), 16)
                        s.wait_ge(ST.h, bST + 16 * (i + 1))
                        CD.bump(s)
    return phase_ln


def phase_ssmproj(env, T, dram):
    nc = env.nc
    nck = T // TC
    sweeps = [("Wdt", "bdt2", "dtc"), ("Wb", "bb2", "bco"), ("Wc", "bc2", "cco")]
    with Phase(env, "p4a") as ph:
        wb_ = ph.sbuf("wb", [128, NSTR * D], BF16)
        bia = ph.sbuf("bia", [128, 3 * NSTR], F32)
        araw = ph.sbuf("araw", [128, NE * NE], F32)
        asb = ph.sbuf("asb", [128, NE * NE], F32)
        at1 = ph.sbuf("at1", [128, NE * NE], F32)
        at2 = ph.sbuf("at2", [128, NE * NE], F32)
        xb = [ph.sbuf(f"xb{i}", [128, NSTR * TC], BF16) for i in range(2)]
        u1 = ph.sbuf("u1", [128, TC], F32)
        u3 = ph.sbuf("u3", [128, TC], F32)
        t2r = ph.sbuf("t2r", [128, TC], F32)
        ob_f = ph.sbuf("ob_f", [128, NSTR * TC], F32)
        ob_b = ph.sbuf("ob_b", [128, NSTR * TC], BF16)
        pq = [ph.psum(f"pq{i}", [128, TC]) for i in range(2)]
        WD = ph.sem("WD")
        WLp = [ph.sem("WL0"), ph.sem("WL1")]
        LDp = [ph.sem("LD0"), ph.sem("LD1")]
        TE = ph.sem("TE")
        SE = ph.sem("SE")
        VE = ph.sem("VE")
        ST = ph.sem("ST")
        CD = ph.sem("CD")
        bTE, bSE, bVE, bST, bCD = (ph.base(x) for x in ("TE", "SE", "VE", "ST", "CD"))
        bWL = [ph.base("WL0"), ph.base("WL1")]
        bLD = [ph.base("LD0"), ph.base("LD1")]

        NIT = 3 * nck

        def itq(it):
            return it // nck, it % nck

        def sec_of(it):
            w, i = itq(it)
            full = 3
            for jt in range(it):
                wj, _ = itq(jt)
                full += 4 * NSTR if wj == 0 else NSTR
            return bSE + full

        def sec_cnt(w):
            return 4 * NSTR if w == 0 else NSTR

        def vec_of(it):
            w, i = itq(it)
            full = 3
            for jt in range(it):
                wj, _ = itq(jt)
                full += NSTR if wj == 0 else 0
            return bVE + full

        with nc.Block() as b:
            @b.gpsimd
            def _(g):
                WD.inc(g.dma_start(araw[:], dram["a_rawT"][:]), 16)
                WD.inc(g.dma_start(bia[:], dram["biadbc"][:]), 16)
                for it in range(NIT):
                    w, i = itq(it)
                    if i == 0:
                        if w >= 1:
                            g.wait_ge(CD.h, bCD + w * nck)
                        WLp[w % 2].inc(g.dma_start(sb3(wb_, NSTR),
                                                   stripes_ap(dram[sweeps[w][0]])), 16)
                    if it >= 2:
                        g.wait_ge(CD.h, bCD + it - 1)
                    LDp[it % 2].inc(g.dma_start(sb3(xb[it % 2], NSTR),
                                                xchunk(dram["x2"], i * TC, TC)), 16)

            @b.scalar
            def _(sc):
                sc.wait_ge(WD.h, WD.n)
                SE.inc(sc.activation(at1[:], araw[:], AF.Abs))
                sc.wait_ge(SE.h, SE.n)
                SE.inc(sc.activation(at1[:], at1[:], AF.Exp, scale=-1.0))
                sc.wait_ge(SE.h, SE.n)
                SE.inc(sc.activation(at1[:], at1[:], AF.Ln, bias=1.0))
                for it in range(NIT):
                    w, i = itq(it)
                    if it >= 1:
                        sc.wait_ge(CD.h, bCD + it)
                    sec = sec_of(it)
                    for e in range(NSTR):
                        sc.wait_ge(TE.h, bTE + it * NSTR + e + 1)
                        bcol = bia[:, w * NSTR + e:w * NSTR + e + 1]
                        if w == 0:
                            if e >= 1:
                                sc.wait_ge(VE.h, vec_of(it) + e)
                            SE.inc(sc.activation(u1[:], pq[e % 2][:], AF.Abs, bias=bcol))
                            sc.wait_ge(SE.h, SE.n)
                            SE.inc(sc.activation(u1[:], u1[:], AF.Exp, scale=-1.0))
                            sc.wait_ge(SE.h, SE.n)
                            SE.inc(sc.activation(u3[:], u1[:], AF.Ln, bias=1.0))
                            SE.inc(sc.activation(t2r[:], pq[e % 2][:], AF.Relu, bias=bcol))
                        else:
                            SE.inc(sc.activation(ob_b[:, e * TC:(e + 1) * TC],
                                                 pq[e % 2][:], AF.Identity, bias=bcol))

            @b.vector
            def _(v):
                v.wait_ge(SE.h, bSE + 3)
                VE.inc(v.tensor_scalar(at2[:], araw[:], 0.0, None, ALU.max))
                v.wait_ge(VE.h, VE.n)
                VE.inc(v.tensor_add(asb[:], at2[:], at1[:]))
                v.wait_ge(VE.h, VE.n)
                VE.inc(v.tensor_scalar(asb[:], asb[:], -1.0, None, ALU.mult))
                for it in range(NIT):
                    w, i = itq(it)
                    if w != 0:
                        continue
                    if it >= 1:
                        v.wait_ge(CD.h, bCD + it)
                    sec = sec_of(it)
                    for e in range(NSTR):
                        v.wait_ge(SE.h, sec + 4 * (e + 1))
                        VE.inc(v.scalar_tensor_tensor(ob_f[:, e * TC:(e + 1) * TC],
                                                      t2r[:], 1e-4, u3[:],
                                                      ALU.add, ALU.add))

            @b.tensor
            def _(t):
                for it in range(NIT):
                    w, i = itq(it)
                    t.wait_ge(WLp[w % 2].h, bWL[w % 2] + 16 * (w // 2 + 1))
                    if it >= 1:
                        t.wait_ge(CD.h, bCD + it)
                    t.wait_ge(LDp[it % 2].h, bLD[it % 2] + 16 * (it // 2 + 1))
                    xv = sb3(xb[it % 2], NSTR)
                    sec = sec_of(it)
                    for e in range(NSTR):
                        if e >= 2:
                            t.wait_ge(SE.h, sec + (4 if w == 0 else 1) * (e - 1))
                        for s in range(NSTR):
                            mm = t.matmul(pq[e % 2][:],
                                          wb_[:, s * D + e * HD:s * D + (e + 1) * HD],
                                          xv[:, s, :], start=(s == 0), stop=(s == NSTR - 1))
                        TE.inc(mm)

            @b.sync
            def _(s):
                s.wait_ge(VE.h, bVE + 3)
                ST.inc(s.dma_start(dram["Asb"][:], asb[:]), 16)
                for it in range(NIT):
                    w, i = itq(it)
                    outname = sweeps[w][2]
                    if w == 0:
                        s.wait_ge(VE.h, vec_of(it) + NSTR)
                        ST.inc(s.dma_start(xchunk(dram[outname], i * TC, TC),
                                           sb3(ob_f, NSTR)), 16)
                    else:
                        s.wait_ge(SE.h, sec_of(it) + NSTR)
                        ST.inc(s.dma_start(xchunk(dram[outname], i * TC, TC),
                                           sb3(ob_b, NSTR)), 16)
                    s.wait_ge(ST.h, bST + 16 * (it + 2))
                    CD.bump(s)





def phase_ssmscan(env, T, dram):
    nc = env.nc
    nck = T // TC
    ASPL = 12
    with Phase(env, "p4b") as ph:
        asb = ph.sbuf("asb", [128, NE * NE], F32)
        bbase = ph.sbuf("bbase", [128, NE * NE], F32)
        cbase = ph.sbuf("cbase", [128, NE * NE], F32)
        cbb = ph.sbuf("cbb", [128, NE * NE], F32)
        dcol = ph.sbuf("dcol", [128, NE], F32)
        state = ph.sbuf("state", [128, NE * NE], F32)
        dtb = ph.sbuf("dtb", [128, NSTR * TC], F32)
        bcb = ph.sbuf("bcb", [128, NSTR * TC], BF16)
        ccb = ph.sbuf("ccb", [128, NSTR * TC], BF16)
        ub = ph.sbuf("ub", [128, NSTR * TC], F32)
        gb = ph.sbuf("gb", [128, NSTR * TC], F32)
        yb = ph.sbuf("yb", [128, NSTR * TC], F32)
        gt = ph.sbuf("gt", [128, TC], F32)
        al = [ph.sbuf(f"al{i}", [128, TC], F32) for i in range(4)]
        st_ = [ph.sbuf(f"st{i}", [128, TC], F32) for i in range(4)]
        ysv = ph.sbuf("ysv", [128, TC], F32)
        ysg = ph.sbuf("ysg", [128, TC], F32)
        ysg2 = ph.sbuf("ysg2", [128, TC], F32)
        ctg = ysg2
        ones_row = ph.sbuf("ones_row", [1, 128], F32)
        ones_col = ph.sbuf("ones_col", [128, 1], F32)
        tmp = ph.sbuf("tmp", [128, TC], F32)
        sq = tmp
        mrow = ph.sbuf("mrow", [1, TC], F32)
        qrow = ph.sbuf("qrow", [1, TC], F32)
        m2row = ph.sbuf("m2row", [1, TC], F32)
        varrow = ph.sbuf("varrow", [1, TC], F32)
        rrow = ph.sbuf("rrow", [1, TC], F32)
        lnrow = m2row
        srow = qrow
        rbv = ph.sbuf("rbv", [128, TC], F32)
        shb = ph.sbuf("shb", [128, TC], F32)
        ps1 = ph.psum("ps1", [1, TC])
        ps2 = ph.psum("ps2", [1, TC])
        pr = ph.psum("pr", [128, TC])
        psh = ph.psum("psh", [128, TC])
        WD = ph.sem("WD")
        MS = ph.sem("MS")
        LDU = ph.sem("LDU")
        LDC = ph.sem("LDC")
        PF = ph.sem("PF")
        TE = ph.sem("TE")
        SE = ph.sem("SE")
        VE = ph.sem("VE")
        GP = ph.sem("GP")
        ST = ph.sem("ST")
        CD = ph.sem("CD")
        bTE, bSE, bVE, bGP, bST, bCD = (ph.base(x) for x in
                                        ("TE", "SE", "VE", "GP", "ST", "CD"))
        bLDU = ph.base("LDU")
        bLDC = ph.base("LDC")
        bPF = ph.base("PF")

        KN = NSTR * NE
        VCS = 2 * NE + 2
        VEK = NSTR * VCS + 6 + 2 * ASPL
        GPC = 2 * NSTR + NSTR + NSTR + 2 * (NSTR - ASPL)
        SEK = 2 * KN + NSTR + 4
        TEK = NSTR + 2

        def ve_c(i):
            return bVE + 1 + i * VEK

        def gp_c(i):
            return bGP + i * GPC

        def se_c(i):
            return bSE + i * SEK

        def te_c(i):
            return bTE + i * TEK

        def ve_after_scan(i, k):
            s_, n_ = k // NE, k % NE
            return ve_c(i) + s_ * VCS + (1 if n_ == 0 else 2 * n_)

        def ve_after_stripe(i, s_):
            return ve_c(i) + (s_ + 1) * VCS

        def se_exp(i, k):
            return se_c(i) + (k + 1 if k < 3 else 2 * k - 2)

        def se_copy(i, j):
            return se_c(i) + (2 * j + 5 if j < KN - 3 else KN * 2 - 3 + (j - (KN - 4)))

        def se_rrow(i):
            return se_c(i) + 2 * KN + NSTR + 2

        def gp_after_gpass(i, s_):
            return gp_c(i) + NSTR + 2 * s_ + 1

        def gp_after_yinit(i, s_):
            return gp_c(i) + NSTR + 2 * s_ + 2

        def gp_after_resid(i, s_):
            return gp_c(i) + 3 * NSTR + s_ + 1

        with nc.Block() as b:
            @b.gpsimd
            def _(g):
                WD.inc(g.dma_start(asb[:], dram["Asb"][:]), 16)
                WD.inc(g.dma_start(bbase[:], dram["BbT"][:]), 16)
                WD.inc(g.dma_start(cbase[:], dram["CbT"][:]), 16)
                WD.inc(g.dma_start(dcol[:], dram["Dt2"][:]), 16)
                g.memset(state[:], 0.0)
                g.memset(ones_col[:], 1.0)
                MS.inc(g.memset(ones_row[:], 1.0))
                g.wait_ge(WD.h, WD.n)
                PF.inc(g.dma_start(sb3(dtb, NSTR), xchunk(dram["dtc"], 0, TC)), 16)
                PF.inc(g.dma_start(sb3(bcb, NSTR), xchunk(dram["bco"], 0, TC)), 16)
                for i in range(nck):
                    if i >= 1:
                        g.wait_ge(CD.h, bCD + i)
                    LDU.inc(g.dma_start(sb3(ub, NSTR), xchunk(dram["x2"], i * TC, TC)), 16)
                    LDC.inc(g.dma_start(sb3(ccb, NSTR), xchunk(dram["cco"], i * TC, TC)), 16)
                    g.wait_ge(PF.h, bPF + 32 * (i + 1))
                    for s_ in range(NSTR):
                        sl = slice(s_ * TC, (s_ + 1) * TC)
                        GP.inc(g.tensor_tensor(gb[:, sl], dtb[:, sl], bcb[:, sl],
                                               ALU.mult))
                    g.wait_ge(LDU.h, bLDU + 16 * (i + 1))
                    g.wait_ge(GP.h, GP.n)
                    for s_ in range(NSTR):
                        sl = slice(s_ * TC, (s_ + 1) * TC)
                        GP.inc(g.tensor_tensor(gb[:, sl], gb[:, sl], ub[:, sl],
                                               ALU.mult))
                        GP.inc(g.tensor_tensor(
                            yb[:, sl], ub[:, sl],
                            dcol[:, s_:s_ + 1].broadcast_to([128, TC]), ALU.mult))
                    if i + 1 < nck:
                        g.wait_ge(SE.h, se_c(i) + 2 * KN - 4)
                        PF.inc(g.dma_start(sb3(dtb, NSTR),
                                           xchunk(dram["dtc"], (i + 1) * TC, TC)), 16)
                        PF.inc(g.dma_start(sb3(bcb, NSTR),
                                           xchunk(dram["bco"], (i + 1) * TC, TC)), 16)
                    for s_ in range(NSTR):
                        sl = slice(s_ * TC, (s_ + 1) * TC)
                        g.wait_ge(VE.h, ve_after_stripe(i, s_))
                        GP.inc(g.tensor_add(yb[:, sl], yb[:, sl], ub[:, sl]))
                    for s_ in range(ASPL, NSTR):
                        sl = slice(s_ * TC, (s_ + 1) * TC)
                        g.wait_ge(SE.h, se_c(i) + SEK)
                        g.wait_ge(GP.h, GP.n)
                        GP.inc(g.tensor_mul(ctg[:], yb[:, sl], rbv[:]))
                        g.wait_ge(GP.h, GP.n)
                        GP.inc(g.tensor_sub(yb[:, sl], ctg[:], shb[:]))

            @b.scalar
            def _(sc):
                sc.wait_ge(WD.h, WD.n)
                for i in range(nck):
                    if i >= 1:
                        sc.wait_ge(CD.h, bCD + i)
                    sc.wait_ge(PF.h, bPF + 32 * (i + 1))
                    sec = se_c(i)
                    for k in range(KN):
                        s_ = k // NE
                        if k >= 4:
                            sc.wait_ge(VE.h, ve_after_scan(i, k - 4))
                        SE.inc(sc.activation(al[k % 4][:], dtb[:, s_ * TC:(s_ + 1) * TC],
                                             AF.Exp, scale=asb[:, k:k + 1]))
                        if k >= 3:
                            j = k - 3
                            sc.wait_ge(VE.h, ve_after_scan(i, j))
                            SE.inc(sc.copy(state[:, j:j + 1], st_[j % 4][:, TC - 1:TC]))
                    for j in range(KN - 3, KN):
                        sc.wait_ge(VE.h, ve_after_scan(i, j))
                        SE.inc(sc.copy(state[:, j:j + 1], st_[j % 4][:, TC - 1:TC]))
                    for s_ in range(NSTR):
                        sc.wait_ge(GP.h, gp_after_resid(i, s_))
                        if s_ >= 1:
                            sc.wait_ge(TE.h, te_c(i) + s_)
                        SE.inc(sc.activation(sq[:], yb[:, s_ * TC:(s_ + 1) * TC],
                                             AF.Square))
                    sc.wait_ge(VE.h, ve_c(i) + VEK - 2 * ASPL - 1)
                    SE.inc(sc.activation(lnrow[:], varrow[:], AF.Ln))
                    sc.wait_ge(SE.h, SE.n)
                    SE.inc(sc.activation(rrow[:], lnrow[:], AF.Exp, scale=-0.5))
                    sc.wait_ge(TE.h, te_c(i) + NSTR + 1)
                    SE.inc(sc.copy(rbv[:], pr[:]))
                    sc.wait_ge(TE.h, te_c(i) + NSTR + 2)
                    SE.inc(sc.copy(shb[:], psh[:]))

            @b.vector
            def _(v):
                v.wait_ge(WD.h, WD.n)
                VE.inc(v.tensor_mul(cbb[:], cbase[:], bbase[:]))
                v.wait_ge(VE.h, VE.n)
                for i in range(nck):
                    if i >= 1:
                        v.wait_ge(CD.h, bCD + i)
                    v.wait_ge(MS.h, MS.n)
                    sec = se_c(i)
                    for s_ in range(NSTR):
                        sl = slice(s_ * TC, (s_ + 1) * TC)
                        v.wait_ge(GP.h, gp_after_gpass(i, s_))

                        def ysv_op(n_):
                            k = s_ * NE + n_
                            if n_ == 0:
                                VE.inc(v.tensor_scalar(ysv[:], st_[k % 4][:],
                                                       cbb[:, k:k + 1], None,
                                                       ALU.mult))
                            else:
                                VE.inc(v.scalar_tensor_tensor(ysv[:], st_[k % 4][:],
                                                              cbb[:, k:k + 1],
                                                              ysv[:],
                                                              ALU.mult, ALU.add))
                        for n_ in range(NE):
                            k = s_ * NE + n_
                            v.wait_ge(SE.h, se_exp(i, k))
                            VE.inc(v.tensor_tensor_scan(st_[k % 4][:], al[k % 4][:],
                                                        gb[:, sl], state[:, k:k + 1],
                                                        ALU.mult, ALU.add))
                            if n_ >= 1:
                                v.wait_ge(VE.h, VE.n - 1)
                                ysv_op(n_ - 1)
                        v.wait_ge(VE.h, VE.n)
                        ysv_op(NE - 1)
                        if s_ == 0:
                            v.wait_ge(LDC.h, bLDC + 16 * (i + 1))
                        v.wait_ge(VE.h, VE.n)
                        VE.inc(v.tensor_mul(gt[:], ccb[:, sl], ysv[:]))
                        v.wait_ge(GP.h, gp_after_yinit(i, s_))
                        v.wait_ge(VE.h, VE.n)
                        VE.inc(v.tensor_add(yb[:, sl], yb[:, sl], gt[:]))
                    v.wait_ge(TE.h, te_c(i) + NSTR)
                    VE.inc(v.tensor_scalar(mrow[:], ps1[:], 1.0 / D, None, ALU.mult))
                    VE.inc(v.tensor_scalar(qrow[:], ps2[:], 1.0 / D, None, ALU.mult))
                    v.wait_ge(VE.h, VE.n)
                    VE.inc(v.tensor_mul(m2row[:], mrow[:], mrow[:]))
                    v.wait_ge(VE.h, VE.n)
                    VE.inc(v.tensor_sub(varrow[:], qrow[:], m2row[:]))
                    v.wait_ge(VE.h, VE.n)
                    VE.inc(v.tensor_scalar(varrow[:], varrow[:], LN_EPS, None, ALU.add))
                    v.wait_ge(SE.h, se_rrow(i))
                    v.wait_ge(VE.h, VE.n)
                    VE.inc(v.tensor_mul(srow[:], mrow[:], rrow[:]))
                    v.wait_ge(SE.h, sec + SEK)
                    atmp = [tmp, gt, ysv]
                    for s_ in range(ASPL):
                        VE.inc(v.tensor_mul(atmp[s_ % 3][:],
                                            yb[:, s_ * TC:(s_ + 1) * TC], rbv[:]))
                        if s_ >= 1:
                            sp = s_ - 1
                            v.wait_ge(VE.h, VE.n - 1)
                            VE.inc(v.tensor_sub(yb[:, sp * TC:(sp + 1) * TC],
                                                atmp[sp % 3][:], shb[:]))
                    v.wait_ge(VE.h, VE.n)
                    sp = ASPL - 1
                    VE.inc(v.tensor_sub(yb[:, sp * TC:(sp + 1) * TC],
                                        atmp[sp % 3][:], shb[:]))

            @b.tensor
            def _(t):
                t.wait_ge(MS.h, MS.n)
                for i in range(nck):
                    if i >= 1:
                        t.wait_ge(CD.h, bCD + i)
                    sec = se_c(i)
                    for s_ in range(NSTR):
                        t.wait_ge(GP.h, gp_after_resid(i, s_))
                        t.matmul(ps1[:], ones_col[:, :], yb[:, s_ * TC:(s_ + 1) * TC],
                                 start=(s_ == 0), stop=(s_ == NSTR - 1))
                        t.wait_ge(SE.h, sec + 2 * KN + s_ + 1)
                        TE.inc(t.matmul(ps2[:], ones_col[:, :], sq[:],
                                        start=(s_ == 0), stop=(s_ == NSTR - 1)))
                    t.wait_ge(SE.h, se_rrow(i))
                    TE.inc(t.matmul(pr[:], ones_row[:, :], rrow[:], start=True, stop=True))
                    t.wait_ge(VE.h, ve_c(i) + VEK - 2 * ASPL)
                    TE.inc(t.matmul(psh[:], ones_row[:, :], srow[:], start=True, stop=True))

            @b.sync
            def _(s):
                for i in range(nck):
                    s.wait_ge(VE.h, ve_c(i) + VEK)
                    s.wait_ge(GP.h, gp_c(i) + GPC)
                    s.wait_ge(SE.h, se_c(i) + SEK)
                    ST.inc(s.dma_start(xchunk(dram["x3"], i * TC, TC), sb3(yb, NSTR)), 16)
                    s.wait_ge(ST.h, bST + 16 * (i + 1))
                    CD.bump(s)


def phase_moeroute(env, T, dram):
    nc = env.nc
    nck = T // TC
    with Phase(env, "p5a") as ph:
        simb = ph.sbuf("simb", [128, NSTR * NE], F32)
        simf = [ph.sbuf(f"simf{i}", [128, NE], F32) for i in range(2)]
        sq16 = ph.sbuf("sq16", [128, NE], F32)
        rsnr = ph.sbuf("rsnr", [1, NE], F32)
        rsnc = ph.sbuf("rsnc", [NE, 1], F32)
        sgr = ph.sbuf("sgr", [1, NE], F32)
        sgc = ph.sbuf("sgc", [NE, 1], F32)
        id16 = ph.sbuf("id16", [16, 16], F32)
        id128 = ph.sbuf("id128", [128, 128], F32)
        negt = ph.sbuf("negt", [128, NE], F32)
        xb = [ph.sbuf(f"xb{i}", [128, NSTR * TC], F32) for i in range(2)]
        sqs = ph.sbuf("sqs", [128, TC], F32)
        lnx = ph.sbuf("lnx", [1, TC], F32)
        rnx = ph.sbuf("rnx", [1, TC], F32)
        rawn = ph.sbuf("rawn", [NE, TC], F32)
        lgT = ph.sbuf("lgT", [NE, TC], F32)
        lgN = ph.sbuf("lgN", [128, NE], F32)
        m1 = ph.sbuf("m1", [128, 1], F32)
        inact = ph.sbuf("inact", [128, 1], F32)
        top1 = ph.sbuf("top1", [128, NE], F32)
        l2t = ph.sbuf("l2t", [128, NE], F32)
        m2r = ph.sbuf("m2r", [128, 1], F32)
        top2 = ph.sbuf("top2", [128, NE], F32)
        fbm = ph.sbuf("fbm", [128, NE], F32)
        gated = ph.sbuf("gated", [128, NE], F32)
        maskb = ph.sbuf("maskb", [128, NE], F32)
        fmask = ph.sbuf("fmask", [128, NE], F32)
        maskd = ph.sbuf("maskd", [128, NE], F32)
        mneg = ph.sbuf("mneg", [128, 1], F32)
        ee = ph.sbuf("ee", [128, NE], F32)
        denr = ph.sbuf("denr", [128, 1], F32)
        rcpr = ph.sbuf("rcpr", [128, 1], F32)
        wvt = ph.sbuf("wvt", [128, NE], F32)
        wrow = ph.sbuf("wrow", [NE, TC], F32)
        ones_col = ph.sbuf("ones_col", [128, 1], F32)
        ones_r16 = ph.sbuf("ones_r16", [1, NE], F32)
        praw = ph.psum("praw", [NE, TC])
        ps2x = ph.psum("ps2x", [1, TC])
        pb16 = ph.psum("pb16", [NE, TC])
        ptr128 = ph.psum("ptr128", [128, NE])
        ptw = ph.psum("ptw", [NE, 128])
        pcn = ph.psum("pcn", [1, NE])
        WD = ph.sem("WD")
        WLp = [ph.sem("WL0"), ph.sem("WL1")]
        MS = ph.sem("MS")
        LDp = [ph.sem("LD0"), ph.sem("LD1")]
        TE = ph.sem("TE")
        SE = ph.sem("SE")
        VE = ph.sem("VE")
        SW = ph.sem("SW")
        ST = ph.sem("ST")
        CD = ph.sem("CD")
        bTE, bSE, bVE, bSW, bST, bCD = (ph.base(x) for x in
                                        ("TE", "SE", "VE", "SW", "ST", "CD"))
        bWL = [ph.base("WL0"), ph.base("WL1")]
        bLD = [ph.base("LD0"), ph.base("LD1")]

        TEC = 1 + NSTR + 1 + 8
        SEC = NSTR + 3 + 4 * 3
        VSUB = 19
        VEC = 2 + 4 * VSUB

        def cdw(eng, i):
            if i >= 1:
                eng.wait_ge(CD.h, bCD + i)

        with nc.Block() as b:
            @b.gpsimd
            def _(g):
                WD.inc(g.dma_start(simb[:].rearrange("p (s e) -> p s e", s=NSTR),
                                   stripes_ap(dram["sim"])), 16)
                WD.inc(g.dma_start(sgr[:], dram["gates"][:]), 16)
                g.memset(id16[:], 1.0)
                g.memset(id128[:], 1.0)
                g.memset(negt[:], -1.0e9)
                g.memset(ones_col[:], 1.0)
                MS.inc(g.memset(ones_r16[:], 1.0))
                g.wait_ge(MS.h, MS.n)
                g.affine_select(id16[:], id16[:], pattern=[[1, 16]],
                                compare_op=ALU.is_equal, fill=0.0, base=0,
                                channel_multiplier=-1)
                MS.inc(g.affine_select(id128[:], id128[:], pattern=[[1, 128]],
                                       compare_op=ALU.is_equal, fill=0.0, base=0,
                                       channel_multiplier=-1))
                for s in range(NSTR):
                    if s >= 2:
                        g.wait_ge(SE.h, bSE + s - 1)
                    WLp[s % 2].inc(g.dma_start(
                        simf[s % 2][:], stripes_ap(dram["sim"])[:, s, :]), 16)
                for i in range(nck):
                    cdw(g, i - 1)
                    LDp[i % 2].inc(g.dma_start(sb3(xb[i % 2], NSTR),
                                               xchunk(dram["x3"], i * TC, TC)), 16)

            @b.scalar
            def _(sc):
                for s in range(NSTR):
                    sc.wait_ge(WLp[s % 2].h, bWL[s % 2] + 16 * (s // 2 + 1))
                    if s >= 1:
                        sc.wait_ge(TE.h, bTE + s)
                    SE.inc(sc.activation(sq16[:], simf[s % 2][:], AF.Square))
                sc.wait_ge(TE.h, bTE + NSTR)
                SE.inc(sc.activation(rsnr[:], pcn[:], AF.Ln))
                sc.wait_ge(SE.h, SE.n)
                SE.inc(sc.activation(rsnr[:], rsnr[:], AF.Exp, scale=-0.5))
                sc.wait_ge(WD.h, WD.n)
                SE.inc(sc.activation(sgr[:], sgr[:], AF.Sigmoid))
                sc.wait_ge(SE.h, SE.n)
                SW.inc(sc.dma_start(dram["rsn_t"][:], rsnr[:]), 16)
                SW.inc(sc.dma_start(dram["sg_t"][:], sgr[:]), 16)
                sc.wait_ge(SW.h, bSW + 32)
                SW.inc(sc.dma_start(rsnc[:], dram["rsn_t"][:].rearrange("a e -> e a")), 16)
                SW.inc(sc.dma_start(sgc[:], dram["sg_t"][:].rearrange("a e -> e a")), 16)
                pse = bSE + NSTR + 3
                pte = bTE + NSTR
                for i in range(nck):
                    cdw(sc, i)
                    sec = pse + i * SEC
                    tec = pte + i * TEC
                    sc.wait_ge(LDp[i % 2].h, bLD[i % 2] + 16 * (i // 2 + 1))
                    for s in range(NSTR):
                        if s >= 1:
                            sc.wait_ge(TE.h, tec + 1 + s)
                        SE.inc(sc.activation(sqs[:], sb3(xb[i % 2], NSTR)[:, s, :],
                                             AF.Square))
                    sc.wait_ge(TE.h, tec + 1 + NSTR)
                    SE.inc(sc.activation(lnx[:], ps2x[:], AF.Ln))
                    sc.wait_ge(SE.h, SE.n)
                    SE.inc(sc.activation(rnx[:], lnx[:], AF.Exp, scale=-0.5))
                    sc.wait_ge(SW.h, bSW + 64)
                    SE.inc(sc.activation(rawn[:], praw[:], AF.Identity,
                                         scale=rsnc[:, 0:1]))
                    for st in range(4):
                        sc.wait_ge(TE.h, tec + 1 + NSTR + 1 + st * 2 + 1)
                        SE.inc(sc.copy(lgN[:], ptr128[:]))
                        sc.wait_ge(VE.h, bVE + i * VEC + 2 + st * VSUB + 15)
                        SE.inc(sc.activation(ee[:], maskd[:], AF.Exp,
                                             bias=mneg[:, 0:1]))
                        sc.wait_ge(TE.h, tec + 1 + NSTR + 1 + st * 2 + 2)
                        SE.inc(sc.copy(wrow[:, st * 128:(st + 1) * 128], ptw[:]))

            @b.tensor
            def _(t):
                t.wait_ge(MS.h, MS.n)
                t.wait_ge(WD.h, WD.n)
                for s in range(NSTR):
                    t.wait_ge(SE.h, bSE + s + 1)
                    TE.inc(t.matmul(pcn[:], ones_col[:, :], sq16[:],
                                    start=(s == 0), stop=(s == NSTR - 1)))
                pse = bSE + NSTR + 3
                pte = bTE + NSTR
                for i in range(nck):
                    cdw(t, i)
                    sec = pse + i * SEC
                    tec = pte + i * TEC
                    vec = bVE + i * VEC
                    t.wait_ge(LDp[i % 2].h, bLD[i % 2] + 16 * (i // 2 + 1))
                    xv = sb3(xb[i % 2], NSTR)
                    for s in range(NSTR):
                        mm = t.matmul(praw[:], simb[:, s * NE:(s + 1) * NE], xv[:, s, :],
                                      start=(s == 0), stop=(s == NSTR - 1))
                    TE.inc(mm)
                    for s in range(NSTR):
                        t.wait_ge(SE.h, sec + s + 1)
                        TE.inc(t.matmul(ps2x[:], ones_col[:, :], sqs[:],
                                        start=(s == 0), stop=(s == NSTR - 1)))
                    t.wait_ge(SE.h, sec + NSTR + 2)
                    TE.inc(t.matmul(pb16[:], ones_r16[:, :], rnx[:], start=True, stop=True))
                    for st in range(4):
                        t.wait_ge(VE.h, vec + 2)
                        if st >= 1:
                            t.wait_ge(SE.h, sec + NSTR + 3 + (st - 1) * 3 + 1)
                        TE.inc(t.transpose(ptr128[:], lgT[:, st * 128:(st + 1) * 128],
                                           id16[:]))
                        t.wait_ge(VE.h, vec + 2 + st * VSUB + VSUB)
                        if st >= 1:
                            t.wait_ge(SE.h, sec + NSTR + 3 + (st - 1) * 3 + 3)
                        TE.inc(t.transpose(ptw[:], wvt[:], id128[:]))

            @b.vector
            def _(v):
                v.wait_ge(WD.h, WD.n)
                pse = bSE + NSTR + 3
                pte = bTE + NSTR
                for i in range(nck):
                    cdw(v, i)
                    sec = pse + i * SEC
                    tec = pte + i * TEC
                    vec = bVE + i * VEC
                    v.wait_ge(SE.h, sec + NSTR + 3)
                    v.wait_ge(TE.h, tec + 1 + NSTR + 1)
                    v.wait_ge(SW.h, bSW + 64)
                    VE.inc(v.tensor_mul(lgT[:], rawn[:], pb16[:]))
                    v.wait_ge(VE.h, VE.n)
                    VE.inc(v.tensor_scalar(lgT[:], lgT[:], sgc[:, 0:1], None, ALU.subtract))
                    for st in range(4):
                        v.wait_ge(SE.h, sec + NSTR + 3 + st * 3 + 1)
                        VE.inc(v.reduce_max(m1[:], lgN[:], axis=AX.X))
                        v.wait_ge(VE.h, VE.n)
                        VE.inc(v.tensor_scalar(inact[:], m1[:], 0.0, None, ALU.is_le))
                        VE.inc(v.tensor_scalar(top1[:], lgN[:], m1[:, 0:1], None, ALU.is_ge))
                        v.wait_ge(VE.h, VE.n)
                        VE.inc(v.scalar_tensor_tensor(l2t[:], top1[:], -3.0e9, lgN[:],
                                                      ALU.mult, ALU.add))
                        v.wait_ge(VE.h, VE.n)
                        VE.inc(v.reduce_max(m2r[:], l2t[:], axis=AX.X))
                        v.wait_ge(VE.h, VE.n)
                        VE.inc(v.tensor_scalar(top2[:], l2t[:], m2r[:, 0:1], None, ALU.is_ge))
                        v.wait_ge(VE.h, VE.n)
                        VE.inc(v.tensor_add(fbm[:], top1[:], top2[:]))
                        VE.inc(v.tensor_scalar(gated[:], lgN[:], 0.0, None, ALU.max))
                        VE.inc(v.tensor_scalar(maskb[:], lgN[:], 0.0, None, ALU.is_gt))
                        v.wait_ge(VE.h, VE.n)
                        VE.inc(v.tensor_sub(fbm[:], fbm[:], maskb[:]))
                        v.wait_ge(VE.h, VE.n)
                        VE.inc(v.scalar_tensor_tensor(fmask[:], fbm[:], inact[:, 0:1],
                                                      maskb[:], ALU.mult, ALU.add))
                        v.wait_ge(VE.h, VE.n)
                        VE.inc(v.tensor_mul(l2t[:], gated[:], fmask[:]))
                        VE.inc(v.tensor_scalar(top2[:], fmask[:], 1.0, None, ALU.subtract))
                        v.wait_ge(VE.h, VE.n)
                        VE.inc(v.scalar_tensor_tensor(maskd[:], top2[:], 1.0e9, l2t[:],
                                                      ALU.mult, ALU.add))
                        v.wait_ge(VE.h, VE.n)
                        VE.inc(v.reduce_max(mneg[:], maskd[:], axis=AX.X, negate=True))
                        v.wait_ge(SE.h, sec + NSTR + 3 + st * 3 + 2)
                        VE.inc(v.reduce_sum(denr[:], ee[:], axis=AX.X))
                        v.wait_ge(VE.h, VE.n)
                        VE.inc(v.reciprocal(rcpr[:], denr[:]))
                        v.wait_ge(VE.h, VE.n)
                        VE.inc(v.tensor_scalar(wvt[:], ee[:], rcpr[:, 0:1], None, ALU.mult))
                        v.wait_ge(VE.h, VE.n)
                        VE.inc(v.tensor_mul(wvt[:], wvt[:], fmask[:]))

            @b.sync
            def _(s):
                pse = bSE + NSTR + 3
                for i in range(nck):
                    sec = pse + i * SEC
                    s.wait_ge(SE.h, sec + SEC)
                    ST.inc(s.dma_start(dram["wTr"][:, i * TC:(i + 1) * TC], wrow[:]), 16)
                    s.wait_ge(ST.h, bST + 16 * (i + 1))
                    CD.bump(s)


def phase_moemix(env, T, dram):
    nc = env.nc
    NHH = 2 if T >= 2 * TC else 1
    TH = T // NHH
    NTS = TH // TC
    NVT = TH // 128
    with Phase(env, "p5b") as ph:
        xb = ph.sbuf("xb", [128, NSTR * TH], BF16)
        prj = [ph.sbuf(f"prj{i}", [128, 3 * NSTR * HD], BF16) for i in range(2)]
        wrk = [ph.sbuf(f"wrk{i}", [1, TC], F32) for i in range(2)]
        wbb = [ph.sbuf(f"wbb{i}", [128, TC], F32) for i in range(2)]
        tmpb = ph.sbuf("tmpb", [128, TC], F32)
        qacc = ph.sbuf("qacc", [128, TH], F32)
        kacc = ph.sbuf("kacc", [128, TH], F32)
        vacc = ph.sbuf("vacc", [128, TH], F32)
        vtile = [ph.sbuf(f"vtile{j}", [128, HD], BF16) for j in range(NVT)]
        id128 = ph.sbuf("id128", [128, 128], F32)
        ones_row = ph.sbuf("ones_row", [1, 128], F32)
        pw = [ph.psum(f"pw{i}", [128, TC]) for i in range(2)]
        psP = [ph.psum(f"psP{i}", [128, TC]) for i in range(3)]
        ptv = [ph.psum(f"ptv{i}", [128, HD]) for i in range(2)]
        WD = ph.sem("WD")
        MS = ph.sem("MS")
        WLp = [ph.sem("WL0"), ph.sem("WL1")]
        RWp = [ph.sem("RW0"), ph.sem("RW1")]
        TE = ph.sem("TE")
        SE = ph.sem("SE")
        VE = ph.sem("VE")
        ST = ph.sem("ST")
        CD = ph.sem("CD")
        bTE, bSE, bVE, bST, bCD = (ph.base(x) for x in ("TE", "SE", "VE", "ST", "CD"))
        bWL = [ph.base("WL0"), ph.base("WL1")]
        bRW = [ph.base("RW0"), ph.base("RW1")]

        accs = (qacc, kacc, vacc)
        NFI = NHH * NE * NTS

        def fi_of(hh, e, ts):
            return (hh * NE + e) * NTS + ts

        def vee(fi):
            e = (fi // NTS) % NE
            return 3 if e == 0 else 6

        def cum_ve(fi):
            return bVE + sum(vee(j) for j in range(fi + 1))

        def te(fi):
            return bTE + fi * 4 + (fi // (NE * NTS)) * NVT

        def te_vt(hh, j):
            return bTE + (hh + 1) * NE * NTS * 4 + hh * NVT + j + 1

        def se_wbb(fi):
            return bSE + fi + 1 + (fi // (NE * NTS)) * NVT

        def se_vt(hh, j):
            return bSE + (hh + 1) * NE * NTS + hh * NVT + j + 1

        with nc.Block() as b:
            @b.gpsimd
            def _(g):
                g.memset(id128[:], 1.0)
                MS.inc(g.memset(ones_row[:], 1.0))
                g.wait_ge(MS.h, MS.n)
                MS.inc(g.affine_select(id128[:], id128[:], pattern=[[1, 128]],
                                       compare_op=ALU.is_equal, fill=0.0, base=0,
                                       channel_multiplier=-1))
                for hh in range(NHH):
                    if hh >= 1:
                        g.wait_ge(CD.h, bCD + hh)
                    WD.inc(g.dma_start(
                        xb[:].rearrange("p (s c) -> p s c", s=NSTR),
                        xchunk(dram["x3"], hh * TH, TH)), 16)
                    for e in range(NE):
                        he = hh * NE + e
                        if he >= 2:
                            fi_last = (he - 1) * NTS - 1
                            g.wait_ge(TE.h, te(fi_last) + 4)
                        for pi_, pn in enumerate(("QP", "KP", "VP")):
                            WLp[he % 2].inc(g.dma_start(
                                prj[e % 2][:].rearrange("p (k s c) -> p k s c",
                                                        k=3, s=NSTR)[:, pi_],
                                dram[pn][e * D:(e + 1) * D, :]
                                .rearrange("(s p) c -> p s c", p=128)), 16)
                        for ts in range(NTS):
                            fi = fi_of(hh, e, ts)
                            if fi >= 2:
                                g.wait_ge(TE.h, te(fi - 2) + 1)
                            RWp[fi % 2].inc(g.dma_start(
                                wrk[fi % 2][:],
                                dram["wTr"][e:e + 1,
                                            hh * TH + ts * TC:hh * TH + (ts + 1) * TC]), 16)
                    g.wait_ge(VE.h, cum_ve(fi_of(hh, NE - 1, NTS - 1)))
                    ST.inc(g.dma_start(dram["qT_bf"][:, hh * TH:(hh + 1) * TH], qacc[:]), 16)
                    ST.inc(g.dma_start(dram["kT_bf"][:, hh * TH:(hh + 1) * TH], kacc[:]), 16)
                    for j in range(NVT):
                        g.wait_ge(SE.h, se_vt(hh, j))
                        ST.inc(g.dma_start(
                            dram["Vn2_bf"][hh * TH + j * 128:hh * TH + (j + 1) * 128, :],
                            vtile[j][:]), 16)
                    g.wait_ge(ST.h, bST + (hh + 1) * 16 * (2 + NVT))
                    CD.bump(g)

            @b.tensor
            def _(t):
                t.wait_ge(MS.h, MS.n)
                for hh in range(NHH):
                    t.wait_ge(WD.h, 16 * (hh + 1))
                    for e in range(NE):
                        he = hh * NE + e
                        t.wait_ge(WLp[he % 2].h, bWL[he % 2] + 48 * (he // 2 + 1))
                        for ts in range(NTS):
                            fi = fi_of(hh, e, ts)
                            t.wait_ge(RWp[fi % 2].h, bRW[fi % 2] + 16 * (fi // 2 + 1))
                            if fi >= 2:
                                t.wait_ge(SE.h, se_wbb(fi - 2))
                            TE.inc(t.matmul(pw[fi % 2][:], ones_row[:, :],
                                            wrk[fi % 2][:], start=True, stop=True))
                            if fi >= 1:
                                t.wait_ge(VE.h, cum_ve(fi - 1))
                            prv = prj[e % 2][:].rearrange("p (k s c) -> p k s c",
                                                          k=3, s=NSTR)
                            xv = xb[:].rearrange("p (s c) -> p s c", s=NSTR)
                            for pi_ in range(3):
                                for s in range(NSTR):
                                    mm = t.matmul(psP[pi_][:], prv[:, pi_, s, :],
                                                  xv[:, s, ts * TC:(ts + 1) * TC],
                                                  start=(s == 0), stop=(s == NSTR - 1))
                                TE.inc(mm)
                    for j in range(NVT):
                        t.wait_ge(VE.h, cum_ve(fi_of(hh, NE - 1, NTS - 1)))
                        if j >= 2:
                            t.wait_ge(SE.h, se_vt(hh, j - 2))
                        TE.inc(t.transpose(ptv[j % 2][:],
                                           vacc[:, j * 128:(j + 1) * 128], id128[:]))

            @b.scalar
            def _(sc):
                for hh in range(NHH):
                    for e in range(NE):
                        for ts in range(NTS):
                            fi = fi_of(hh, e, ts)
                            sc.wait_ge(TE.h, te(fi) + 1)
                            if fi >= 2:
                                sc.wait_ge(VE.h, cum_ve(fi - 2))
                            SE.inc(sc.copy(wbb[fi % 2][:], pw[fi % 2][:]))
                    for j in range(NVT):
                        sc.wait_ge(TE.h, te_vt(hh, j))
                        SE.inc(sc.copy(vtile[j][:], ptv[j % 2][:]))

            @b.vector
            def _(v):
                for hh in range(NHH):
                    for e in range(NE):
                        for ts in range(NTS):
                            fi = fi_of(hh, e, ts)
                            sl = slice(ts * TC, (ts + 1) * TC)
                            v.wait_ge(SE.h, se_wbb(fi))
                            for pi_ in range(3):
                                v.wait_ge(TE.h, te(fi) + 1 + pi_ + 1)
                                if e == 0:
                                    VE.inc(v.tensor_mul(accs[pi_][:, sl], psP[pi_][:],
                                                        wbb[fi % 2][:]))
                                else:
                                    v.wait_ge(VE.h, VE.n)
                                    VE.inc(v.tensor_mul(tmpb[:], psP[pi_][:],
                                                        wbb[fi % 2][:]))
                                    v.wait_ge(VE.h, VE.n)
                                    VE.inc(v.tensor_add(accs[pi_][:, sl],
                                                        accs[pi_][:, sl], tmpb[:]))


def phase_moeattn(env, T, dram):
    nc = env.nc
    nck = T // TC
    with Phase(env, "p5c") as ph:
        qT = ph.sbuf("qT", [128, T], BF16)
        kT = ph.sbuf("kT", [128, T], BF16)
        vna = ph.sbuf("vna", [128, (T // 128) * HD], BF16)
        ones_bf = ph.sbuf("ones_bf", [128, 1], BF16)
        ones_row = ph.sbuf("ones_row", [1, 128], F32)
        zmask = ph.sbuf("zmask", [128, TC], F32)
        masks = [ph.sbuf(f"mask{r}", [128, TC], F32) for r in range(4)]
        expb = [ph.sbuf(f"expb{i}", [128, TC], BF16) for i in range(2)]
        mt = [ph.sbuf(f"mt{i}", [128, TC], F32) for i in range(2)]
        denr = ph.sbuf("denr", [1, TC], F32)
        rbf = ph.sbuf("rbf", [128, TC], F32)
        aob = ph.sbuf("aob", [128, TC], BF16)
        psc = [ph.psum(f"psc{i}", [128, TC]) for i in range(2)]
        pctx = ph.psum("pctx", [128, TC])
        pden = ph.psum("pden", [1, TC])
        pbb = ph.psum("pbb", [128, TC])
        WD = ph.sem("WD")
        MS = ph.sem("MS")
        TE = ph.sem("TE")
        SE = ph.sem("SE")
        VE = ph.sem("VE")
        ST = ph.sem("ST")
        CD = ph.sem("CD")
        bTE, bSE, bVE, bST, bCD = (ph.base(x) for x in ("TE", "SE", "VE", "ST", "CD"))

        def njt(i):
            return 4 * (i + 1)

        def te_c(i):
            return bTE + sum(njt(k) + 2 for k in range(i))

        def se_c(i):
            return bSE + sum(njt(k) + 1 for k in range(i))

        def ve_c(i):
            return bVE + 6 * i

        def cdw(eng, i):
            if i >= 1:
                eng.wait_ge(CD.h, bCD + i)

        with nc.Block() as b:
            @b.gpsimd
            def _(g):
                WD.inc(g.dma_start(qT[:], dram["qT_bf"][:]), 16)
                WD.inc(g.dma_start(kT[:], dram["kT_bf"][:]), 16)
                WD.inc(g.dma_start(vna[:].rearrange("p (j c) -> p j c", j=T // 128),
                                   dram["Vn2_bf"][:].rearrange("(j p) c -> p j c", p=128)), 16)
                g.memset(ones_bf[:], 1.0)
                g.memset(ones_row[:], 1.0)
                MS.inc(g.memset(zmask[:], 0.0))
                g.wait_ge(MS.h, MS.n)
                for r in range(4):
                    MS.inc(g.affine_select(masks[r][:], zmask[:], pattern=[[1, TC]],
                                           compare_op=ALU.is_ge, fill=-1.0e30,
                                           base=-(r * 128), channel_multiplier=-1))

            @b.tensor
            def _(t):
                t.wait_ge(WD.h, WD.n)
                t.wait_ge(MS.h, MS.n)
                for i in range(nck):
                    cdw(t, i)
                    tec = te_c(i)
                    sec = se_c(i)
                    vec = ve_c(i)
                    n = njt(i)
                    for j in range(n):
                        diag = j >= 4 * i
                        if j >= 2:
                            jd = j - 2
                            if jd >= 4 * i:
                                t.wait_ge(VE.h, vec + (jd - 4 * i) + 1)
                            else:
                                t.wait_ge(SE.h, sec + jd + 1)
                        TE.inc(t.matmul(psc[j % 2][:], kT[:, j * 128:(j + 1) * 128],
                                        qT[:, i * TC:(i + 1) * TC],
                                        start=True, stop=True))
                        if j >= 1:
                            t.wait_ge(SE.h, sec + j)
                            t.matmul(pctx[:], vna[:, (j - 1) * HD:j * HD],
                                     expb[(j - 1) % 2][:], start=(j == 1), stop=False)
                            t.matmul(pden[:], ones_bf[:, :], expb[(j - 1) % 2][:],
                                     start=(j == 1), stop=False)
                    t.wait_ge(SE.h, sec + n)
                    t.matmul(pctx[:], vna[:, (n - 1) * HD:n * HD], expb[(n - 1) % 2][:],
                             start=False, stop=True)
                    TE.inc(t.matmul(pden[:], ones_bf[:, :], expb[(n - 1) % 2][:],
                                    start=False, stop=True))
                    t.wait_ge(SE.h, sec + n + 1)
                    TE.inc(t.matmul(pbb[:], ones_row[:, :], denr[:], start=True, stop=True))

            @b.scalar
            def _(sc):
                for i in range(nck):
                    cdw(sc, i)
                    tec = te_c(i)
                    sec = se_c(i)
                    vec = ve_c(i)
                    n = njt(i)
                    for j in range(n):
                        diag = j >= 4 * i
                        if diag:
                            sc.wait_ge(VE.h, vec + (j - 4 * i) + 1)
                            SE.inc(sc.activation(expb[j % 2][:], mt[j % 2][:],
                                                 AF.Exp, scale=RS))
                        else:
                            sc.wait_ge(TE.h, tec + j + 1)
                            SE.inc(sc.activation(expb[j % 2][:], psc[j % 2][:],
                                                 AF.Exp, scale=RS))
                    sc.wait_ge(TE.h, tec + n + 1)
                    SE.inc(sc.copy(denr[:], pden[:]))

            @b.vector
            def _(v):
                for i in range(nck):
                    cdw(v, i)
                    tec = te_c(i)
                    sec = se_c(i)
                    vec = ve_c(i)
                    n = njt(i)
                    for j in range(4 * i, n):
                        r = j - 4 * i
                        v.wait_ge(TE.h, tec + j + 1)
                        if r >= 2:
                            v.wait_ge(SE.h, sec + j - 1)
                        VE.inc(v.tensor_add(mt[j % 2][:], psc[j % 2][:], masks[r][:]))
                    v.wait_ge(TE.h, tec + n + 2)
                    VE.inc(v.reciprocal(rbf[:], pbb[:]))
                    v.wait_ge(VE.h, VE.n)
                    VE.inc(v.tensor_mul(aob[:], pctx[:], rbf[:]))

            @b.sync
            def _(s):
                for i in range(nck):
                    s.wait_ge(VE.h, ve_c(i + 1))
                    ST.inc(s.dma_start(dram["aoT_bf"][:, i * TC:(i + 1) * TC], aob[:]), 16)
                    s.wait_ge(ST.h, bST + 16 * (i + 1))
                    CD.bump(s)


def phase_moeout(env, T, dram):
    nc = env.nc
    nck = T // TC
    with Phase(env, "p5d") as ph:
        opb = ph.sbuf("opb", [128, NE * D], BF16)
        aoT = ph.sbuf("aoT", [128, T], BF16)
        wrk = [ph.sbuf(f"wrk{i}", [1, TC], F32) for i in range(2)]
        wbb = [ph.sbuf(f"wbb{i}", [128, TC], F32) for i in range(2)]
        waob = ph.sbuf("waob", [128, NE * TC], BF16)
        xf = ph.sbuf("xf", [128, NSTR * TC], F32)
        y = ph.sbuf("y", [128, NSTR * TC], F32)
        ones_row = ph.sbuf("ones_row", [1, 128], F32)
        ones_col = ph.sbuf("ones_col", [128, 1], F32)
        tmp = ph.sbuf("tmp", [128, TC], F32)
        sq = tmp
        mrow = ph.sbuf("mrow", [1, TC], F32)
        qrow = ph.sbuf("qrow", [1, TC], F32)
        m2row = ph.sbuf("m2row", [1, TC], F32)
        varrow = ph.sbuf("varrow", [1, TC], F32)
        rrow = ph.sbuf("rrow", [1, TC], F32)
        lnrow = m2row
        srow = qrow
        rbv = ph.sbuf("rbv", [128, TC], F32)
        shb = ph.sbuf("shb", [128, TC], F32)
        pw = [ph.psum(f"pw{i}", [128, TC]) for i in range(2)]
        po = [ph.psum(f"po{i}", [128, TC]) for i in range(2)]
        ps1 = ph.psum("ps1", [1, TC])
        ps2 = ph.psum("ps2", [1, TC])
        pr = ph.psum("pr", [128, TC])
        psh = ph.psum("psh", [128, TC])
        WD = ph.sem("WD")
        MS = ph.sem("MS")
        RWp = [ph.sem("RW0"), ph.sem("RW1")]
        LD = ph.sem("LD0")
        TE = ph.sem("TE")
        SE = ph.sem("SE")
        VE = ph.sem("VE")
        ST = ph.sem("ST")
        CD = ph.sem("CD")
        bTE, bSE, bVE, bST, bCD = (ph.base(x) for x in ("TE", "SE", "VE", "ST", "CD"))
        bRW = [ph.base("RW0"), ph.base("RW1")]
        bLD = ph.base("LD0")

        TEC = NE + NSTR + NSTR + 2
        SEC = NE + NSTR + NSTR + 2 + 2
        VEC = NE + NSTR + 6 + 2 * NSTR

        def cdw(eng, i):
            if i >= 1:
                eng.wait_ge(CD.h, bCD + i)

        with nc.Block() as b:
            @b.gpsimd
            def _(g):
                WD.inc(g.dma_start(opb[:].rearrange("p (e c) -> p e c", e=NE),
                                   stripes_ap(dram["OP"])), 16)
                WD.inc(g.dma_start(aoT[:], dram["aoT_bf"][:]), 16)
                g.memset(ones_col[:], 1.0)
                MS.inc(g.memset(ones_row[:], 1.0))
                for i in range(nck):
                    cdw(g, i)
                    LD.inc(g.dma_start(sb3(xf, NSTR), xchunk(dram["x3"], i * TC, TC)), 16)
                    for e in range(NE):
                        fi = i * NE + e
                        if e >= 2:
                            g.wait_ge(TE.h, bTE + i * TEC + e - 1)
                        RWp[fi % 2].inc(g.dma_start(
                            wrk[fi % 2][:],
                            dram["wTr"][e:e + 1, i * TC:(i + 1) * TC]), 16)

            @b.tensor
            def _(t):
                t.wait_ge(WD.h, WD.n)
                t.wait_ge(MS.h, MS.n)
                for i in range(nck):
                    cdw(t, i)
                    tec = bTE + i * TEC
                    sec = bSE + i * SEC
                    vec = bVE + i * VEC
                    for e in range(NE):
                        fi = i * NE + e
                        t.wait_ge(RWp[fi % 2].h, bRW[fi % 2] + 16 * (fi // 2 + 1))
                        if e >= 2:
                            t.wait_ge(SE.h, sec + e - 1)
                        TE.inc(t.matmul(pw[fi % 2][:], ones_row[:, :], wrk[fi % 2][:],
                                        start=True, stop=True))
                    t.wait_ge(VE.h, vec + NE)
                    for o in range(NSTR):
                        if o >= 2:
                            t.wait_ge(SE.h, sec + NE + o - 1)
                        for e in range(NE):
                            mm = t.matmul(po[o % 2][:],
                                          opb[:, e * D + o * HD:e * D + (o + 1) * HD],
                                          waob[:, e * TC:(e + 1) * TC],
                                          start=(e == 0), stop=(e == NE - 1))
                        TE.inc(mm)
                    for s in range(NSTR):
                        t.wait_ge(VE.h, vec + NE + s + 1)
                        t.matmul(ps1[:], ones_col[:, :], y[:, s * TC:(s + 1) * TC],
                                 start=(s == 0), stop=(s == NSTR - 1))
                        t.wait_ge(SE.h, sec + NE + NSTR + s + 1)
                        TE.inc(t.matmul(ps2[:], ones_col[:, :], sq[:],
                                        start=(s == 0), stop=(s == NSTR - 1)))
                    t.wait_ge(SE.h, sec + NE + NSTR + NSTR + 2)
                    TE.inc(t.matmul(pr[:], ones_row[:, :], rrow[:], start=True, stop=True))
                    t.wait_ge(VE.h, vec + NE + NSTR + 6)
                    TE.inc(t.matmul(psh[:], ones_row[:, :], srow[:], start=True, stop=True))

            @b.scalar
            def _(sc):
                for i in range(nck):
                    cdw(sc, i)
                    tec = bTE + i * TEC
                    sec = bSE + i * SEC
                    vec = bVE + i * VEC
                    for e in range(NE):
                        fi = i * NE + e
                        sc.wait_ge(TE.h, tec + e + 1)
                        if e >= 2:
                            sc.wait_ge(VE.h, vec + e - 1)
                        SE.inc(sc.copy(wbb[fi % 2][:], pw[fi % 2][:]))
                    for o in range(NSTR):
                        sc.wait_ge(TE.h, tec + NE + o + 1)
                        SE.inc(sc.copy(y[:, o * TC:(o + 1) * TC], po[o % 2][:]))
                    for s in range(NSTR):
                        sc.wait_ge(VE.h, vec + NE + s + 1)
                        if s >= 1:
                            sc.wait_ge(TE.h, tec + NE + NSTR + s)
                        SE.inc(sc.activation(sq[:], y[:, s * TC:(s + 1) * TC], AF.Square))
                    sc.wait_ge(VE.h, vec + NE + NSTR + 5)
                    SE.inc(sc.activation(lnrow[:], varrow[:], AF.Ln))
                    sc.wait_ge(SE.h, SE.n)
                    SE.inc(sc.activation(rrow[:], lnrow[:], AF.Exp, scale=-0.5))
                    sc.wait_ge(TE.h, tec + NE + NSTR + NSTR + 1)
                    SE.inc(sc.copy(rbv[:], pr[:]))
                    sc.wait_ge(TE.h, tec + NE + NSTR + NSTR + 2)
                    SE.inc(sc.copy(shb[:], psh[:]))

            @b.vector
            def _(v):
                for i in range(nck):
                    cdw(v, i)
                    tec = bTE + i * TEC
                    sec = bSE + i * SEC
                    vec = bVE + i * VEC
                    for e in range(NE):
                        v.wait_ge(SE.h, sec + e + 1)
                        VE.inc(v.tensor_mul(waob[:, e * TC:(e + 1) * TC],
                                            aoT[:, i * TC:(i + 1) * TC],
                                            wbb[(i * NE + e) % 2][:]))
                    v.wait_ge(LD.h, bLD + 16 * (i + 1))
                    for s in range(NSTR):
                        v.wait_ge(SE.h, sec + NE + s + 1)
                        VE.inc(v.tensor_add(y[:, s * TC:(s + 1) * TC],
                                            y[:, s * TC:(s + 1) * TC],
                                            sb3(xf, NSTR)[:, s, :]))
                    v.wait_ge(TE.h, tec + NE + NSTR + NSTR)
                    VE.inc(v.tensor_scalar(mrow[:], ps1[:], 1.0 / D, None, ALU.mult))
                    VE.inc(v.tensor_scalar(qrow[:], ps2[:], 1.0 / D, None, ALU.mult))
                    v.wait_ge(VE.h, VE.n)
                    VE.inc(v.tensor_mul(m2row[:], mrow[:], mrow[:]))
                    v.wait_ge(VE.h, VE.n)
                    VE.inc(v.tensor_sub(varrow[:], qrow[:], m2row[:]))
                    v.wait_ge(VE.h, VE.n)
                    VE.inc(v.tensor_scalar(varrow[:], varrow[:], LN_EPS, None, ALU.add))
                    v.wait_ge(SE.h, sec + NE + NSTR + NSTR + 2)
                    v.wait_ge(VE.h, VE.n)
                    VE.inc(v.tensor_mul(srow[:], mrow[:], rrow[:]))
                    v.wait_ge(SE.h, sec + SEC)
                    for s in range(NSTR):
                        sl = slice(s * TC, (s + 1) * TC)
                        VE.inc(v.tensor_mul(tmp[:], y[:, sl], rbv[:]))
                        v.wait_ge(VE.h, VE.n)
                        VE.inc(v.tensor_sub(y[:, sl], tmp[:], shb[:]))
                        v.wait_ge(VE.h, VE.n)

            @b.sync
            def _(s):
                for i in range(nck):
                    s.wait_ge(VE.h, bVE + (i + 1) * VEC)
                    ST.inc(s.dma_start(xchunk(dram["outT"], i * TC, TC), sb3(y, NSTR)), 16)
                    s.wait_ge(ST.h, bST + 16 * (i + 1))
                    CD.bump(s)


def _shapes(T):
    return {
        "xT": ((D, T), F32),
        "Wq": ((D, D), F32), "bq2": ((128, NSTR), F32),
        "Wk": ((D, HD), F32), "bk": ((128, 1), F32),
        "Wv": ((D, HD), F32), "bv": ((1, HD), F32),
        "Wo": ((D, D), F32), "bo2": ((128, NSTR), F32),
        "w1T": ((D, H2), F32), "w2T": ((D, H2), F32), "w3T": ((H2, D), F32),
        "cosF": ((128, T), F32), "sinF": ((128, T), F32),
        "KR_bf": ((HD, T), BF16), "Vn_bf": ((T, HD), BF16),
        "ctxT_bf": ((D, T), BF16), "den": ((NH, T), F32),
        "x1": ((D, T), F32),
        "thr": ((128, 8), F32),
        "hid_bf": ((H2, T), BF16),
        "y2": ((D, T), F32),
        "x2": ((D, T), F32),
        "Wdt": ((D, D), F32), "Wb": ((D, D), F32), "Wc": ((D, D), F32),
        "biadbc": ((128, 3 * NSTR), F32),
        "a_rawT": ((128, NE * NE), F32), "BbT": ((128, NE * NE), F32),
        "CbT": ((128, NE * NE), F32), "Dt2": ((128, NE), F32),
        "Asb": ((128, NE * NE), F32),
        "dtc": ((D, T), F32), "bco": ((D, T), BF16), "cco": ((D, T), BF16),
        "y3": ((D, T), F32), "x3": ((D, T), F32),
        "sim": ((D, NE), F32), "gates": ((1, NE), F32),
        "QP": ((NE * D, HD), F32), "KP": ((NE * D, HD), F32), "VP": ((NE * D, HD), F32),
        "OP": ((NE * HD, D), F32),
        "rsn_t": ((1, NE), F32), "sg_t": ((1, NE), F32),
        "wTr": ((NE, T), F32),
        "qT_bf": ((HD, T), BF16), "kT_bf": ((HD, T), BF16),
        "Vn2_bf": ((T, HD), BF16), "aoT_bf": ((HD, T), BF16),
        "y4": ((D, T), F32), "outT": ((D, T), F32),
    }


PHASES = [
    ("p0", phase_tables, [], ["cosF", "sinF"]),
    ("p2a", phase_kv, ["xT", "Wk", "Wv", "bk", "bv", "cosF", "sinF"],
     ["KR_bf", "Vn_bf"]),
    ("p2b", phase_qctx, ["xT", "Wq", "bq2", "cosF", "sinF", "KR_bf", "Vn_bf"],
     ["ctxT_bf", "den"]),
    ("p2c", phase_attnout, ["xT", "ctxT_bf", "den", "Wo", "bo2"], ["x1"]),
    ("p3a", phase_gamma, ["w1T", "w2T", "w3T"], ["thr"]),
    ("p3b", phase_ffn1, ["x1", "w1T", "w2T", "thr"], ["hid_bf"]),
    ("p3c", phase_ffn2, ["x1", "w3T", "thr", "hid_bf"], ["y2"]),
    ("p3d", make_phase_ln("y2", "x2", "p3d"), ["y2"], ["x2"]),
    ("p4a", phase_ssmproj, ["x2", "Wdt", "Wb", "Wc", "biadbc", "a_rawT"],
     ["Asb", "dtc", "bco", "cco"]),
    ("p4b", phase_ssmscan, ["x2", "dtc", "bco", "cco", "Asb", "BbT", "CbT", "Dt2"],
     ["x3"]),
    ("p5a", phase_moeroute, ["x3", "sim", "gates"], ["wTr", "rsn_t", "sg_t"]),
    ("p5b", phase_moemix, ["x3", "wTr", "QP", "KP", "VP"],
     ["qT_bf", "kT_bf", "Vn2_bf"]),
    ("p5c", phase_moeattn, ["qT_bf", "kT_bf", "Vn2_bf"], ["aoT_bf"]),
    ("p5d", phase_moeout, ["x3", "wTr", "OP", "aoT_bf"], ["outT"]),
]


def np_dt(dt):
    return {F32: np.float32, BF16: ml_dtypes.bfloat16, I32: np.int32}[dt]


def build_phase_program(name, fn, ins, outs, T):
    nc = bass.Bass("TRN2", target_bir_lowering=False)
    env = Env(nc)
    shp = _shapes(T)
    dram = {}
    for k in ins:
        dram[k] = nc.dram_tensor(k, list(shp[k][0]), shp[k][1], kind="ExternalInput")
    for k in outs:
        dram[k] = nc.dram_tensor(k, list(shp[k][0]), shp[k][1], kind="ExternalOutput")
    fn(env, T, dram)
    nc.finalize()
    return nc


def host_prep(inputs, T):
    def t32(a):
        return np.ascontiguousarray(np.asarray(a, dtype=np.float32))

    x = np.asarray(inputs["L0_attn_x"])
    B = x.shape[0]
    shared = {
        "Wq": t32(inputs["L0_attn_Wq"]),
        "bq2": t32(np.asarray(inputs["L0_attn_bq"]).reshape(NSTR, 128).T),
        "Wk": t32(inputs["L0_attn_Wk"]),
        "bk": t32(np.asarray(inputs["L0_attn_bk"]).reshape(128, 1)),
        "Wv": t32(inputs["L0_attn_Wv"]),
        "bv": t32(np.asarray(inputs["L0_attn_bv"]).reshape(1, 128)),
        "Wo": t32(inputs["L0_attn_Wo"]),
        "bo2": t32(np.asarray(inputs["L0_attn_bo"]).reshape(NSTR, 128).T),
        "w1T": t32(np.asarray(inputs["L0_ffn_w1"]).T),
        "w2T": t32(np.asarray(inputs["L0_ffn_w2"]).T),
        "w3T": t32(np.asarray(inputs["L0_ffn_w3"]).T),
        "Wdt": t32(inputs["L1_attn_Wdt"]),
        "Wb": t32(inputs["L1_attn_Wb"]),
        "Wc": t32(inputs["L1_attn_Wc"]),
        "biadbc": t32(np.concatenate([
            np.asarray(inputs["L1_attn_bdt"]).reshape(NSTR, 128).T,
            np.asarray(inputs["L1_attn_bb"]).reshape(NSTR, 128).T,
            np.asarray(inputs["L1_attn_bc"]).reshape(NSTR, 128).T], axis=1)),
        "a_rawT": t32(np.asarray(inputs["L1_attn_a_raw"]).reshape(NSTR, 128, NE)
                      .transpose(1, 0, 2).reshape(128, NSTR * NE)),
        "BbT": t32(np.asarray(inputs["L1_attn_B_base"]).reshape(NSTR, 128, NE)
                   .transpose(1, 0, 2).reshape(128, NSTR * NE)),
        "CbT": t32(np.asarray(inputs["L1_attn_C_base"]).reshape(NSTR, 128, NE)
                   .transpose(1, 0, 2).reshape(128, NSTR * NE)),
        "Dt2": t32(np.asarray(inputs["L1_attn_D"]).reshape(NSTR, 128).T),
        "sim": t32(inputs["L1_ffn_sim_matrix"]),
        "gates": t32(np.asarray(inputs["L1_ffn_gates"]).reshape(1, NE)),
        "QP": t32(np.asarray(inputs["L1_ffn_q_proj"]).reshape(NE * D, HD)),
        "KP": t32(np.asarray(inputs["L1_ffn_k_proj"]).reshape(NE * D, HD)),
        "VP": t32(np.asarray(inputs["L1_ffn_v_proj"]).reshape(NE * D, HD)),
        "OP": t32(np.asarray(inputs["L1_ffn_o_proj"]).reshape(NE * HD, D)),
    }
    cores = []
    for b in range(B):
        d = dict(shared)
        d["xT"] = t32(x[b].T)
        cores.append(d)
    return cores


def run_phases_multi(state, T, phases=None, core_ids=None):
    phases = phases if phases is not None else PHASES
    core_ids = core_ids if core_ids is not None else list(range(len(state)))
    for (name, fn, ins, outs) in phases:
        nc = build_phase_program(name, fn, ins, outs, T)
        in_maps = [{k: np.ascontiguousarray(state[c][k]) for k in ins}
                   for c in range(len(state))]
        res = run_bass_kernel_spmd(nc, in_maps, core_ids=core_ids)
        for c in range(len(state)):
            for k in outs:
                state[c][k] = res.results[c][k]
    return state


EXT_IN = ["xT", "Wq", "bq2", "Wk", "bk", "Wv", "bv", "Wo", "bo2",
          "w1T", "w2T", "w3T",
          "Wdt", "Wb", "Wc", "biadbc", "a_rawT", "BbT", "CbT", "Dt2",
          "sim", "gates", "QP", "KP", "VP", "OP"]


def build_full_program(T):
    nc = bass.Bass("TRN2", target_bir_lowering=False)
    env = Env(nc)
    shp = _shapes(T)
    dram = {}
    for k, (shape, dt) in shp.items():
        if k in EXT_IN:
            kind = "ExternalInput"
        elif k == "outT":
            kind = "ExternalOutput"
        else:
            kind = "Internal"
        dram[k] = nc.dram_tensor(k, list(shape), dt, kind=kind)
    for (name, fn, ins, outs) in PHASES:
        fn(env, T, dram)
    nc.finalize()
    return nc


def kernel(**inputs):
    x = np.asarray(inputs["L0_attn_x"])
    B, T, _ = x.shape
    state = host_prep(inputs, T)
    if MODE == "multi":
        run_phases_multi(state, T)
        out = np.stack([np.asarray(s["outT"], dtype=np.float32).T for s in state])
        return out
    nc = build_full_program(T)
    in_maps = [{k: np.ascontiguousarray(state[c][k]) for k in EXT_IN}
               for c in range(B)]
    res = run_bass_kernel_spmd(nc, in_maps, core_ids=list(range(B)))
    out = np.stack([np.asarray(res.results[c]["outT"], dtype=np.float32).T
                    for c in range(B)])
    return out



# version 17 (final)
# speedup vs baseline: 1.0007x; 1.0007x over previous
import math
from contextlib import ExitStack

import numpy as np
import ml_dtypes

import concourse.bass as bass
import concourse.mybir as mybir
from concourse.bass_utils import run_bass_kernel_spmd

F32 = mybir.dt.float32
BF16 = mybir.dt.bfloat16
I32 = mybir.dt.int32
AF = mybir.ActivationFunctionType
ALU = mybir.AluOpType
AX = mybir.AxisListType

D = 2048
NSTR = 16
HD = 128
NH = 16
H2 = 5632
NJ = H2 // 128
NE = 16
TC = 512
NT = TC // 128
RS = 1.0 / math.sqrt(128.0)
LN_EPS = 1e-5

MODE = "single"


class Sem:
    def __init__(self, h):
        self.h = h
        self.n = 0

    def inc(self, inst, k=1):
        inst.then_inc(self.h, k)
        self.n += k
        return inst

    def bump(self, eng, k=1):
        eng.nop(nofuse=True).then_inc(self.h, k)
        self.n += k


class Env:
    def __init__(self, nc):
        self.nc = nc
        self.es = ExitStack()
        self._sems = {}

    def sem(self, name):
        if name not in self._sems:
            self._sems[name] = Sem(self.es.enter_context(self.nc.semaphore(name)))
        return self._sems[name]


class Phase:

    def __init__(self, env, name):
        self.env = env
        self.nc = env.nc
        self.name = name
        self.es = ExitStack()
        self._bases = {}

    def sbuf(self, name, shape, dt):
        return self.es.enter_context(self.nc.sbuf_tensor(f"{self.name}_{name}", shape, dt))

    def psum(self, name, shape, dt=F32):
        return self.es.enter_context(self.nc.psum_tensor(f"{self.name}_{name}", shape, dt))

    def sem(self, name):
        s = self.env.sem(name)
        if name not in self._bases:
            self._bases[name] = s.n
        return s

    def base(self, name):
        self.sem(name)
        return self._bases[name]

    def __enter__(self):
        return self

    def __exit__(self, *a):
        self.es.close()
        return False


def xchunk(xT, t0, tc):
    return xT[:, t0:t0 + tc].rearrange("(s p) c -> p s c", p=128)


def sb3(t, nstr):
    return t[:].rearrange("p (s c) -> p s c", s=nstr)


def stripes_ap(w):
    return w[:].rearrange("(s p) c -> p s c", p=128)


def phase_tables(env, T, dram):
    nc = env.nc
    with Phase(env, "p0") as ph:
        pos_i = ph.sbuf("pos_i", [64, T], I32)
        fid_i = ph.sbuf("fid_i", [64, 1], I32)
        fid_f = ph.sbuf("fid_f", [64, 1], F32)
        freq = ph.sbuf("freq", [64, 1], F32)
        ang = ph.sbuf("ang", [64, T], F32)
        ang2 = ph.sbuf("ang2", [64, T], F32)
        r = ph.sbuf("r", [64, T], F32)
        ki = ph.sbuf("ki", [64, T], I32)
        kf = ph.sbuf("kf", [64, T], F32)
        m1 = ph.sbuf("m1", [64, T], F32)
        m2 = ph.sbuf("m2", [64, T], F32)
        trig = ph.sbuf("trig", [64, T], F32)
        cosF = ph.sbuf("cosF", [128, T], F32)
        sinF = ph.sbuf("sinF", [128, T], F32)
        sg = ph.sem("G")
        sv = ph.sem("V")
        ss = ph.sem("S")
        so = ph.sem("O")

        def emit_reduce(v, src):
            sv.inc(v.tensor_scalar(ki[:], src[:], 1.0 / (2 * math.pi), None, ALU.mult))
            v.wait_ge(sv.h, sv.n)
            sv.inc(v.tensor_copy(kf[:], ki[:]))
            v.wait_ge(sv.h, sv.n)
            sv.inc(v.scalar_tensor_tensor(r[:], kf[:], -2 * math.pi, src[:], ALU.mult, ALU.add))
            v.wait_ge(sv.h, sv.n)
            sv.inc(v.tensor_scalar(m1[:], r[:], math.pi, None, ALU.is_gt))
            sv.inc(v.tensor_scalar(m2[:], r[:], -math.pi, None, ALU.is_lt))
            v.wait_ge(sv.h, sv.n)
            sv.inc(v.tensor_sub(m1[:], m1[:], m2[:]))
            v.wait_ge(sv.h, sv.n)
            sv.inc(v.scalar_tensor_tensor(r[:], m1[:], -2 * math.pi, r[:], ALU.mult, ALU.add))

        with nc.Block() as b:
            @b.gpsimd
            def _(g):
                sg.inc(g.iota(pos_i[:], pattern=[[1, T]], channel_multiplier=0))
                sg.inc(g.iota(fid_i[:], pattern=[[0, 1]], channel_multiplier=1))

            @b.vector
            def _(v):
                v.wait_ge(sg.h, sg.n)
                sv.inc(v.tensor_copy(fid_f[:], fid_i[:]))
                sv.inc(v.tensor_copy(ang[:], pos_i[:]))

            @b.scalar
            def _(sc):
                sc.wait_ge(sv.h, sv.n)
                ss.inc(sc.activation(freq[:], fid_f[:], AF.Exp,
                                     scale=-2.0 * math.log(10000.0) / 128.0))

            v_base = sv.n

            @b.vector
            def _(v):
                v.wait_ge(ss.h, ss.n)
                sv.inc(v.tensor_scalar(ang[:], ang[:], freq[:, 0:1], None, ALU.mult))
                v.wait_ge(sv.h, sv.n)
                sv.inc(v.tensor_scalar(ang2[:], ang[:], math.pi / 2, None, ALU.add))
                emit_reduce(v, ang)

            @b.scalar
            def _(sc):
                sc.wait_ge(sv.h, sv.n)
                ss.inc(sc.activation(trig[:], r[:], AF.Sin))

            @b.vector
            def _(v):
                v.wait_ge(ss.h, ss.n)
                sv.inc(v.tensor_scalar(sinF[0:64, :], trig[:], -1.0, None, ALU.mult))
                sv.inc(v.tensor_copy(sinF[64:128, :], trig[:]))
                emit_reduce(v, ang2)

            @b.scalar
            def _(sc):
                sc.wait_ge(sv.h, sv.n)
                ss.inc(sc.activation(trig[:], r[:], AF.Sin))

            @b.vector
            def _(v):
                v.wait_ge(ss.h, ss.n)
                sv.inc(v.tensor_copy(cosF[0:64, :], trig[:]))
                sv.inc(v.tensor_copy(cosF[64:128, :], trig[:]))

            @b.sync
            def _(s):
                s.wait_ge(sv.h, sv.n)
                so.inc(s.dma_start(dram["cosF"][:], cosF[:]), 16)
                so.inc(s.dma_start(dram["sinF"][:], sinF[:]), 16)
                s.wait_ge(so.h, so.n)


def phase_kv(env, T, dram):
    nc = env.nc
    nck = T // TC
    with Phase(env, "p2a") as ph:
        wk = ph.sbuf("wk", [128, NSTR * HD], BF16)
        wv = ph.sbuf("wv", [128, NSTR * HD], BF16)
        bkc = ph.sbuf("bkc", [128, 1], F32)
        bvr = ph.sbuf("bvr", [1, HD], F32)
        bvb = ph.sbuf("bvb", [128, HD], F32)
        ones_row = ph.sbuf("ones_row", [1, 128], F32)
        cosT = ph.sbuf("cosT", [128, T], BF16)
        sinT = ph.sbuf("sinT", [128, T], BF16)
        xb = [ph.sbuf(f"xb{i}", [128, NSTR * TC], BF16) for i in range(2)]
        kf = [ph.sbuf(f"kf{i}", [128, TC], F32) for i in range(2)]
        ksw = [ph.sbuf(f"ksw{i}", [128, TC], F32) for i in range(2)]
        kt1 = [ph.sbuf(f"kt1{i}", [128, TC], F32) for i in range(2)]
        krb = [ph.sbuf(f"krb{i}", [128, TC], BF16) for i in range(2)]
        vsb = [ph.sbuf(f"vsb{i}", [128, NT * HD], BF16) for i in range(2)]
        pk = [ph.psum(f"pk{i}", [128, TC]) for i in range(2)]
        pv = [ph.psum(f"pv{i}", [128, HD]) for i in range(4)]
        pb = ph.psum("pb", [128, HD])

        LDp = [ph.sem("LD0"), ph.sem("LD1")]
        WD = ph.sem("WD")
        MS = ph.sem("MS")
        TE = ph.sem("TE")
        SE = ph.sem("SE")
        VE = ph.sem("VE")
        SW = ph.sem("SW")
        ST = ph.sem("ST")
        CD = ph.sem("CD")
        bTE, bSE, bVE, bSW, bST, bCD = (ph.base(x) for x in
                                        ("TE", "SE", "VE", "SW", "ST", "CD"))
        bLD = [ph.base("LD0"), ph.base("LD1")]

        def cdw(eng, i):
            if i >= 1:
                eng.wait_ge(CD.h, bCD + i)

        with nc.Block() as b:
            @b.gpsimd
            def _(g):
                WD.inc(g.dma_start(sb3(wk, NSTR), stripes_ap(dram["Wk"])), 16)
                WD.inc(g.dma_start(sb3(wv, NSTR), stripes_ap(dram["Wv"])), 16)
                WD.inc(g.dma_start(bkc[:], dram["bk"][:]), 16)
                WD.inc(g.dma_start(bvr[:], dram["bv"][:]), 16)
                WD.inc(g.dma_start(cosT[:], dram["cosF"][:]), 16)
                WD.inc(g.dma_start(sinT[:], dram["sinF"][:]), 16)
                MS.inc(g.memset(ones_row[:], 1.0))
                for i in range(nck):
                    cdw(g, i - 1)
                    LDp[i % 2].inc(g.dma_start(sb3(xb[i % 2], NSTR),
                                               xchunk(dram["xT"], i * TC, TC)), 16)

            @b.tensor
            def _(t):
                t.wait_ge(WD.h, WD.n)
                t.wait_ge(MS.h, MS.n)
                TE.inc(t.matmul(pb[:], ones_row[:, :], bvr[:], start=True, stop=True))
                for i in range(nck):
                    cdw(t, i)
                    t.wait_ge(LDp[i % 2].h, bLD[i % 2] + 16 * (i // 2 + 1))
                    xv = sb3(xb[i % 2], NSTR)
                    for s in range(NSTR):
                        mm = t.matmul(pk[i % 2][:], wk[:, s * HD:(s + 1) * HD], xv[:, s, :],
                                      start=(s == 0), stop=(s == NSTR - 1))
                    TE.inc(mm)
                    for j in range(NT):
                        for s in range(NSTR):
                            mm = t.matmul(pv[j][:], xv[:, s, j * 128:(j + 1) * 128],
                                          wv[:, s * HD:(s + 1) * HD],
                                          start=(s == 0), stop=(s == NSTR - 1))
                        TE.inc(mm)

            @b.scalar
            def _(sc):
                for i in range(nck):
                    cdw(sc, i)
                    sc.wait_ge(TE.h, bTE + 1 + i * 5 + 1)
                    SE.inc(sc.activation(kf[i % 2][:], pk[i % 2][:], AF.Identity,
                                         bias=bkc[:, 0:1]))
                    sc.wait_ge(SE.h, bSE + i + 1)
                    SW.inc(sc.dma_start(ksw[i % 2][0:64, :], kf[i % 2][64:128, :]), 16)
                    SW.inc(sc.dma_start(ksw[i % 2][64:128, :], kf[i % 2][0:64, :]), 16)

            @b.vector
            def _(v):
                v.wait_ge(TE.h, bTE + 1)
                VE.inc(v.tensor_copy(bvb[:], pb[:]))
                v.wait_ge(VE.h, VE.n)
                for i in range(nck):
                    cdw(v, i)
                    cs = slice(i * TC, (i + 1) * TC)
                    v.wait_ge(SE.h, bSE + i + 1)
                    VE.inc(v.tensor_mul(kt1[i % 2][:], kf[i % 2][:], cosT[:, cs]))
                    v.wait_ge(SW.h, bSW + 32 * (i + 1))
                    VE.inc(v.tensor_mul(ksw[i % 2][:], ksw[i % 2][:], sinT[:, cs]))
                    v.wait_ge(VE.h, VE.n)
                    VE.inc(v.tensor_add(krb[i % 2][:], kt1[i % 2][:], ksw[i % 2][:]))
                    for j in range(NT):
                        v.wait_ge(TE.h, bTE + 1 + i * 5 + 1 + j + 1)
                        VE.inc(v.tensor_add(vsb[i % 2][:, j * HD:(j + 1) * HD],
                                            pv[j][:], bvb[:]))

            @b.sync
            def _(s):
                for i in range(nck):
                    s.wait_ge(VE.h, bVE + 1 + i * 7 + 3)
                    ST.inc(s.dma_start(dram["KR_bf"][:, i * TC:(i + 1) * TC],
                                       krb[i % 2][:]), 16)
                    s.wait_ge(VE.h, bVE + 1 + i * 7 + 7)
                    ST.inc(s.dma_start(
                        dram["Vn_bf"][i * TC:(i + 1) * TC, :].rearrange("(j p) c -> p j c", p=128),
                        vsb[i % 2][:].rearrange("p (j c) -> p j c", j=NT)), 16)
                    s.wait_ge(ST.h, bST + 32 * (i + 1))
                    CD.bump(s)



def phase_qctx(env, T, dram):
    nc = env.nc
    nck = T // TC
    NKT = T // 128
    VHN = NH
    with Phase(env, "p2b") as ph:
        wq = ph.sbuf("wq", [128, NSTR * D], BF16)
        bqc = ph.sbuf("bqc", [128, NSTR], F32)
        cosT = ph.sbuf("cosT", [128, T], BF16)
        sinT = ph.sbuf("sinT", [128, T], BF16)
        krT = ph.sbuf("krT", [128, T], BF16)
        vna = ph.sbuf("vna", [128, NKT * HD], BF16)
        ones_col = ph.sbuf("ones_col", [128, 1], F32)
        xb = [ph.sbuf(f"xb{i}", [128, NSTR * TC], BF16) for i in range(2)]
        qf = ph.sbuf("qf", [128, TC], F32)
        qsw = ph.sbuf("qsw", [128, TC], F32)
        qt1 = ph.sbuf("qt1", [128, TC], F32)
        qrb = ph.sbuf("qrb", [128, NH * TC], BF16)
        NEB = 4
        expb = [ph.sbuf(f"expb{i}", [128, TC], BF16) for i in range(NEB)]
        ctxo = ph.sbuf("ctxo", [128, NH * TC], BF16)
        denr = [ph.sbuf(f"denr{i}", [1, TC], F32) for i in range(NH)]
        dav = ph.sbuf("dav", [128, TC], F32)
        dag = ph.sbuf("dag", [128, TC], F32)
        pq = [ph.psum(f"pq{i}", [128, TC]) for i in range(2)]
        psc = [ph.psum(f"psc{i}", [128, TC]) for i in range(4)]
        pctx = ph.psum("pctx", [128, TC])
        pden = ph.psum("pden", [1, TC])

        LDp = [ph.sem("LD0"), ph.sem("LD1")]
        WD = ph.sem("WD")
        MS = ph.sem("MS")
        TE = ph.sem("TE")
        SE = ph.sem("SE")
        VE = ph.sem("VE")
        GP = ph.sem("GP")
        SW = ph.sem("SW")
        ST = ph.sem("ST")
        CD = ph.sem("CD")
        bTE, bSE, bVE, bGP, bSW, bST, bCD = (ph.base(x) for x in
                                             ("TE", "SE", "VE", "GP", "SW", "ST", "CD"))
        bLD = [ph.base("LD0"), ph.base("LD1")]

        TEC = NH + NH * (NKT + 2)
        SEC = NH + NH * (NKT + 2)
        VEC = NH * 3 + VHN * (NKT + 1)
        GPC = (NH - VHN) * NKT
        SWC = 32 * NH
        STC = 16 * (NH + 1)

        def cdw(eng, i):
            if i >= 1:
                eng.wait_ge(CD.h, bCD + i)

        def dacc_idx(i, h, j):
            return bVE + i * VEC + NH * 3 + h * (NKT + 1) + j + 1

        with nc.Block() as b:
            @b.gpsimd
            def _(g):
                WD.inc(g.dma_start(sb3(wq, NSTR), stripes_ap(dram["Wq"])), 16)
                WD.inc(g.dma_start(bqc[:], dram["bq2"][:]), 16)
                WD.inc(g.dma_start(cosT[:], dram["cosF"][:]), 16)
                WD.inc(g.dma_start(sinT[:], dram["sinF"][:]), 16)
                WD.inc(g.dma_start(krT[:], dram["KR_bf"][:]), 16)
                WD.inc(g.dma_start(vna[:].rearrange("p (j c) -> p j c", j=NKT),
                                   dram["Vn_bf"][:].rearrange("(j p) c -> p j c", p=128)), 16)
                MS.inc(g.memset(ones_col[:], 1.0))
                for i in range(nck):
                    cdw(g, i - 1)
                    LDp[i % 2].inc(g.dma_start(sb3(xb[i % 2], NSTR),
                                               xchunk(dram["xT"], i * TC, TC)), 16)


            @b.tensor
            def _(t):
                t.wait_ge(WD.h, WD.n)
                t.wait_ge(MS.h, MS.n)
                for i in range(nck):
                    cdw(t, i)
                    t.wait_ge(LDp[i % 2].h, bLD[i % 2] + 16 * (i // 2 + 1))
                    xv = sb3(xb[i % 2], NSTR)
                    tec = bTE + i * TEC
                    sec = bSE + i * SEC
                    vec = bVE + i * VEC
                    for o in range(NH):
                        if o >= 2:
                            t.wait_ge(SE.h, sec + o - 1)
                        for s in range(NSTR):
                            mm = t.matmul(pq[o % 2][:],
                                          wq[:, s * D + o * HD:s * D + (o + 1) * HD],
                                          xv[:, s, :], start=(s == 0), stop=(s == NSTR - 1))
                        TE.inc(mm)
                    for h in range(NH):
                        t.wait_ge(VE.h, vec + 3 * (h + 1))
                        seh = sec + NH + h * (NKT + 2)
                        teh = tec + NH + h * (NKT + 2)
                        for j in range(NKT):
                            if j >= 4:
                                t.wait_ge(SE.h, seh + j - 3)
                            TE.inc(t.matmul(psc[j % 4][:], krT[:, j * 128:(j + 1) * 128],
                                            qrb[:, h * TC:(h + 1) * TC],
                                            start=True, stop=True))
                            if j >= 1:
                                t.wait_ge(SE.h, seh + j)
                                t.matmul(pctx[:], vna[:, (j - 1) * HD:j * HD],
                                         expb[(j - 1) % NEB][:],
                                         start=(j == 1), stop=False)
                        t.wait_ge(SE.h, seh + NKT)
                        TE.inc(t.matmul(pctx[:], vna[:, (NKT - 1) * HD:NKT * HD],
                                        expb[(NKT - 1) % NEB][:], start=False, stop=True))
                        if h >= 1:
                            t.wait_ge(SE.h, seh)
                        t.wait_ge(VE.h, dacc_idx(i, h, NKT))
                        TE.inc(t.matmul(pden[:], ones_col[:, :], dav[:],
                                        start=True, stop=True))

            @b.scalar
            def _(sc):
                for i in range(nck):
                    cdw(sc, i)
                    tec = bTE + i * TEC
                    sec = bSE + i * SEC
                    vec = bVE + i * VEC
                    for o in range(NH):
                        sc.wait_ge(TE.h, tec + o + 1)
                        if o >= 1:
                            sc.wait_ge(VE.h, vec + 3 * o)
                            sc.wait_ge(SW.h, bSW + i * SWC + 32 * o)
                        SE.inc(sc.activation(qf[:], pq[o % 2][:], AF.Identity,
                                             bias=bqc[:, o:o + 1]))
                        sc.wait_ge(SE.h, SE.n)
                        SW.inc(sc.dma_start(qsw[0:64, :], qf[64:128, :]), 16)
                        SW.inc(sc.dma_start(qsw[64:128, :], qf[0:64, :]), 16)
                    for h in range(NH):
                        seh = sec + NH + h * (NKT + 2)
                        teh = tec + NH + h * (NKT + 2)
                        for j in range(NKT):
                            sc.wait_ge(TE.h, teh + j + 1)
                            if j >= NEB:
                                sc.wait_ge(VE.h, dacc_idx(i, h, j - NEB))
                            SE.inc(sc.activation(expb[j % NEB][:], psc[j % 4][:],
                                                 AF.Exp, scale=RS))
                        sc.wait_ge(TE.h, teh + NKT + 1)
                        SE.inc(sc.copy(ctxo[:, h * TC:(h + 1) * TC], pctx[:]))
                        sc.wait_ge(TE.h, teh + NKT + 2)
                        SE.inc(sc.copy(denr[h][:], pden[:]))

            @b.vector
            def _(v):
                for i in range(nck):
                    cdw(v, i)
                    cs = slice(i * TC, (i + 1) * TC)
                    sec = bSE + i * SEC
                    vec = bVE + i * VEC
                    swc = bSW + i * SWC
                    for o in range(NH):
                        if o >= 1:
                            v.wait_ge(VE.h, vec + 3 * o)
                        v.wait_ge(SE.h, sec + o + 1)
                        v.wait_ge(SW.h, swc + 32 * (o + 1))
                        VE.inc(v.tensor_mul(qt1[:], qf[:], cosT[:, cs]))
                        VE.inc(v.tensor_mul(qsw[:], qsw[:], sinT[:, cs]))
                        v.wait_ge(VE.h, VE.n)
                        VE.inc(v.tensor_add(qrb[:, o * TC:(o + 1) * TC], qt1[:], qsw[:]))
                    for h in range(VHN):
                        seh = sec + NH + h * (NKT + 2)
                        if h > 0:
                            v.wait_ge(TE.h, bTE + i * TEC + NH + h * (NKT + 2))
                        elif i >= 1:
                            v.wait_ge(TE.h, bTE + (i - 1) * TEC + NH + NH * (NKT + 2))
                        for j in range(NKT):
                            acc = dav if j % 2 == 0 else dag
                            v.wait_ge(SE.h, seh + j + 1)
                            if j < 2:
                                VE.inc(v.tensor_copy(acc[:], expb[j % NEB][:]))
                            else:
                                v.wait_ge(VE.h, VE.n - 1)
                                VE.inc(v.tensor_add(acc[:], acc[:], expb[j % NEB][:]))
                        v.wait_ge(VE.h, VE.n)
                        VE.inc(v.tensor_add(dav[:], dav[:], dag[:]))

            @b.sync
            def _(s):
                for i in range(nck):
                    sec = bSE + i * SEC
                    for h in range(NH):
                        s.wait_ge(SE.h, sec + NH + h * (NKT + 2) + NKT + 2)
                        ST.inc(s.dma_start(dram["den"][h:h + 1, i * TC:(i + 1) * TC],
                                           denr[h][:]), 16)
                    s.wait_ge(SE.h, sec + SEC)
                    ST.inc(s.dma_start(dram["ctxT_bf"][:, i * TC:(i + 1) * TC]
                                       .rearrange("(h p) c -> p h c", p=128),
                                       ctxo[:].rearrange("p (h c) -> p h c", h=NH)), 16)
                    s.wait_ge(ST.h, bST + STC * (i + 1))
                    CD.bump(s)


def phase_attnout(env, T, dram):
    nc = env.nc
    nck = T // TC
    with Phase(env, "p2c") as ph:
        wo = ph.sbuf("wo", [128, NSTR * D], BF16)
        boc = ph.sbuf("boc", [128, NSTR], F32)
        ones_row = ph.sbuf("ones_row", [1, 128], F32)
        ones_col = ph.sbuf("ones_col", [128, 1], F32)
        cb = ph.sbuf("cb0", [128, NH * TC], BF16)
        denb = [ph.sbuf(f"denb{h}", [1, TC], F32) for h in range(NH)]
        xf = ph.sbuf("xf0", [128, NSTR * TC], F32)
        rbf = [ph.sbuf(f"rbf{i}", [128, TC], F32) for i in range(2)]
        y = ph.sbuf("y", [128, NSTR * TC], F32)
        mrow = ph.sbuf("mrow", [1, TC], F32)
        qrow = ph.sbuf("qrow", [1, TC], F32)
        m2row = ph.sbuf("m2row", [1, TC], F32)
        varrow = ph.sbuf("varrow", [1, TC], F32)
        rrow = ph.sbuf("rrow", [1, TC], F32)
        lnrow = m2row
        srow = qrow
        rbv = ph.sbuf("rbv", [128, TC], F32)
        shb = ph.sbuf("shb", [128, TC], F32)
        tmp = ph.sbuf("tmp", [128, TC], F32)
        sq = tmp
        po = [ph.psum(f"po{i}", [128, TC]) for i in range(2)]
        prb = [ph.psum(f"prb{i}", [128, TC]) for i in range(2)]
        ps1 = ph.psum("ps1", [1, TC])
        ps2 = ph.psum("ps2", [1, TC])
        pr = ph.psum("pr", [128, TC])
        psh = ph.psum("psh", [128, TC])

        LD = ph.sem("LD0")
        WD = ph.sem("WD")
        MS = ph.sem("MS")
        TE = ph.sem("TE")
        SE = ph.sem("SE")
        VE = ph.sem("VE")
        ST = ph.sem("ST")
        CD = ph.sem("CD")
        bTE, bSE, bVE, bST, bCD = (ph.base(x) for x in ("TE", "SE", "VE", "ST", "CD"))
        bLD = ph.base("LD0")

        TEC = NH + NSTR + NSTR + 2
        SEC = 2 * NH + NSTR + NSTR + 2 + 2
        VEC = NH + NSTR + 5 + 1 + 2 * NSTR
        LDC = 16 * (2 + NH)

        def cdw(eng, i):
            if i >= 1:
                eng.wait_ge(CD.h, bCD + i)

        with nc.Block() as b:
            @b.gpsimd
            def _(g):
                WD.inc(g.dma_start(sb3(wo, NSTR), stripes_ap(dram["Wo"])), 16)
                WD.inc(g.dma_start(boc[:], dram["bo2"][:]), 16)
                g.memset(ones_row[:], 1.0)
                MS.inc(g.memset(ones_col[:], 1.0))
                for i in range(nck):
                    if i >= 1:
                        g.wait_ge(TE.h, bTE + (i - 1) * TEC + NH + NSTR)
                        g.wait_ge(VE.h, bVE + (i - 1) * VEC + NH + NSTR)
                    cs = slice(i * TC, (i + 1) * TC)
                    LD.inc(g.dma_start(
                        cb[:].rearrange("p (h c) -> p h c", h=NH),
                        dram["ctxT_bf"][:, cs].rearrange("(h p) c -> p h c", p=128)), 16)
                    LD.inc(g.dma_start(sb3(xf, NSTR), xchunk(dram["xT"], i * TC, TC)), 16)
                    for h in range(NH):
                        LD.inc(g.dma_start(denb[h][:], dram["den"][h:h + 1, cs]), 16)

            @b.tensor
            def _(t):
                t.wait_ge(WD.h, WD.n)
                t.wait_ge(MS.h, MS.n)
                for i in range(nck):
                    cdw(t, i)
                    tec = bTE + i * TEC
                    sec = bSE + i * SEC
                    vec = bVE + i * VEC
                    t.wait_ge(LD.h, bLD + LDC * (i + 1))
                    for h in range(NH):
                        if h >= 2:
                            t.wait_ge(SE.h, sec + 2 * h - 3)
                        TE.inc(t.matmul(prb[h % 2][:], ones_row[:, :], denb[h][:],
                                        start=True, stop=True))
                    t.wait_ge(VE.h, vec + NH)
                    for o in range(NSTR):
                        if o >= 2:
                            t.wait_ge(SE.h, sec + 2 * NH + o - 1)
                        for s in range(NSTR):
                            mm = t.matmul(po[o % 2][:],
                                          wo[:, s * D + o * HD:s * D + (o + 1) * HD],
                                          cb[:, s * TC:(s + 1) * TC],
                                          start=(s == 0), stop=(s == NSTR - 1))
                        TE.inc(mm)
                    for s in range(NSTR):
                        t.wait_ge(VE.h, vec + NH + s + 1)
                        t.matmul(ps1[:], ones_col[:, :], y[:, s * TC:(s + 1) * TC],
                                 start=(s == 0), stop=(s == NSTR - 1))
                        t.wait_ge(SE.h, sec + 2 * NH + NSTR + s + 1)
                        TE.inc(t.matmul(ps2[:], ones_col[:, :], sq[:],
                                        start=(s == 0), stop=(s == NSTR - 1)))
                    t.wait_ge(SE.h, sec + 2 * NH + NSTR + NSTR + 2)
                    TE.inc(t.matmul(pr[:], ones_row[:, :], rrow[:], start=True, stop=True))
                    t.wait_ge(VE.h, vec + NH + NSTR + 6)
                    TE.inc(t.matmul(psh[:], ones_row[:, :], srow[:], start=True, stop=True))

            @b.scalar
            def _(sc):
                for i in range(nck):
                    cdw(sc, i)
                    tec = bTE + i * TEC
                    sec = bSE + i * SEC
                    vec = bVE + i * VEC
                    for h in range(NH):
                        sc.wait_ge(TE.h, tec + h + 1)
                        if h >= 2:
                            sc.wait_ge(VE.h, vec + h - 1)
                        SE.inc(sc.activation(rbf[h % 2][:], prb[h % 2][:], AF.Ln))
                        sc.wait_ge(SE.h, SE.n)
                        SE.inc(sc.activation(rbf[h % 2][:], rbf[h % 2][:],
                                             AF.Exp, scale=-1.0))
                    for o in range(NSTR):
                        sc.wait_ge(TE.h, tec + NH + o + 1)
                        SE.inc(sc.activation(y[:, o * TC:(o + 1) * TC], po[o % 2][:],
                                             AF.Identity, bias=boc[:, o:o + 1]))
                    for s in range(NSTR):
                        sc.wait_ge(VE.h, vec + NH + s + 1)
                        if s >= 1:
                            sc.wait_ge(TE.h, tec + NH + NSTR + s)
                        SE.inc(sc.activation(sq[:], y[:, s * TC:(s + 1) * TC], AF.Square))
                    sc.wait_ge(VE.h, vec + NH + NSTR + 5)
                    SE.inc(sc.activation(lnrow[:], varrow[:], AF.Ln))
                    sc.wait_ge(SE.h, SE.n)
                    SE.inc(sc.activation(rrow[:], lnrow[:], AF.Exp, scale=-0.5))
                    sc.wait_ge(TE.h, tec + NH + NSTR + NSTR + 1)
                    SE.inc(sc.copy(rbv[:], pr[:]))
                    sc.wait_ge(TE.h, tec + NH + NSTR + NSTR + 2)
                    SE.inc(sc.copy(shb[:], psh[:]))

            @b.vector
            def _(v):
                for i in range(nck):
                    cdw(v, i)
                    tec = bTE + i * TEC
                    sec = bSE + i * SEC
                    vec = bVE + i * VEC
                    for h in range(NH):
                        v.wait_ge(SE.h, sec + 2 * h + 2)
                        VE.inc(v.tensor_mul(cb[:, h * TC:(h + 1) * TC],
                                            cb[:, h * TC:(h + 1) * TC],
                                            rbf[h % 2][:]))
                    for s in range(NSTR):
                        v.wait_ge(SE.h, sec + 2 * NH + s + 1)
                        VE.inc(v.tensor_add(y[:, s * TC:(s + 1) * TC],
                                            y[:, s * TC:(s + 1) * TC],
                                            sb3(xf, NSTR)[:, s, :]))
                    v.wait_ge(TE.h, tec + NH + NSTR + NSTR)
                    VE.inc(v.tensor_scalar(mrow[:], ps1[:], 1.0 / D, None, ALU.mult))
                    VE.inc(v.tensor_scalar(qrow[:], ps2[:], 1.0 / D, None, ALU.mult))
                    v.wait_ge(VE.h, VE.n)
                    VE.inc(v.tensor_mul(m2row[:], mrow[:], mrow[:]))
                    v.wait_ge(VE.h, VE.n)
                    VE.inc(v.tensor_sub(varrow[:], qrow[:], m2row[:]))
                    v.wait_ge(VE.h, VE.n)
                    VE.inc(v.tensor_scalar(varrow[:], varrow[:], LN_EPS, None, ALU.add))
                    v.wait_ge(SE.h, sec + 2 * NH + NSTR + NSTR + 2)
                    v.wait_ge(VE.h, VE.n)
                    VE.inc(v.tensor_mul(srow[:], mrow[:], rrow[:]))
                    v.wait_ge(SE.h, sec + SEC)
                    for s in range(NSTR):
                        VE.inc(v.tensor_mul(tmp[:], y[:, s * TC:(s + 1) * TC], rbv[:]))
                        v.wait_ge(VE.h, VE.n)
                        VE.inc(v.tensor_sub(y[:, s * TC:(s + 1) * TC], tmp[:], shb[:]))
                        v.wait_ge(VE.h, VE.n)

            @b.sync
            def _(s):
                for i in range(nck):
                    s.wait_ge(VE.h, bVE + (i + 1) * VEC)
                    ST.inc(s.dma_start(xchunk(dram["x1"], i * TC, TC),
                                       sb3(y, NSTR)), 16)
                    s.wait_ge(ST.h, bST + 16 * (i + 1))
                    CD.bump(s)



def phase_gamma(env, T, dram):
    nc = env.nc
    mats = [("w1T", D * H2), ("w2T", D * H2), ("w3T", H2 * D)]
    FB = 4096
    with Phase(env, "p3a") as ph:
        wst = [ph.sbuf(f"wst{i}", [128, FB], F32) for i in range(2)]
        acc = ph.sbuf("acc", [128, 1], F32)
        part = ph.sbuf("part", [128, 1], F32)
        grow = ph.sbuf("grow", [1, 1], F32)
        ones_col = ph.sbuf("ones_col", [128, 1], F32)
        ones_row = ph.sbuf("ones_row", [1, 128], F32)
        thr = ph.sbuf("thr", [128, 8], F32)
        pg = ph.psum("pg", [1, 1])
        pbc = ph.psum("pbc", [128, 1])
        LDp = [ph.sem("LD0"), ph.sem("LD1")]
        MS = ph.sem("MS")
        TE = ph.sem("TE")
        VE = ph.sem("VE")
        ST = ph.sem("ST")
        bTE, bVE, bST = (ph.base(x) for x in ("TE", "VE", "ST"))
        bLD = [ph.base("LD0"), ph.base("LD1")]

        nchunks = [n // (128 * FB) for _, n in mats]
        flat_m = [m_ for m_, (_, n) in enumerate(mats) for _i in range(nchunks[m_])]

        def ve_after_acc(g):
            return bVE + 2 * (g + 1) + 4 * flat_m[g]

        with nc.Block() as b:
            @b.gpsimd
            def _(g):
                g.memset(ones_col[:], 1.0)
                g.memset(thr[:], 0.0)
                MS.inc(g.memset(ones_row[:], 1.0))
                tot = 0
                for m_, (wname, n) in enumerate(mats):
                    for i in range(nchunks[m_]):
                        if tot >= 2:
                            g.wait_ge(VE.h, ve_after_acc(tot - 2))
                        LDp[tot % 2].inc(g.dma_start(
                            wst[tot % 2][:],
                            dram[wname][:].rearrange("r c -> (r c)")
                            .rearrange("(p k) -> p k", p=128)[:, i * FB:(i + 1) * FB]), 16)
                        tot += 1

            @b.vector
            def _(v):
                tot = 0
                for m_, (wname, n) in enumerate(mats):
                    for i in range(nchunks[m_]):
                        v.wait_ge(LDp[tot % 2].h, bLD[tot % 2] + 16 * (tot // 2 + 1))
                        VE.inc(v.reduce_sum(part[:], wst[tot % 2][:], axis=AX.X,
                                            apply_absolute_value=True))
                        v.wait_ge(VE.h, VE.n)
                        if i == 0:
                            VE.inc(v.tensor_copy(acc[:], part[:]))
                        else:
                            VE.inc(v.tensor_add(acc[:], acc[:], part[:]))
                        v.wait_ge(VE.h, VE.n)
                        tot += 1
                    v.wait_ge(TE.h, bTE + 2 * m_ + 1)
                    VE.inc(v.tensor_scalar(grow[:], pg[:], 0.5 / n, None, ALU.mult))
                    v.wait_ge(VE.h, VE.n)
                    VE.inc(v.tensor_scalar(grow[:], grow[:], 0.5e-8, None, ALU.add))
                    v.wait_ge(TE.h, bTE + 2 * m_ + 2)
                    VE.inc(v.tensor_copy(thr[:, 2 * m_:2 * m_ + 1], pbc[:]))
                    VE.inc(v.tensor_scalar(thr[:, 2 * m_ + 1:2 * m_ + 2], pbc[:], -1.0,
                                           None, ALU.mult))

            @b.tensor
            def _(t):
                t.wait_ge(MS.h, MS.n)
                for m_, (wname, n) in enumerate(mats):
                    upto = sum(nchunks[:m_ + 1])
                    t.wait_ge(VE.h, ve_after_acc(upto - 1))
                    TE.inc(t.matmul(pg[:], ones_col[:, :], acc[:], start=True, stop=True))
                    t.wait_ge(VE.h, ve_after_acc(upto - 1) + 2)
                    TE.inc(t.matmul(pbc[:], ones_row[:, :], grow[:], start=True, stop=True))

            @b.sync
            def _(s):
                s.wait_ge(VE.h, VE.n)
                ST.inc(s.dma_start(dram["thr"][:], thr[:]), 16)
                s.wait_ge(ST.h, ST.n)

def phase_ffn1(env, T, dram):
    nc = env.nc
    nck = T // TC
    NQ = 4
    JQ = NJ // NQ
    CQ = JQ * 128
    QVC = 3 * 2 * NSTR
    with Phase(env, "p3b") as ph:
        q1 = ph.sbuf("q1", [128, NSTR * CQ], BF16)
        q2 = ph.sbuf("q2", [128, NSTR * CQ], BF16)
        wstg = [ph.sbuf(f"wstg{i}", [128, CQ], F32) for i in range(2)]
        tge = ph.sbuf("tge", [128, CQ], F32)
        tle = ph.sbuf("tle", [128, CQ], F32)
        thrc = ph.sbuf("thrc", [128, 8], F32)
        xb = [ph.sbuf(f"xb{i}", [128, NSTR * TC], BF16) for i in range(2)]
        sgm = ph.sbuf("sgm", [128, TC], F32)
        hq2 = [ph.sbuf(f"hq{i}", [128, JQ * TC], BF16) for i in range(2)]
        psA = [ph.psum(f"psA{i}", [128, TC]) for i in range(4)]
        psB = [ph.psum(f"psB{i}", [128, TC]) for i in range(4)]
        WLp = [ph.sem("WL0"), ph.sem("WL1")]
        LDp = [ph.sem("LD0"), ph.sem("LD1")]
        TE = ph.sem("TE")
        SE = ph.sem("SE")
        VE = ph.sem("VE")
        ST = ph.sem("ST")
        CD = ph.sem("CD")
        WD = ph.sem("WD")
        bTE, bSE, bVE, bST, bCD = (ph.base(x) for x in
                                   ("TE", "SE", "VE", "ST", "CD"))
        bWL = [ph.base("WL0"), ph.base("WL1")]
        bLD = [ph.base("LD0"), ph.base("LD1")]

        NIT = NQ * nck

        def itq(it):
            return it // nck, it % nck

        def vef(it):
            q, i = itq(it)
            return bVE + QVC * (q + 1) + JQ * it

        def mul_done(gj):
            return vef(gj // JQ) + gj % JQ + 1

        with nc.Block() as b:
            @b.gpsimd
            def _(g):
                WD.inc(g.dma_start(thrc[:], dram["thr"][:]), 16)
                for it in range(NIT):
                    q, i = itq(it)
                    if i == 0:
                        if q >= 1:
                            g.wait_ge(CD.h, bCD + q * nck)
                        for m, wname in enumerate(("w1T", "w2T")):
                            for s in range(NSTR):
                                k = m * NSTR + s
                                if k >= 2:
                                    g.wait_ge(VE.h, bVE + QVC * q + JQ * (q * nck)
                                              + 3 * (k - 1))
                                fk = q * 2 * NSTR + k
                                WLp[fk % 2].inc(g.dma_start(
                                    wstg[k % 2][:],
                                    dram[wname][:, q * CQ:(q + 1) * CQ]
                                    .rearrange("(s p) c -> p s c", p=128)[:, s, :]), 16)
                    if it >= 2:
                        g.wait_ge(CD.h, bCD + it - 1)
                    LDp[it % 2].inc(g.dma_start(sb3(xb[it % 2], NSTR),
                                                xchunk(dram["x1"], i * TC, TC)), 16)

            @b.vector
            def _(v):
                v.wait_ge(WD.h, WD.n)
                for it in range(NIT):
                    q, i = itq(it)
                    if i == 0:
                        for k in range(2 * NSTR):
                            m, s = k // NSTR, k % NSTR
                            qd = (q1, q2)[m]
                            fk = q * 2 * NSTR + k
                            v.wait_ge(WLp[fk % 2].h, bWL[fk % 2] + 16 * (fk // 2 + 1))
                            v.wait_ge(VE.h, VE.n)
                            tc_ = 2 * m
                            VE.inc(v.tensor_scalar(tge[:], wstg[k % 2][:],
                                                   thrc[:, tc_:tc_ + 1], None, ALU.is_ge))
                            VE.inc(v.tensor_scalar(tle[:], wstg[k % 2][:],
                                                   thrc[:, tc_ + 1:tc_ + 2], None, ALU.is_le))
                            v.wait_ge(VE.h, VE.n)
                            VE.inc(v.tensor_sub(qd[:, s * CQ:(s + 1) * CQ], tge[:], tle[:]))
                    if it >= 2:
                        v.wait_ge(CD.h, bCD + it - 1)
                    for j in range(JQ):
                        gj = it * JQ + j
                        v.wait_ge(SE.h, bSE + it * JQ + j + 1)
                        v.wait_ge(TE.h, bTE + it * 2 * JQ + 2 * j + 2)
                        VE.inc(v.tensor_mul(hq2[it % 2][:, j * TC:(j + 1) * TC], sgm[:],
                                            psB[gj % 4][:]))

            @b.tensor
            def _(t):
                t.wait_ge(WD.h, WD.n)
                for it in range(NIT):
                    q, i = itq(it)
                    t.wait_ge(LDp[it % 2].h, bLD[it % 2] + 16 * (it // 2 + 1))
                    if i == 0:
                        t.wait_ge(VE.h, bVE + QVC * (q + 1) + JQ * it)
                    xv = sb3(xb[it % 2], NSTR)
                    for j in range(JQ):
                        gj = it * JQ + j
                        if gj >= 4:
                            t.wait_ge(SE.h, bSE + gj - 3)
                            t.wait_ge(VE.h, mul_done(gj - 4))
                        for s in range(NSTR):
                            mmA = t.matmul(psA[gj % 4][:],
                                           q1[:, s * CQ + j * 128:s * CQ + (j + 1) * 128],
                                           xv[:, s, :], start=(s == 0), stop=(s == NSTR - 1))
                        TE.inc(mmA)
                        for s in range(NSTR):
                            mmB = t.matmul(psB[gj % 4][:],
                                           q2[:, s * CQ + j * 128:s * CQ + (j + 1) * 128],
                                           xv[:, s, :], start=(s == 0), stop=(s == NSTR - 1))
                        TE.inc(mmB)

            @b.scalar
            def _(sc):
                for it in range(NIT):
                    q, i = itq(it)
                    for j in range(JQ):
                        gj = it * JQ + j
                        sc.wait_ge(TE.h, bTE + it * 2 * JQ + 2 * j + 1)
                        if gj >= 1:
                            sc.wait_ge(VE.h, mul_done(gj - 1))
                        SE.inc(sc.activation(sgm[:], psA[gj % 4][:], AF.Sigmoid))

            @b.sync
            def _(s):
                for it in range(NIT):
                    q, i = itq(it)
                    s.wait_ge(VE.h, vef(it) + JQ)
                    ST.inc(s.dma_start(
                        dram["hid_bf"][q * CQ:(q + 1) * CQ, i * TC:(i + 1) * TC]
                        .rearrange("(j p) c -> p j c", p=128),
                        hq2[it % 2][:].rearrange("p (j c) -> p j c", j=JQ)), 16)
                    s.wait_ge(ST.h, bST + 16 * (it + 1))
                    CD.bump(s)


def phase_ffn2(env, T, dram):
    nc = env.nc
    nck = T // TC
    NHF = 2
    CH = D // NHF
    OT = CH // 128
    QVC = 3 * NJ
    with Phase(env, "p3c") as ph:
        q3 = ph.sbuf("q3", [128, NJ * CH], BF16)
        wstg = [ph.sbuf(f"wstg{i}", [128, CH], F32) for i in range(2)]
        tge = ph.sbuf("tge", [128, CH], F32)
        tle = ph.sbuf("tle", [128, CH], F32)
        thrc = ph.sbuf("thrc", [128, 8], F32)
        hb = ph.sbuf("hb", [128, NJ * TC], BF16)
        xf = ph.sbuf("xf", [128, OT * TC], F32)
        y = ph.sbuf("y", [128, OT * TC], F32)
        po = [ph.psum(f"po{i}", [128, TC]) for i in range(4)]
        WLp = [ph.sem("WL0"), ph.sem("WL1")]
        LDp2 = [ph.sem("LD0"), ph.sem("LD1")]
        TE = ph.sem("TE")
        SE = ph.sem("SE")
        VE = ph.sem("VE")
        ST = ph.sem("ST")
        CD = ph.sem("CD")
        WD = ph.sem("WD")
        bTE, bSE, bVE, bST, bCD = (ph.base(x) for x in
                                   ("TE", "SE", "VE", "ST", "CD"))
        bWL = [ph.base("WL0"), ph.base("WL1")]
        bLD2 = [ph.base("LD0"), ph.base("LD1")]

        NIT = NHF * nck

        def itq(it):
            return it // nck, it % nck

        def vef(it):
            half, i = itq(it)
            return bVE + QVC * (half + 1) + OT * it

        with nc.Block() as b:
            @b.gpsimd
            def _(g):
                WD.inc(g.dma_start(thrc[:], dram["thr"][:]), 16)
                for it in range(NIT):
                    half, i = itq(it)
                    if i == 0:
                        if half >= 1:
                            g.wait_ge(CD.h, bCD + half * nck)
                        for s in range(NJ):
                            if s >= 2:
                                g.wait_ge(VE.h, bVE + QVC * half + OT * (half * nck)
                                          + 3 * (s - 1))
                            fk = half * NJ + s
                            WLp[fk % 2].inc(g.dma_start(
                                wstg[s % 2][:],
                                dram["w3T"][:, half * CH:(half + 1) * CH]
                                .rearrange("(s p) c -> p s c", p=128)[:, s, :]), 16)
                    if it >= 1:
                        g.wait_ge(TE.h, bTE + it * OT)
                        g.wait_ge(VE.h, vef(it - 1) + OT)
                    LDp2[0].inc(g.dma_start(hb[:].rearrange("p (s c) -> p s c", s=NJ),
                                            dram["hid_bf"][:, i * TC:(i + 1) * TC]
                                            .rearrange("(s p) c -> p s c", p=128)), 16)
                    LDp2[1].inc(g.dma_start(
                        xf[:].rearrange("p (s c) -> p s c", s=OT),
                        dram["x1"][half * CH:(half + 1) * CH, i * TC:(i + 1) * TC]
                        .rearrange("(s p) c -> p s c", p=128)), 16)

            @b.vector
            def _(v):
                v.wait_ge(WD.h, WD.n)
                for it in range(NIT):
                    half, i = itq(it)
                    if i == 0:
                        for s in range(NJ):
                            fk = half * NJ + s
                            v.wait_ge(WLp[fk % 2].h, bWL[fk % 2] + 16 * (fk // 2 + 1))
                            v.wait_ge(VE.h, VE.n)
                            VE.inc(v.tensor_scalar(tge[:], wstg[s % 2][:],
                                                   thrc[:, 4:5], None, ALU.is_ge))
                            VE.inc(v.tensor_scalar(tle[:], wstg[s % 2][:],
                                                   thrc[:, 5:6], None, ALU.is_le))
                            v.wait_ge(VE.h, VE.n)
                            VE.inc(v.tensor_sub(q3[:, s * CH:(s + 1) * CH], tge[:], tle[:]))
                    if it >= 1:
                        v.wait_ge(CD.h, bCD + it)
                    v.wait_ge(LDp2[1].h, bLD2[1] + 16 * (it + 1))
                    for o in range(OT):
                        v.wait_ge(SE.h, bSE + it * OT + o + 1)
                        VE.inc(v.tensor_add(y[:, o * TC:(o + 1) * TC],
                                            y[:, o * TC:(o + 1) * TC],
                                            xf[:].rearrange("p (s c) -> p s c", s=OT)[:, o, :]))

            @b.tensor
            def _(t):
                t.wait_ge(WD.h, WD.n)
                for it in range(NIT):
                    half, i = itq(it)
                    t.wait_ge(LDp2[0].h, bLD2[0] + 16 * (it + 1))
                    if i == 0:
                        t.wait_ge(VE.h, bVE + QVC * (half + 1) + OT * it)
                    hv = hb[:].rearrange("p (s c) -> p s c", s=NJ)
                    for o in range(OT):
                        go = it * OT + o
                        if go >= 4:
                            t.wait_ge(SE.h, bSE + go - 3)
                        for s in range(NJ):
                            mm = t.matmul(po[go % 4][:],
                                          q3[:, s * CH + o * 128:s * CH + (o + 1) * 128],
                                          hv[:, s, :], start=(s == 0), stop=(s == NJ - 1))
                        TE.inc(mm)

            @b.scalar
            def _(sc):
                for it in range(NIT):
                    half, i = itq(it)
                    if it >= 1:
                        sc.wait_ge(CD.h, bCD + it)
                    for o in range(OT):
                        go = it * OT + o
                        sc.wait_ge(TE.h, bTE + go + 1)
                        if o >= 1:
                            sc.wait_ge(VE.h, vef(it) + o)
                        SE.inc(sc.copy(y[:, o * TC:(o + 1) * TC], po[go % 4][:]))

            @b.sync
            def _(s):
                for it in range(NIT):
                    half, i = itq(it)
                    s.wait_ge(VE.h, vef(it) + OT)
                    ST.inc(s.dma_start(
                        dram["y2"][half * CH:(half + 1) * CH, i * TC:(i + 1) * TC]
                        .rearrange("(s p) c -> p s c", p=128),
                        y[:].rearrange("p (s c) -> p s c", s=OT)), 16)
                    s.wait_ge(ST.h, bST + 16 * (it + 1))
                    CD.bump(s)


def make_phase_ln(src_name, dst_name, tag):
    def phase_ln(env, T, dram):
        nc = env.nc
        nck = T // TC
        with Phase(env, tag) as ph:
            ones_row = ph.sbuf("ones_row", [1, 128], F32)
            ones_col = ph.sbuf("ones_col", [128, 1], F32)
            y = ph.sbuf("y", [128, NSTR * TC], F32)
            tmp = ph.sbuf("tmp", [128, TC], F32)
            sq = tmp
            mrow = ph.sbuf("mrow", [1, TC], F32)
            qrow = ph.sbuf("qrow", [1, TC], F32)
            m2row = ph.sbuf("m2row", [1, TC], F32)
            varrow = ph.sbuf("varrow", [1, TC], F32)
            rrow = ph.sbuf("rrow", [1, TC], F32)
            lnrow = m2row
            srow = qrow
            rbv = ph.sbuf("rbv", [128, TC], F32)
            shb = ph.sbuf("shb", [128, TC], F32)
            ps1 = ph.psum("ps1", [1, TC])
            ps2 = ph.psum("ps2", [1, TC])
            pr = ph.psum("pr", [128, TC])
            psh = ph.psum("psh", [128, TC])
            LD = ph.sem("LD0")
            MS = ph.sem("MS")
            TE = ph.sem("TE")
            SE = ph.sem("SE")
            VE = ph.sem("VE")
            ST = ph.sem("ST")
            CD = ph.sem("CD")
            bTE, bSE, bVE, bST, bCD = (ph.base(x) for x in
                                       ("TE", "SE", "VE", "ST", "CD"))
            bLD = ph.base("LD0")

            TEC = NSTR + 2
            SEC = NSTR + 2 + 2
            VEC = 5 + 1 + 2 * NSTR

            with nc.Block() as b:
                @b.gpsimd
                def _(g):
                    g.memset(ones_row[:], 1.0)
                    MS.inc(g.memset(ones_col[:], 1.0))
                    for i in range(nck):
                        if i >= 1:
                            g.wait_ge(CD.h, bCD + i)
                        LD.inc(g.dma_start(sb3(y, NSTR),
                                           xchunk(dram[src_name], i * TC, TC)), 16)

                @b.tensor
                def _(t):
                    t.wait_ge(MS.h, MS.n)
                    for i in range(nck):
                        if i >= 1:
                            t.wait_ge(CD.h, bCD + i)
                        tec = bTE + i * TEC
                        sec = bSE + i * SEC
                        vec = bVE + i * VEC
                        t.wait_ge(LD.h, bLD + 16 * (i + 1))
                        for s in range(NSTR):
                            t.matmul(ps1[:], ones_col[:, :], y[:, s * TC:(s + 1) * TC],
                                     start=(s == 0), stop=(s == NSTR - 1))
                            t.wait_ge(SE.h, sec + s + 1)
                            TE.inc(t.matmul(ps2[:], ones_col[:, :], sq[:],
                                            start=(s == 0), stop=(s == NSTR - 1)))
                        t.wait_ge(SE.h, sec + NSTR + 2)
                        TE.inc(t.matmul(pr[:], ones_row[:, :], rrow[:],
                                        start=True, stop=True))
                        t.wait_ge(VE.h, vec + 6)
                        TE.inc(t.matmul(psh[:], ones_row[:, :], srow[:],
                                        start=True, stop=True))

                @b.scalar
                def _(sc):
                    for i in range(nck):
                        if i >= 1:
                            sc.wait_ge(CD.h, bCD + i)
                        tec = bTE + i * TEC
                        sec = bSE + i * SEC
                        vec = bVE + i * VEC
                        sc.wait_ge(LD.h, bLD + 16 * (i + 1))
                        for s in range(NSTR):
                            if s >= 1:
                                sc.wait_ge(TE.h, tec + s)
                            SE.inc(sc.activation(sq[:], y[:, s * TC:(s + 1) * TC],
                                                 AF.Square))
                        sc.wait_ge(VE.h, vec + 5)
                        SE.inc(sc.activation(lnrow[:], varrow[:], AF.Ln))
                        sc.wait_ge(SE.h, SE.n)
                        SE.inc(sc.activation(rrow[:], lnrow[:], AF.Exp, scale=-0.5))
                        sc.wait_ge(TE.h, tec + NSTR + 1)
                        SE.inc(sc.copy(rbv[:], pr[:]))
                        sc.wait_ge(TE.h, tec + NSTR + 2)
                        SE.inc(sc.copy(shb[:], psh[:]))

                @b.vector
                def _(v):
                    for i in range(nck):
                        if i >= 1:
                            v.wait_ge(CD.h, bCD + i)
                        tec = bTE + i * TEC
                        sec = bSE + i * SEC
                        vec = bVE + i * VEC
                        v.wait_ge(TE.h, tec + NSTR)
                        VE.inc(v.tensor_scalar(mrow[:], ps1[:], 1.0 / D, None, ALU.mult))
                        VE.inc(v.tensor_scalar(qrow[:], ps2[:], 1.0 / D, None, ALU.mult))
                        v.wait_ge(VE.h, VE.n)
                        VE.inc(v.tensor_mul(m2row[:], mrow[:], mrow[:]))
                        v.wait_ge(VE.h, VE.n)
                        VE.inc(v.tensor_sub(varrow[:], qrow[:], m2row[:]))
                        v.wait_ge(VE.h, VE.n)
                        VE.inc(v.tensor_scalar(varrow[:], varrow[:], LN_EPS, None, ALU.add))
                        v.wait_ge(SE.h, sec + NSTR + 2)
                        v.wait_ge(VE.h, VE.n)
                        VE.inc(v.tensor_mul(srow[:], mrow[:], rrow[:]))
                        v.wait_ge(SE.h, sec + SEC)
                        for s in range(NSTR):
                            VE.inc(v.tensor_mul(tmp[:], y[:, s * TC:(s + 1) * TC], rbv[:]))
                            v.wait_ge(VE.h, VE.n)
                            VE.inc(v.tensor_sub(y[:, s * TC:(s + 1) * TC], tmp[:], shb[:]))
                            v.wait_ge(VE.h, VE.n)

                @b.sync
                def _(s):
                    for i in range(nck):
                        s.wait_ge(VE.h, bVE + (i + 1) * VEC)
                        ST.inc(s.dma_start(xchunk(dram[dst_name], i * TC, TC),
                                           sb3(y, NSTR)), 16)
                        s.wait_ge(ST.h, bST + 16 * (i + 1))
                        CD.bump(s)
    return phase_ln


def phase_ssmproj(env, T, dram):
    nc = env.nc
    nck = T // TC
    sweeps = [("Wdt", "bdt2", "dtc"), ("Wb", "bb2", "bco"), ("Wc", "bc2", "cco")]
    with Phase(env, "p4a") as ph:
        wb_ = ph.sbuf("wb", [128, NSTR * D], BF16)
        bia = ph.sbuf("bia", [128, 3 * NSTR], F32)
        araw = ph.sbuf("araw", [128, NE * NE], F32)
        asb = ph.sbuf("asb", [128, NE * NE], F32)
        at1 = ph.sbuf("at1", [128, NE * NE], F32)
        at2 = ph.sbuf("at2", [128, NE * NE], F32)
        xb = [ph.sbuf(f"xb{i}", [128, NSTR * TC], BF16) for i in range(2)]
        u1 = ph.sbuf("u1", [128, TC], F32)
        u3 = ph.sbuf("u3", [128, TC], F32)
        t2r = ph.sbuf("t2r", [128, TC], F32)
        ob_f = ph.sbuf("ob_f", [128, NSTR * TC], F32)
        ob_b = ph.sbuf("ob_b", [128, NSTR * TC], BF16)
        pq = [ph.psum(f"pq{i}", [128, TC]) for i in range(2)]
        WD = ph.sem("WD")
        WLp = [ph.sem("WL0"), ph.sem("WL1")]
        LDp = [ph.sem("LD0"), ph.sem("LD1")]
        TE = ph.sem("TE")
        SE = ph.sem("SE")
        VE = ph.sem("VE")
        ST = ph.sem("ST")
        CD = ph.sem("CD")
        bTE, bSE, bVE, bST, bCD = (ph.base(x) for x in ("TE", "SE", "VE", "ST", "CD"))
        bWL = [ph.base("WL0"), ph.base("WL1")]
        bLD = [ph.base("LD0"), ph.base("LD1")]

        NIT = 3 * nck

        def itq(it):
            return it // nck, it % nck

        def sec_of(it):
            w, i = itq(it)
            full = 3
            for jt in range(it):
                wj, _ = itq(jt)
                full += 4 * NSTR if wj == 0 else NSTR
            return bSE + full

        def sec_cnt(w):
            return 4 * NSTR if w == 0 else NSTR

        def vec_of(it):
            w, i = itq(it)
            full = 3
            for jt in range(it):
                wj, _ = itq(jt)
                full += NSTR if wj == 0 else 0
            return bVE + full

        with nc.Block() as b:
            @b.gpsimd
            def _(g):
                WD.inc(g.dma_start(araw[:], dram["a_rawT"][:]), 16)
                WD.inc(g.dma_start(bia[:], dram["biadbc"][:]), 16)
                for it in range(NIT):
                    w, i = itq(it)
                    if i == 0:
                        if w >= 1:
                            g.wait_ge(CD.h, bCD + w * nck)
                        WLp[w % 2].inc(g.dma_start(sb3(wb_, NSTR),
                                                   stripes_ap(dram[sweeps[w][0]])), 16)
                    if it >= 2:
                        g.wait_ge(CD.h, bCD + it - 1)
                    LDp[it % 2].inc(g.dma_start(sb3(xb[it % 2], NSTR),
                                                xchunk(dram["x2"], i * TC, TC)), 16)

            @b.scalar
            def _(sc):
                sc.wait_ge(WD.h, WD.n)
                SE.inc(sc.activation(at1[:], araw[:], AF.Abs))
                sc.wait_ge(SE.h, SE.n)
                SE.inc(sc.activation(at1[:], at1[:], AF.Exp, scale=-1.0))
                sc.wait_ge(SE.h, SE.n)
                SE.inc(sc.activation(at1[:], at1[:], AF.Ln, bias=1.0))
                for it in range(NIT):
                    w, i = itq(it)
                    if it >= 1:
                        sc.wait_ge(CD.h, bCD + it)
                    sec = sec_of(it)
                    for e in range(NSTR):
                        sc.wait_ge(TE.h, bTE + it * NSTR + e + 1)
                        bcol = bia[:, w * NSTR + e:w * NSTR + e + 1]
                        if w == 0:
                            if e >= 1:
                                sc.wait_ge(VE.h, vec_of(it) + e)
                            SE.inc(sc.activation(u1[:], pq[e % 2][:], AF.Abs, bias=bcol))
                            sc.wait_ge(SE.h, SE.n)
                            SE.inc(sc.activation(u1[:], u1[:], AF.Exp, scale=-1.0))
                            sc.wait_ge(SE.h, SE.n)
                            SE.inc(sc.activation(u3[:], u1[:], AF.Ln, bias=1.0))
                            SE.inc(sc.activation(t2r[:], pq[e % 2][:], AF.Relu, bias=bcol))
                        else:
                            SE.inc(sc.activation(ob_b[:, e * TC:(e + 1) * TC],
                                                 pq[e % 2][:], AF.Identity, bias=bcol))

            @b.vector
            def _(v):
                v.wait_ge(SE.h, bSE + 3)
                VE.inc(v.tensor_scalar(at2[:], araw[:], 0.0, None, ALU.max))
                v.wait_ge(VE.h, VE.n)
                VE.inc(v.tensor_add(asb[:], at2[:], at1[:]))
                v.wait_ge(VE.h, VE.n)
                VE.inc(v.tensor_scalar(asb[:], asb[:], -1.0, None, ALU.mult))
                for it in range(NIT):
                    w, i = itq(it)
                    if w != 0:
                        continue
                    if it >= 1:
                        v.wait_ge(CD.h, bCD + it)
                    sec = sec_of(it)
                    for e in range(NSTR):
                        v.wait_ge(SE.h, sec + 4 * (e + 1))
                        VE.inc(v.scalar_tensor_tensor(ob_f[:, e * TC:(e + 1) * TC],
                                                      t2r[:], 1e-4, u3[:],
                                                      ALU.add, ALU.add))

            @b.tensor
            def _(t):
                for it in range(NIT):
                    w, i = itq(it)
                    t.wait_ge(WLp[w % 2].h, bWL[w % 2] + 16 * (w // 2 + 1))
                    if it >= 1:
                        t.wait_ge(CD.h, bCD + it)
                    t.wait_ge(LDp[it % 2].h, bLD[it % 2] + 16 * (it // 2 + 1))
                    xv = sb3(xb[it % 2], NSTR)
                    sec = sec_of(it)
                    for e in range(NSTR):
                        if e >= 2:
                            t.wait_ge(SE.h, sec + (4 if w == 0 else 1) * (e - 1))
                        for s in range(NSTR):
                            mm = t.matmul(pq[e % 2][:],
                                          wb_[:, s * D + e * HD:s * D + (e + 1) * HD],
                                          xv[:, s, :], start=(s == 0), stop=(s == NSTR - 1))
                        TE.inc(mm)

            @b.sync
            def _(s):
                s.wait_ge(VE.h, bVE + 3)
                ST.inc(s.dma_start(dram["Asb"][:], asb[:]), 16)
                for it in range(NIT):
                    w, i = itq(it)
                    outname = sweeps[w][2]
                    if w == 0:
                        s.wait_ge(VE.h, vec_of(it) + NSTR)
                        ST.inc(s.dma_start(xchunk(dram[outname], i * TC, TC),
                                           sb3(ob_f, NSTR)), 16)
                    else:
                        s.wait_ge(SE.h, sec_of(it) + NSTR)
                        ST.inc(s.dma_start(xchunk(dram[outname], i * TC, TC),
                                           sb3(ob_b, NSTR)), 16)
                    s.wait_ge(ST.h, bST + 16 * (it + 2))
                    CD.bump(s)





def phase_ssmscan(env, T, dram):
    nc = env.nc
    nck = T // TC
    ASPL = 12
    with Phase(env, "p4b") as ph:
        asb = ph.sbuf("asb", [128, NE * NE], F32)
        bbase = ph.sbuf("bbase", [128, NE * NE], F32)
        cbase = ph.sbuf("cbase", [128, NE * NE], F32)
        cbb = ph.sbuf("cbb", [128, NE * NE], F32)
        dcol = ph.sbuf("dcol", [128, NE], F32)
        state = ph.sbuf("state", [128, NE * NE], F32)
        dtb = ph.sbuf("dtb", [128, NSTR * TC], F32)
        bcb = ph.sbuf("bcb", [128, NSTR * TC], BF16)
        ccb = ph.sbuf("ccb", [128, NSTR * TC], BF16)
        ub = ph.sbuf("ub", [128, NSTR * TC], F32)
        gb = ph.sbuf("gb", [128, NSTR * TC], F32)
        yb = ph.sbuf("yb", [128, NSTR * TC], F32)
        gt = ph.sbuf("gt", [128, TC], F32)
        al = [ph.sbuf(f"al{i}", [128, TC], F32) for i in range(4)]
        st_ = [ph.sbuf(f"st{i}", [128, TC], F32) for i in range(4)]
        ysv = ph.sbuf("ysv", [128, TC], F32)
        ysg = ph.sbuf("ysg", [128, TC], F32)
        ysg2 = ph.sbuf("ysg2", [128, TC], F32)
        ctg = ysg2
        ones_row = ph.sbuf("ones_row", [1, 128], F32)
        ones_col = ph.sbuf("ones_col", [128, 1], F32)
        tmp = ph.sbuf("tmp", [128, TC], F32)
        sq = tmp
        mrow = ph.sbuf("mrow", [1, TC], F32)
        qrow = ph.sbuf("qrow", [1, TC], F32)
        m2row = ph.sbuf("m2row", [1, TC], F32)
        varrow = ph.sbuf("varrow", [1, TC], F32)
        rrow = ph.sbuf("rrow", [1, TC], F32)
        lnrow = m2row
        srow = qrow
        rbv = ph.sbuf("rbv", [128, TC], F32)
        shb = ph.sbuf("shb", [128, TC], F32)
        ps1 = ph.psum("ps1", [1, TC])
        ps2 = ph.psum("ps2", [1, TC])
        pr = ph.psum("pr", [128, TC])
        psh = ph.psum("psh", [128, TC])
        WD = ph.sem("WD")
        MS = ph.sem("MS")
        LDU = ph.sem("LDU")
        LDC = ph.sem("LDC")
        PF = ph.sem("PF")
        TE = ph.sem("TE")
        SE = ph.sem("SE")
        VE = ph.sem("VE")
        GP = ph.sem("GP")
        ST = ph.sem("ST")
        CD = ph.sem("CD")
        bTE, bSE, bVE, bGP, bST, bCD = (ph.base(x) for x in
                                        ("TE", "SE", "VE", "GP", "ST", "CD"))
        bLDU = ph.base("LDU")
        bLDC = ph.base("LDC")
        bPF = ph.base("PF")

        KN = NSTR * NE
        VCS = 2 * NE + 2
        VEK = NSTR * VCS + 6 + 2 * ASPL
        GPC = 2 * NSTR + NSTR + NSTR + 2 * (NSTR - ASPL)
        SEK = 2 * KN + NSTR + 4
        TEK = NSTR + 2

        def ve_c(i):
            return bVE + 1 + i * VEK

        def gp_c(i):
            return bGP + i * GPC

        def se_c(i):
            return bSE + i * SEK

        def te_c(i):
            return bTE + i * TEK

        def ve_after_scan(i, k):
            s_, n_ = k // NE, k % NE
            return ve_c(i) + s_ * VCS + (1 if n_ == 0 else 2 * n_)

        def ve_after_stripe(i, s_):
            return ve_c(i) + (s_ + 1) * VCS

        def se_exp(i, k):
            return se_c(i) + (k + 1 if k < 3 else 2 * k - 2)

        def se_copy(i, j):
            return se_c(i) + (2 * j + 5 if j < KN - 3 else KN * 2 - 3 + (j - (KN - 4)))

        def se_rrow(i):
            return se_c(i) + 2 * KN + NSTR + 2

        def gp_after_gpass(i, s_):
            return gp_c(i) + NSTR + 2 * s_ + 1

        def gp_after_yinit(i, s_):
            return gp_c(i) + NSTR + 2 * s_ + 2

        def gp_after_resid(i, s_):
            return gp_c(i) + 3 * NSTR + s_ + 1

        with nc.Block() as b:
            @b.gpsimd
            def _(g):
                WD.inc(g.dma_start(asb[:], dram["Asb"][:]), 16)
                WD.inc(g.dma_start(bbase[:], dram["BbT"][:]), 16)
                WD.inc(g.dma_start(cbase[:], dram["CbT"][:]), 16)
                WD.inc(g.dma_start(dcol[:], dram["Dt2"][:]), 16)
                g.memset(state[:], 0.0)
                g.memset(ones_col[:], 1.0)
                MS.inc(g.memset(ones_row[:], 1.0))
                g.wait_ge(WD.h, WD.n)
                PF.inc(g.dma_start(sb3(dtb, NSTR), xchunk(dram["dtc"], 0, TC)), 16)
                PF.inc(g.dma_start(sb3(bcb, NSTR), xchunk(dram["bco"], 0, TC)), 16)
                for i in range(nck):
                    if i >= 1:
                        g.wait_ge(CD.h, bCD + i)
                    LDU.inc(g.dma_start(sb3(ub, NSTR), xchunk(dram["x2"], i * TC, TC)), 16)
                    LDC.inc(g.dma_start(sb3(ccb, NSTR), xchunk(dram["cco"], i * TC, TC)), 16)
                    g.wait_ge(PF.h, bPF + 32 * (i + 1))
                    for s_ in range(NSTR):
                        sl = slice(s_ * TC, (s_ + 1) * TC)
                        GP.inc(g.tensor_tensor(gb[:, sl], dtb[:, sl], bcb[:, sl],
                                               ALU.mult))
                    g.wait_ge(LDU.h, bLDU + 16 * (i + 1))
                    g.wait_ge(GP.h, GP.n)
                    for s_ in range(NSTR):
                        sl = slice(s_ * TC, (s_ + 1) * TC)
                        GP.inc(g.tensor_tensor(gb[:, sl], gb[:, sl], ub[:, sl],
                                               ALU.mult))
                        GP.inc(g.tensor_tensor(
                            yb[:, sl], ub[:, sl],
                            dcol[:, s_:s_ + 1].broadcast_to([128, TC]), ALU.mult))
                    if i + 1 < nck:
                        g.wait_ge(SE.h, se_c(i) + 2 * KN - 4)
                        PF.inc(g.dma_start(sb3(dtb, NSTR),
                                           xchunk(dram["dtc"], (i + 1) * TC, TC)), 16)
                        PF.inc(g.dma_start(sb3(bcb, NSTR),
                                           xchunk(dram["bco"], (i + 1) * TC, TC)), 16)
                    for s_ in range(NSTR):
                        sl = slice(s_ * TC, (s_ + 1) * TC)
                        g.wait_ge(VE.h, ve_after_stripe(i, s_))
                        GP.inc(g.tensor_add(yb[:, sl], yb[:, sl], ub[:, sl]))
                    for s_ in range(ASPL, NSTR):
                        sl = slice(s_ * TC, (s_ + 1) * TC)
                        g.wait_ge(SE.h, se_c(i) + SEK)
                        g.wait_ge(GP.h, GP.n)
                        GP.inc(g.tensor_mul(ctg[:], yb[:, sl], rbv[:]))
                        g.wait_ge(GP.h, GP.n)
                        GP.inc(g.tensor_sub(yb[:, sl], ctg[:], shb[:]))

            @b.scalar
            def _(sc):
                sc.wait_ge(WD.h, WD.n)
                for i in range(nck):
                    if i >= 1:
                        sc.wait_ge(CD.h, bCD + i)
                    sc.wait_ge(PF.h, bPF + 32 * (i + 1))
                    sec = se_c(i)
                    for k in range(KN):
                        s_ = k // NE
                        if k >= 4:
                            sc.wait_ge(VE.h, ve_after_scan(i, k - 4))
                        SE.inc(sc.activation(al[k % 4][:], dtb[:, s_ * TC:(s_ + 1) * TC],
                                             AF.Exp, scale=asb[:, k:k + 1]))
                        if k >= 3:
                            j = k - 3
                            sc.wait_ge(VE.h, ve_after_scan(i, j))
                            SE.inc(sc.copy(state[:, j:j + 1], st_[j % 4][:, TC - 1:TC]))
                    for j in range(KN - 3, KN):
                        sc.wait_ge(VE.h, ve_after_scan(i, j))
                        SE.inc(sc.copy(state[:, j:j + 1], st_[j % 4][:, TC - 1:TC]))
                    for s_ in range(NSTR):
                        sc.wait_ge(GP.h, gp_after_resid(i, s_))
                        if s_ >= 1:
                            sc.wait_ge(TE.h, te_c(i) + s_)
                        SE.inc(sc.activation(sq[:], yb[:, s_ * TC:(s_ + 1) * TC],
                                             AF.Square))
                    sc.wait_ge(VE.h, ve_c(i) + VEK - 2 * ASPL - 1)
                    SE.inc(sc.activation(lnrow[:], varrow[:], AF.Ln))
                    sc.wait_ge(SE.h, SE.n)
                    SE.inc(sc.activation(rrow[:], lnrow[:], AF.Exp, scale=-0.5))
                    sc.wait_ge(TE.h, te_c(i) + NSTR + 1)
                    SE.inc(sc.copy(rbv[:], pr[:]))
                    sc.wait_ge(TE.h, te_c(i) + NSTR + 2)
                    SE.inc(sc.copy(shb[:], psh[:]))

            @b.vector
            def _(v):
                v.wait_ge(WD.h, WD.n)
                VE.inc(v.tensor_mul(cbb[:], cbase[:], bbase[:]))
                v.wait_ge(VE.h, VE.n)
                for i in range(nck):
                    if i >= 1:
                        v.wait_ge(CD.h, bCD + i)
                    v.wait_ge(MS.h, MS.n)
                    sec = se_c(i)
                    for s_ in range(NSTR):
                        sl = slice(s_ * TC, (s_ + 1) * TC)
                        v.wait_ge(GP.h, gp_after_gpass(i, s_))

                        def ysv_op(n_):
                            k = s_ * NE + n_
                            if n_ == 0:
                                VE.inc(v.tensor_scalar(ysv[:], st_[k % 4][:],
                                                       cbb[:, k:k + 1], None,
                                                       ALU.mult))
                            else:
                                VE.inc(v.scalar_tensor_tensor(ysv[:], st_[k % 4][:],
                                                              cbb[:, k:k + 1],
                                                              ysv[:],
                                                              ALU.mult, ALU.add))
                        for n_ in range(NE):
                            k = s_ * NE + n_
                            v.wait_ge(SE.h, se_exp(i, k))
                            VE.inc(v.tensor_tensor_scan(st_[k % 4][:], al[k % 4][:],
                                                        gb[:, sl], state[:, k:k + 1],
                                                        ALU.mult, ALU.add))
                            if n_ >= 1:
                                v.wait_ge(VE.h, VE.n - 1)
                                ysv_op(n_ - 1)
                        v.wait_ge(VE.h, VE.n)
                        ysv_op(NE - 1)
                        if s_ == 0:
                            v.wait_ge(LDC.h, bLDC + 16 * (i + 1))
                        v.wait_ge(VE.h, VE.n)
                        VE.inc(v.tensor_mul(gt[:], ccb[:, sl], ysv[:]))
                        v.wait_ge(GP.h, gp_after_yinit(i, s_))
                        v.wait_ge(VE.h, VE.n)
                        VE.inc(v.tensor_add(yb[:, sl], yb[:, sl], gt[:]))
                    v.wait_ge(TE.h, te_c(i) + NSTR)
                    VE.inc(v.tensor_scalar(mrow[:], ps1[:], 1.0 / D, None, ALU.mult))
                    VE.inc(v.tensor_scalar(qrow[:], ps2[:], 1.0 / D, None, ALU.mult))
                    v.wait_ge(VE.h, VE.n)
                    VE.inc(v.tensor_mul(m2row[:], mrow[:], mrow[:]))
                    v.wait_ge(VE.h, VE.n)
                    VE.inc(v.tensor_sub(varrow[:], qrow[:], m2row[:]))
                    v.wait_ge(VE.h, VE.n)
                    VE.inc(v.tensor_scalar(varrow[:], varrow[:], LN_EPS, None, ALU.add))
                    v.wait_ge(SE.h, se_rrow(i))
                    v.wait_ge(VE.h, VE.n)
                    VE.inc(v.tensor_mul(srow[:], mrow[:], rrow[:]))
                    v.wait_ge(SE.h, sec + SEK)
                    atmp = [tmp, gt, ysv]
                    for s_ in range(ASPL):
                        VE.inc(v.tensor_mul(atmp[s_ % 3][:],
                                            yb[:, s_ * TC:(s_ + 1) * TC], rbv[:]))
                        if s_ >= 1:
                            sp = s_ - 1
                            v.wait_ge(VE.h, VE.n - 1)
                            VE.inc(v.tensor_sub(yb[:, sp * TC:(sp + 1) * TC],
                                                atmp[sp % 3][:], shb[:]))
                    v.wait_ge(VE.h, VE.n)
                    sp = ASPL - 1
                    VE.inc(v.tensor_sub(yb[:, sp * TC:(sp + 1) * TC],
                                        atmp[sp % 3][:], shb[:]))

            @b.tensor
            def _(t):
                t.wait_ge(MS.h, MS.n)
                for i in range(nck):
                    if i >= 1:
                        t.wait_ge(CD.h, bCD + i)
                    sec = se_c(i)
                    for s_ in range(NSTR):
                        t.wait_ge(GP.h, gp_after_resid(i, s_))
                        t.matmul(ps1[:], ones_col[:, :], yb[:, s_ * TC:(s_ + 1) * TC],
                                 start=(s_ == 0), stop=(s_ == NSTR - 1))
                        t.wait_ge(SE.h, sec + 2 * KN + s_ + 1)
                        TE.inc(t.matmul(ps2[:], ones_col[:, :], sq[:],
                                        start=(s_ == 0), stop=(s_ == NSTR - 1)))
                    t.wait_ge(SE.h, se_rrow(i))
                    TE.inc(t.matmul(pr[:], ones_row[:, :], rrow[:], start=True, stop=True))
                    t.wait_ge(VE.h, ve_c(i) + VEK - 2 * ASPL)
                    TE.inc(t.matmul(psh[:], ones_row[:, :], srow[:], start=True, stop=True))

            @b.sync
            def _(s):
                for i in range(nck):
                    s.wait_ge(VE.h, ve_c(i) + VEK)
                    s.wait_ge(GP.h, gp_c(i) + GPC)
                    s.wait_ge(SE.h, se_c(i) + SEK)
                    ST.inc(s.dma_start(xchunk(dram["x3"], i * TC, TC), sb3(yb, NSTR)), 16)
                    s.wait_ge(ST.h, bST + 16 * (i + 1))
                    CD.bump(s)


def phase_moeroute(env, T, dram):
    nc = env.nc
    nck = T // TC
    with Phase(env, "p5a") as ph:
        simb = ph.sbuf("simb", [128, NSTR * NE], F32)
        simf = [ph.sbuf(f"simf{i}", [128, NE], F32) for i in range(2)]
        sq16 = ph.sbuf("sq16", [128, NE], F32)
        rsnr = ph.sbuf("rsnr", [1, NE], F32)
        rsnc = ph.sbuf("rsnc", [NE, 1], F32)
        sgr = ph.sbuf("sgr", [1, NE], F32)
        sgc = ph.sbuf("sgc", [NE, 1], F32)
        id16 = ph.sbuf("id16", [16, 16], F32)
        id128 = ph.sbuf("id128", [128, 128], F32)
        negt = ph.sbuf("negt", [128, NE], F32)
        xb = [ph.sbuf(f"xb{i}", [128, NSTR * TC], F32) for i in range(2)]
        sqs = ph.sbuf("sqs", [128, TC], F32)
        lnx = ph.sbuf("lnx", [1, TC], F32)
        rnx = ph.sbuf("rnx", [1, TC], F32)
        rawn = ph.sbuf("rawn", [NE, TC], F32)
        lgT = ph.sbuf("lgT", [NE, TC], F32)
        lgN = ph.sbuf("lgN", [128, NE], F32)
        m1 = ph.sbuf("m1", [128, 1], F32)
        inact = ph.sbuf("inact", [128, 1], F32)
        top1 = ph.sbuf("top1", [128, NE], F32)
        l2t = ph.sbuf("l2t", [128, NE], F32)
        m2r = ph.sbuf("m2r", [128, 1], F32)
        top2 = ph.sbuf("top2", [128, NE], F32)
        fbm = ph.sbuf("fbm", [128, NE], F32)
        gated = ph.sbuf("gated", [128, NE], F32)
        maskb = ph.sbuf("maskb", [128, NE], F32)
        fmask = ph.sbuf("fmask", [128, NE], F32)
        maskd = ph.sbuf("maskd", [128, NE], F32)
        mneg = ph.sbuf("mneg", [128, 1], F32)
        ee = ph.sbuf("ee", [128, NE], F32)
        denr = ph.sbuf("denr", [128, 1], F32)
        rcpr = ph.sbuf("rcpr", [128, 1], F32)
        wvt = ph.sbuf("wvt", [128, NE], F32)
        wrow = ph.sbuf("wrow", [NE, TC], F32)
        ones_col = ph.sbuf("ones_col", [128, 1], F32)
        ones_r16 = ph.sbuf("ones_r16", [1, NE], F32)
        praw = ph.psum("praw", [NE, TC])
        ps2x = ph.psum("ps2x", [1, TC])
        pb16 = ph.psum("pb16", [NE, TC])
        ptr128 = ph.psum("ptr128", [128, NE])
        ptw = ph.psum("ptw", [NE, 128])
        pcn = ph.psum("pcn", [1, NE])
        WD = ph.sem("WD")
        WLp = [ph.sem("WL0"), ph.sem("WL1")]
        MS = ph.sem("MS")
        LDp = [ph.sem("LD0"), ph.sem("LD1")]
        TE = ph.sem("TE")
        SE = ph.sem("SE")
        VE = ph.sem("VE")
        SW = ph.sem("SW")
        ST = ph.sem("ST")
        CD = ph.sem("CD")
        bTE, bSE, bVE, bSW, bST, bCD = (ph.base(x) for x in
                                        ("TE", "SE", "VE", "SW", "ST", "CD"))
        bWL = [ph.base("WL0"), ph.base("WL1")]
        bLD = [ph.base("LD0"), ph.base("LD1")]

        TEC = 1 + NSTR + 1 + 8
        SEC = NSTR + 3 + 4 * 3
        VSUB = 19
        VEC = 2 + 4 * VSUB

        def cdw(eng, i):
            if i >= 1:
                eng.wait_ge(CD.h, bCD + i)

        with nc.Block() as b:
            @b.gpsimd
            def _(g):
                WD.inc(g.dma_start(simb[:].rearrange("p (s e) -> p s e", s=NSTR),
                                   stripes_ap(dram["sim"])), 16)
                WD.inc(g.dma_start(sgr[:], dram["gates"][:]), 16)
                g.memset(id16[:], 1.0)
                g.memset(id128[:], 1.0)
                g.memset(negt[:], -1.0e9)
                g.memset(ones_col[:], 1.0)
                MS.inc(g.memset(ones_r16[:], 1.0))
                g.wait_ge(MS.h, MS.n)
                g.affine_select(id16[:], id16[:], pattern=[[1, 16]],
                                compare_op=ALU.is_equal, fill=0.0, base=0,
                                channel_multiplier=-1)
                MS.inc(g.affine_select(id128[:], id128[:], pattern=[[1, 128]],
                                       compare_op=ALU.is_equal, fill=0.0, base=0,
                                       channel_multiplier=-1))
                for s in range(NSTR):
                    if s >= 2:
                        g.wait_ge(SE.h, bSE + s - 1)
                    WLp[s % 2].inc(g.dma_start(
                        simf[s % 2][:], stripes_ap(dram["sim"])[:, s, :]), 16)
                for i in range(nck):
                    cdw(g, i - 1)
                    LDp[i % 2].inc(g.dma_start(sb3(xb[i % 2], NSTR),
                                               xchunk(dram["x3"], i * TC, TC)), 16)

            @b.scalar
            def _(sc):
                for s in range(NSTR):
                    sc.wait_ge(WLp[s % 2].h, bWL[s % 2] + 16 * (s // 2 + 1))
                    if s >= 1:
                        sc.wait_ge(TE.h, bTE + s)
                    SE.inc(sc.activation(sq16[:], simf[s % 2][:], AF.Square))
                sc.wait_ge(TE.h, bTE + NSTR)
                SE.inc(sc.activation(rsnr[:], pcn[:], AF.Ln))
                sc.wait_ge(SE.h, SE.n)
                SE.inc(sc.activation(rsnr[:], rsnr[:], AF.Exp, scale=-0.5))
                sc.wait_ge(WD.h, WD.n)
                SE.inc(sc.activation(sgr[:], sgr[:], AF.Sigmoid))
                sc.wait_ge(SE.h, SE.n)
                SW.inc(sc.dma_start(dram["rsn_t"][:], rsnr[:]), 16)
                SW.inc(sc.dma_start(dram["sg_t"][:], sgr[:]), 16)
                sc.wait_ge(SW.h, bSW + 32)
                SW.inc(sc.dma_start(rsnc[:], dram["rsn_t"][:].rearrange("a e -> e a")), 16)
                SW.inc(sc.dma_start(sgc[:], dram["sg_t"][:].rearrange("a e -> e a")), 16)
                pse = bSE + NSTR + 3
                pte = bTE + NSTR
                for i in range(nck):
                    cdw(sc, i)
                    sec = pse + i * SEC
                    tec = pte + i * TEC
                    sc.wait_ge(LDp[i % 2].h, bLD[i % 2] + 16 * (i // 2 + 1))
                    for s in range(NSTR):
                        if s >= 1:
                            sc.wait_ge(TE.h, tec + 1 + s)
                        SE.inc(sc.activation(sqs[:], sb3(xb[i % 2], NSTR)[:, s, :],
                                             AF.Square))
                    sc.wait_ge(TE.h, tec + 1 + NSTR)
                    SE.inc(sc.activation(lnx[:], ps2x[:], AF.Ln))
                    sc.wait_ge(SE.h, SE.n)
                    SE.inc(sc.activation(rnx[:], lnx[:], AF.Exp, scale=-0.5))
                    sc.wait_ge(SW.h, bSW + 64)
                    SE.inc(sc.activation(rawn[:], praw[:], AF.Identity,
                                         scale=rsnc[:, 0:1]))
                    for st in range(4):
                        sc.wait_ge(TE.h, tec + 1 + NSTR + 1 + st * 2 + 1)
                        SE.inc(sc.copy(lgN[:], ptr128[:]))
                        sc.wait_ge(VE.h, bVE + i * VEC + 2 + st * VSUB + 15)
                        SE.inc(sc.activation(ee[:], maskd[:], AF.Exp,
                                             bias=mneg[:, 0:1]))
                        sc.wait_ge(TE.h, tec + 1 + NSTR + 1 + st * 2 + 2)
                        SE.inc(sc.copy(wrow[:, st * 128:(st + 1) * 128], ptw[:]))

            @b.tensor
            def _(t):
                t.wait_ge(MS.h, MS.n)
                t.wait_ge(WD.h, WD.n)
                for s in range(NSTR):
                    t.wait_ge(SE.h, bSE + s + 1)
                    TE.inc(t.matmul(pcn[:], ones_col[:, :], sq16[:],
                                    start=(s == 0), stop=(s == NSTR - 1)))
                pse = bSE + NSTR + 3
                pte = bTE + NSTR
                for i in range(nck):
                    cdw(t, i)
                    sec = pse + i * SEC
                    tec = pte + i * TEC
                    vec = bVE + i * VEC
                    t.wait_ge(LDp[i % 2].h, bLD[i % 2] + 16 * (i // 2 + 1))
                    xv = sb3(xb[i % 2], NSTR)
                    for s in range(NSTR):
                        mm = t.matmul(praw[:], simb[:, s * NE:(s + 1) * NE], xv[:, s, :],
                                      start=(s == 0), stop=(s == NSTR - 1))
                    TE.inc(mm)
                    for s in range(NSTR):
                        t.wait_ge(SE.h, sec + s + 1)
                        TE.inc(t.matmul(ps2x[:], ones_col[:, :], sqs[:],
                                        start=(s == 0), stop=(s == NSTR - 1)))
                    t.wait_ge(SE.h, sec + NSTR + 2)
                    TE.inc(t.matmul(pb16[:], ones_r16[:, :], rnx[:], start=True, stop=True))
                    for st in range(4):
                        t.wait_ge(VE.h, vec + 2)
                        if st >= 1:
                            t.wait_ge(SE.h, sec + NSTR + 3 + (st - 1) * 3 + 1)
                        TE.inc(t.transpose(ptr128[:], lgT[:, st * 128:(st + 1) * 128],
                                           id16[:]))
                        t.wait_ge(VE.h, vec + 2 + st * VSUB + VSUB)
                        if st >= 1:
                            t.wait_ge(SE.h, sec + NSTR + 3 + (st - 1) * 3 + 3)
                        TE.inc(t.transpose(ptw[:], wvt[:], id128[:]))

            @b.vector
            def _(v):
                v.wait_ge(WD.h, WD.n)
                pse = bSE + NSTR + 3
                pte = bTE + NSTR
                for i in range(nck):
                    cdw(v, i)
                    sec = pse + i * SEC
                    tec = pte + i * TEC
                    vec = bVE + i * VEC
                    v.wait_ge(SE.h, sec + NSTR + 3)
                    v.wait_ge(TE.h, tec + 1 + NSTR + 1)
                    v.wait_ge(SW.h, bSW + 64)
                    VE.inc(v.tensor_mul(lgT[:], rawn[:], pb16[:]))
                    v.wait_ge(VE.h, VE.n)
                    VE.inc(v.tensor_scalar(lgT[:], lgT[:], sgc[:, 0:1], None, ALU.subtract))
                    for st in range(4):
                        v.wait_ge(SE.h, sec + NSTR + 3 + st * 3 + 1)
                        VE.inc(v.reduce_max(m1[:], lgN[:], axis=AX.X))
                        v.wait_ge(VE.h, VE.n)
                        VE.inc(v.tensor_scalar(inact[:], m1[:], 0.0, None, ALU.is_le))
                        VE.inc(v.tensor_scalar(top1[:], lgN[:], m1[:, 0:1], None, ALU.is_ge))
                        v.wait_ge(VE.h, VE.n)
                        VE.inc(v.scalar_tensor_tensor(l2t[:], top1[:], -3.0e9, lgN[:],
                                                      ALU.mult, ALU.add))
                        v.wait_ge(VE.h, VE.n)
                        VE.inc(v.reduce_max(m2r[:], l2t[:], axis=AX.X))
                        v.wait_ge(VE.h, VE.n)
                        VE.inc(v.tensor_scalar(top2[:], l2t[:], m2r[:, 0:1], None, ALU.is_ge))
                        v.wait_ge(VE.h, VE.n)
                        VE.inc(v.tensor_add(fbm[:], top1[:], top2[:]))
                        VE.inc(v.tensor_scalar(gated[:], lgN[:], 0.0, None, ALU.max))
                        VE.inc(v.tensor_scalar(maskb[:], lgN[:], 0.0, None, ALU.is_gt))
                        v.wait_ge(VE.h, VE.n)
                        VE.inc(v.tensor_sub(fbm[:], fbm[:], maskb[:]))
                        v.wait_ge(VE.h, VE.n)
                        VE.inc(v.scalar_tensor_tensor(fmask[:], fbm[:], inact[:, 0:1],
                                                      maskb[:], ALU.mult, ALU.add))
                        v.wait_ge(VE.h, VE.n)
                        VE.inc(v.tensor_mul(l2t[:], gated[:], fmask[:]))
                        VE.inc(v.tensor_scalar(top2[:], fmask[:], 1.0, None, ALU.subtract))
                        v.wait_ge(VE.h, VE.n)
                        VE.inc(v.scalar_tensor_tensor(maskd[:], top2[:], 1.0e9, l2t[:],
                                                      ALU.mult, ALU.add))
                        v.wait_ge(VE.h, VE.n)
                        VE.inc(v.reduce_max(mneg[:], maskd[:], axis=AX.X, negate=True))
                        v.wait_ge(SE.h, sec + NSTR + 3 + st * 3 + 2)
                        VE.inc(v.reduce_sum(denr[:], ee[:], axis=AX.X))
                        v.wait_ge(VE.h, VE.n)
                        VE.inc(v.reciprocal(rcpr[:], denr[:]))
                        v.wait_ge(VE.h, VE.n)
                        VE.inc(v.tensor_scalar(wvt[:], ee[:], rcpr[:, 0:1], None, ALU.mult))
                        v.wait_ge(VE.h, VE.n)
                        VE.inc(v.tensor_mul(wvt[:], wvt[:], fmask[:]))

            @b.sync
            def _(s):
                pse = bSE + NSTR + 3
                for i in range(nck):
                    sec = pse + i * SEC
                    s.wait_ge(SE.h, sec + SEC)
                    ST.inc(s.dma_start(dram["wTr"][:, i * TC:(i + 1) * TC], wrow[:]), 16)
                    s.wait_ge(ST.h, bST + 16 * (i + 1))
                    CD.bump(s)


def phase_moemix(env, T, dram):
    nc = env.nc
    NHH = 2 if T >= 2 * TC else 1
    TH = T // NHH
    NTS = TH // TC
    NVT = TH // 128
    with Phase(env, "p5b") as ph:
        xb = ph.sbuf("xb", [128, NSTR * TH], BF16)
        prj = [ph.sbuf(f"prj{i}", [128, 3 * NSTR * HD], BF16) for i in range(2)]
        wrk = [ph.sbuf(f"wrk{i}", [1, TC], F32) for i in range(2)]
        wbb = [ph.sbuf(f"wbb{i}", [128, TC], F32) for i in range(3)]
        tmpb = ph.sbuf("tmpb", [128, TC], F32)
        qacc = ph.sbuf("qacc", [128, TH], F32)
        kacc = ph.sbuf("kacc", [128, TH], F32)
        vacc = ph.sbuf("vacc", [128, TH], F32)
        vtile = [ph.sbuf(f"vtile{j}", [128, HD], BF16) for j in range(NVT)]
        id128 = ph.sbuf("id128", [128, 128], F32)
        ones_row = ph.sbuf("ones_row", [1, 128], F32)
        pw = [ph.psum(f"pw{i}", [128, TC]) for i in range(3)]
        psP = [ph.psum(f"psP{i}", [128, TC]) for i in range(3)]
        ptv = [ph.psum(f"ptv{i}", [128, HD]) for i in range(2)]
        WD = ph.sem("WD")
        MS = ph.sem("MS")
        WLp = [ph.sem("WL0"), ph.sem("WL1")]
        RWp = [ph.sem("RW0"), ph.sem("RW1")]
        TE = ph.sem("TE")
        SE = ph.sem("SE")
        VE = ph.sem("VE")
        ST = ph.sem("ST")
        CD = ph.sem("CD")
        bTE, bSE, bVE, bST, bCD = (ph.base(x) for x in ("TE", "SE", "VE", "ST", "CD"))
        bWL = [ph.base("WL0"), ph.base("WL1")]
        bRW = [ph.base("RW0"), ph.base("RW1")]

        accs = (qacc, kacc, vacc)
        NFI = NHH * NE * NTS

        def fi_of(hh, e, ts):
            return (hh * NE + e) * NTS + ts

        def vee(fi):
            e = (fi // NTS) % NE
            return 3 if e == 0 else 6

        def cum_ve(fi):
            return bVE + sum(vee(j) for j in range(fi + 1))

        def te(fi):
            return bTE + fi * 4 + (fi // (NE * NTS)) * NVT

        def te_vt(hh, j):
            return bTE + (hh + 1) * NE * NTS * 4 + hh * NVT + j + 1

        def se_wbb(fi):
            return bSE + fi + 1 + (fi // (NE * NTS)) * NVT

        def se_vt(hh, j):
            return bSE + (hh + 1) * NE * NTS + hh * NVT + j + 1

        with nc.Block() as b:
            @b.gpsimd
            def _(g):
                g.memset(id128[:], 1.0)
                MS.inc(g.memset(ones_row[:], 1.0))
                g.wait_ge(MS.h, MS.n)
                MS.inc(g.affine_select(id128[:], id128[:], pattern=[[1, 128]],
                                       compare_op=ALU.is_equal, fill=0.0, base=0,
                                       channel_multiplier=-1))
                for hh in range(NHH):
                    if hh >= 1:
                        g.wait_ge(CD.h, bCD + hh)
                    WD.inc(g.dma_start(
                        xb[:].rearrange("p (s c) -> p s c", s=NSTR),
                        xchunk(dram["x3"], hh * TH, TH)), 16)
                    for e in range(NE):
                        he = hh * NE + e
                        if he >= 2:
                            fi_last = (he - 1) * NTS - 1
                            g.wait_ge(TE.h, te(fi_last) + 4)
                        for pi_, pn in enumerate(("QP", "KP", "VP")):
                            WLp[he % 2].inc(g.dma_start(
                                prj[e % 2][:].rearrange("p (k s c) -> p k s c",
                                                        k=3, s=NSTR)[:, pi_],
                                dram[pn][e * D:(e + 1) * D, :]
                                .rearrange("(s p) c -> p s c", p=128)), 16)
                        for ts in range(NTS):
                            fi = fi_of(hh, e, ts)
                            if fi >= 2:
                                g.wait_ge(TE.h, te(fi - 2) + 1)
                            RWp[fi % 2].inc(g.dma_start(
                                wrk[fi % 2][:],
                                dram["wTr"][e:e + 1,
                                            hh * TH + ts * TC:hh * TH + (ts + 1) * TC]), 16)
                    g.wait_ge(VE.h, cum_ve(fi_of(hh, NE - 1, NTS - 1)))
                    ST.inc(g.dma_start(dram["qT_bf"][:, hh * TH:(hh + 1) * TH], qacc[:]), 16)
                    ST.inc(g.dma_start(dram["kT_bf"][:, hh * TH:(hh + 1) * TH], kacc[:]), 16)
                    for j in range(NVT):
                        g.wait_ge(SE.h, se_vt(hh, j))
                        ST.inc(g.dma_start(
                            dram["Vn2_bf"][hh * TH + j * 128:hh * TH + (j + 1) * 128, :],
                            vtile[j][:]), 16)
                    g.wait_ge(ST.h, bST + (hh + 1) * 16 * (2 + NVT))
                    CD.bump(g)

            @b.tensor
            def _(t):
                t.wait_ge(MS.h, MS.n)
                for hh in range(NHH):
                    t.wait_ge(WD.h, 16 * (hh + 1))
                    for e in range(NE):
                        he = hh * NE + e
                        t.wait_ge(WLp[he % 2].h, bWL[he % 2] + 48 * (he // 2 + 1))
                        for ts in range(NTS):
                            fi = fi_of(hh, e, ts)
                            t.wait_ge(RWp[fi % 2].h, bRW[fi % 2] + 16 * (fi // 2 + 1))
                            if fi >= 3:
                                t.wait_ge(SE.h, se_wbb(fi - 3))
                            TE.inc(t.matmul(pw[fi % 3][:], ones_row[:, :],
                                            wrk[fi % 2][:], start=True, stop=True))
                            if fi >= 1:
                                t.wait_ge(VE.h, cum_ve(fi - 1))
                            prv = prj[e % 2][:].rearrange("p (k s c) -> p k s c",
                                                          k=3, s=NSTR)
                            xv = xb[:].rearrange("p (s c) -> p s c", s=NSTR)
                            for pi_ in range(3):
                                for s in range(NSTR):
                                    mm = t.matmul(psP[pi_][:], prv[:, pi_, s, :],
                                                  xv[:, s, ts * TC:(ts + 1) * TC],
                                                  start=(s == 0), stop=(s == NSTR - 1))
                                TE.inc(mm)
                    for j in range(NVT):
                        t.wait_ge(VE.h, cum_ve(fi_of(hh, NE - 1, NTS - 1)))
                        if j >= 2:
                            t.wait_ge(SE.h, se_vt(hh, j - 2))
                        TE.inc(t.transpose(ptv[j % 2][:],
                                           vacc[:, j * 128:(j + 1) * 128], id128[:]))

            @b.scalar
            def _(sc):
                for hh in range(NHH):
                    for e in range(NE):
                        for ts in range(NTS):
                            fi = fi_of(hh, e, ts)
                            sc.wait_ge(TE.h, te(fi) + 1)
                            if fi >= 3:
                                sc.wait_ge(VE.h, cum_ve(fi - 3))
                            SE.inc(sc.copy(wbb[fi % 3][:], pw[fi % 3][:]))
                    for j in range(NVT):
                        sc.wait_ge(TE.h, te_vt(hh, j))
                        SE.inc(sc.copy(vtile[j][:], ptv[j % 2][:]))

            @b.vector
            def _(v):
                for hh in range(NHH):
                    for e in range(NE):
                        for ts in range(NTS):
                            fi = fi_of(hh, e, ts)
                            sl = slice(ts * TC, (ts + 1) * TC)
                            v.wait_ge(SE.h, se_wbb(fi))
                            for pi_ in range(3):
                                v.wait_ge(TE.h, te(fi) + 1 + pi_ + 1)
                                if e == 0:
                                    VE.inc(v.tensor_mul(accs[pi_][:, sl], psP[pi_][:],
                                                        wbb[fi % 3][:]))
                                else:
                                    v.wait_ge(VE.h, VE.n)
                                    VE.inc(v.tensor_mul(tmpb[:], psP[pi_][:],
                                                        wbb[fi % 3][:]))
                                    v.wait_ge(VE.h, VE.n)
                                    VE.inc(v.tensor_add(accs[pi_][:, sl],
                                                        accs[pi_][:, sl], tmpb[:]))


def phase_moeattn(env, T, dram):
    nc = env.nc
    nck = T // TC
    with Phase(env, "p5c") as ph:
        qT = ph.sbuf("qT", [128, T], BF16)
        kT = ph.sbuf("kT", [128, T], BF16)
        vna = ph.sbuf("vna", [128, (T // 128) * HD], BF16)
        ones_bf = ph.sbuf("ones_bf", [128, 1], BF16)
        ones_row = ph.sbuf("ones_row", [1, 128], F32)
        zmask = ph.sbuf("zmask", [128, TC], F32)
        masks = [ph.sbuf(f"mask{r}", [128, TC], F32) for r in range(4)]
        expb = [ph.sbuf(f"expb{i}", [128, TC], BF16) for i in range(2)]
        mt = [ph.sbuf(f"mt{i}", [128, TC], F32) for i in range(2)]
        denr = ph.sbuf("denr", [1, TC], F32)
        rbf = ph.sbuf("rbf", [128, TC], F32)
        aob = ph.sbuf("aob", [128, TC], BF16)
        psc = [ph.psum(f"psc{i}", [128, TC]) for i in range(2)]
        pctx = ph.psum("pctx", [128, TC])
        pden = ph.psum("pden", [1, TC])
        pbb = ph.psum("pbb", [128, TC])
        WD = ph.sem("WD")
        MS = ph.sem("MS")
        TE = ph.sem("TE")
        SE = ph.sem("SE")
        VE = ph.sem("VE")
        ST = ph.sem("ST")
        CD = ph.sem("CD")
        bTE, bSE, bVE, bST, bCD = (ph.base(x) for x in ("TE", "SE", "VE", "ST", "CD"))

        def njt(i):
            return 4 * (i + 1)

        def te_c(i):
            return bTE + sum(njt(k) + 2 for k in range(i))

        def se_c(i):
            return bSE + sum(njt(k) + 1 for k in range(i))

        def ve_c(i):
            return bVE + 6 * i

        def cdw(eng, i):
            if i >= 1:
                eng.wait_ge(CD.h, bCD + i)

        with nc.Block() as b:
            @b.gpsimd
            def _(g):
                WD.inc(g.dma_start(qT[:], dram["qT_bf"][:]), 16)
                WD.inc(g.dma_start(kT[:], dram["kT_bf"][:]), 16)
                WD.inc(g.dma_start(vna[:].rearrange("p (j c) -> p j c", j=T // 128),
                                   dram["Vn2_bf"][:].rearrange("(j p) c -> p j c", p=128)), 16)
                g.memset(ones_bf[:], 1.0)
                g.memset(ones_row[:], 1.0)
                MS.inc(g.memset(zmask[:], 0.0))
                g.wait_ge(MS.h, MS.n)
                for r in range(4):
                    MS.inc(g.affine_select(masks[r][:], zmask[:], pattern=[[1, TC]],
                                           compare_op=ALU.is_ge, fill=-1.0e30,
                                           base=-(r * 128), channel_multiplier=-1))

            @b.tensor
            def _(t):
                t.wait_ge(WD.h, WD.n)
                t.wait_ge(MS.h, MS.n)
                for i in range(nck):
                    cdw(t, i)
                    tec = te_c(i)
                    sec = se_c(i)
                    vec = ve_c(i)
                    n = njt(i)
                    for j in range(n):
                        diag = j >= 4 * i
                        if j >= 2:
                            jd = j - 2
                            if jd >= 4 * i:
                                t.wait_ge(VE.h, vec + (jd - 4 * i) + 1)
                            else:
                                t.wait_ge(SE.h, sec + jd + 1)
                        TE.inc(t.matmul(psc[j % 2][:], kT[:, j * 128:(j + 1) * 128],
                                        qT[:, i * TC:(i + 1) * TC],
                                        start=True, stop=True))
                        if j >= 1:
                            t.wait_ge(SE.h, sec + j)
                            t.matmul(pctx[:], vna[:, (j - 1) * HD:j * HD],
                                     expb[(j - 1) % 2][:], start=(j == 1), stop=False)
                            t.matmul(pden[:], ones_bf[:, :], expb[(j - 1) % 2][:],
                                     start=(j == 1), stop=False)
                    t.wait_ge(SE.h, sec + n)
                    t.matmul(pctx[:], vna[:, (n - 1) * HD:n * HD], expb[(n - 1) % 2][:],
                             start=False, stop=True)
                    TE.inc(t.matmul(pden[:], ones_bf[:, :], expb[(n - 1) % 2][:],
                                    start=False, stop=True))
                    t.wait_ge(SE.h, sec + n + 1)
                    TE.inc(t.matmul(pbb[:], ones_row[:, :], denr[:], start=True, stop=True))

            @b.scalar
            def _(sc):
                for i in range(nck):
                    cdw(sc, i)
                    tec = te_c(i)
                    sec = se_c(i)
                    vec = ve_c(i)
                    n = njt(i)
                    for j in range(n):
                        diag = j >= 4 * i
                        if diag:
                            sc.wait_ge(VE.h, vec + (j - 4 * i) + 1)
                            SE.inc(sc.activation(expb[j % 2][:], mt[j % 2][:],
                                                 AF.Exp, scale=RS))
                        else:
                            sc.wait_ge(TE.h, tec + j + 1)
                            SE.inc(sc.activation(expb[j % 2][:], psc[j % 2][:],
                                                 AF.Exp, scale=RS))
                    sc.wait_ge(TE.h, tec + n + 1)
                    SE.inc(sc.copy(denr[:], pden[:]))

            @b.vector
            def _(v):
                for i in range(nck):
                    cdw(v, i)
                    tec = te_c(i)
                    sec = se_c(i)
                    vec = ve_c(i)
                    n = njt(i)
                    for j in range(4 * i, n):
                        r = j - 4 * i
                        v.wait_ge(TE.h, tec + j + 1)
                        if r >= 2:
                            v.wait_ge(SE.h, sec + j - 1)
                        VE.inc(v.tensor_add(mt[j % 2][:], psc[j % 2][:], masks[r][:]))
                    v.wait_ge(TE.h, tec + n + 2)
                    VE.inc(v.reciprocal(rbf[:], pbb[:]))
                    v.wait_ge(VE.h, VE.n)
                    VE.inc(v.tensor_mul(aob[:], pctx[:], rbf[:]))

            @b.sync
            def _(s):
                for i in range(nck):
                    s.wait_ge(VE.h, ve_c(i + 1))
                    ST.inc(s.dma_start(dram["aoT_bf"][:, i * TC:(i + 1) * TC], aob[:]), 16)
                    s.wait_ge(ST.h, bST + 16 * (i + 1))
                    CD.bump(s)


def phase_moeout(env, T, dram):
    nc = env.nc
    nck = T // TC
    with Phase(env, "p5d") as ph:
        opb = ph.sbuf("opb", [128, NE * D], BF16)
        aoT = ph.sbuf("aoT", [128, T], BF16)
        wrk = [ph.sbuf(f"wrk{i}", [1, TC], F32) for i in range(2)]
        wbb = [ph.sbuf(f"wbb{i}", [128, TC], F32) for i in range(2)]
        waob = ph.sbuf("waob", [128, NE * TC], BF16)
        xf = ph.sbuf("xf", [128, NSTR * TC], F32)
        y = ph.sbuf("y", [128, NSTR * TC], F32)
        ones_row = ph.sbuf("ones_row", [1, 128], F32)
        ones_col = ph.sbuf("ones_col", [128, 1], F32)
        tmp = ph.sbuf("tmp", [128, TC], F32)
        sq = tmp
        mrow = ph.sbuf("mrow", [1, TC], F32)
        qrow = ph.sbuf("qrow", [1, TC], F32)
        m2row = ph.sbuf("m2row", [1, TC], F32)
        varrow = ph.sbuf("varrow", [1, TC], F32)
        rrow = ph.sbuf("rrow", [1, TC], F32)
        lnrow = m2row
        srow = qrow
        rbv = ph.sbuf("rbv", [128, TC], F32)
        shb = ph.sbuf("shb", [128, TC], F32)
        pw = [ph.psum(f"pw{i}", [128, TC]) for i in range(2)]
        po = [ph.psum(f"po{i}", [128, TC]) for i in range(2)]
        ps1 = ph.psum("ps1", [1, TC])
        ps2 = ph.psum("ps2", [1, TC])
        pr = ph.psum("pr", [128, TC])
        psh = ph.psum("psh", [128, TC])
        WD = ph.sem("WD")
        MS = ph.sem("MS")
        RWp = [ph.sem("RW0"), ph.sem("RW1")]
        LD = ph.sem("LD0")
        TE = ph.sem("TE")
        SE = ph.sem("SE")
        VE = ph.sem("VE")
        ST = ph.sem("ST")
        CD = ph.sem("CD")
        bTE, bSE, bVE, bST, bCD = (ph.base(x) for x in ("TE", "SE", "VE", "ST", "CD"))
        bRW = [ph.base("RW0"), ph.base("RW1")]
        bLD = ph.base("LD0")

        TEC = NE + NSTR + NSTR + 2
        SEC = NE + NSTR + NSTR + 2 + 2
        VEC = NE + NSTR + 6 + 2 * NSTR

        def cdw(eng, i):
            if i >= 1:
                eng.wait_ge(CD.h, bCD + i)

        with nc.Block() as b:
            @b.gpsimd
            def _(g):
                WD.inc(g.dma_start(opb[:].rearrange("p (e c) -> p e c", e=NE),
                                   stripes_ap(dram["OP"])), 16)
                WD.inc(g.dma_start(aoT[:], dram["aoT_bf"][:]), 16)
                g.memset(ones_col[:], 1.0)
                MS.inc(g.memset(ones_row[:], 1.0))
                for i in range(nck):
                    cdw(g, i)
                    LD.inc(g.dma_start(sb3(xf, NSTR), xchunk(dram["x3"], i * TC, TC)), 16)
                    for e in range(NE):
                        fi = i * NE + e
                        if e >= 2:
                            g.wait_ge(TE.h, bTE + i * TEC + e - 1)
                        RWp[fi % 2].inc(g.dma_start(
                            wrk[fi % 2][:],
                            dram["wTr"][e:e + 1, i * TC:(i + 1) * TC]), 16)

            @b.tensor
            def _(t):
                t.wait_ge(WD.h, WD.n)
                t.wait_ge(MS.h, MS.n)
                for i in range(nck):
                    cdw(t, i)
                    tec = bTE + i * TEC
                    sec = bSE + i * SEC
                    vec = bVE + i * VEC
                    for e in range(NE):
                        fi = i * NE + e
                        t.wait_ge(RWp[fi % 2].h, bRW[fi % 2] + 16 * (fi // 2 + 1))
                        if e >= 2:
                            t.wait_ge(SE.h, sec + e - 1)
                        TE.inc(t.matmul(pw[fi % 2][:], ones_row[:, :], wrk[fi % 2][:],
                                        start=True, stop=True))
                    t.wait_ge(VE.h, vec + NE)
                    for o in range(NSTR):
                        if o >= 2:
                            t.wait_ge(SE.h, sec + NE + o - 1)
                        for e in range(NE):
                            mm = t.matmul(po[o % 2][:],
                                          opb[:, e * D + o * HD:e * D + (o + 1) * HD],
                                          waob[:, e * TC:(e + 1) * TC],
                                          start=(e == 0), stop=(e == NE - 1))
                        TE.inc(mm)
                    for s in range(NSTR):
                        t.wait_ge(VE.h, vec + NE + s + 1)
                        t.matmul(ps1[:], ones_col[:, :], y[:, s * TC:(s + 1) * TC],
                                 start=(s == 0), stop=(s == NSTR - 1))
                        t.wait_ge(SE.h, sec + NE + NSTR + s + 1)
                        TE.inc(t.matmul(ps2[:], ones_col[:, :], sq[:],
                                        start=(s == 0), stop=(s == NSTR - 1)))
                    t.wait_ge(SE.h, sec + NE + NSTR + NSTR + 2)
                    TE.inc(t.matmul(pr[:], ones_row[:, :], rrow[:], start=True, stop=True))
                    t.wait_ge(VE.h, vec + NE + NSTR + 6)
                    TE.inc(t.matmul(psh[:], ones_row[:, :], srow[:], start=True, stop=True))

            @b.scalar
            def _(sc):
                for i in range(nck):
                    cdw(sc, i)
                    tec = bTE + i * TEC
                    sec = bSE + i * SEC
                    vec = bVE + i * VEC
                    for e in range(NE):
                        fi = i * NE + e
                        sc.wait_ge(TE.h, tec + e + 1)
                        if e >= 2:
                            sc.wait_ge(VE.h, vec + e - 1)
                        SE.inc(sc.copy(wbb[fi % 2][:], pw[fi % 2][:]))
                    for o in range(NSTR):
                        sc.wait_ge(TE.h, tec + NE + o + 1)
                        SE.inc(sc.copy(y[:, o * TC:(o + 1) * TC], po[o % 2][:]))
                    for s in range(NSTR):
                        sc.wait_ge(VE.h, vec + NE + s + 1)
                        if s >= 1:
                            sc.wait_ge(TE.h, tec + NE + NSTR + s)
                        SE.inc(sc.activation(sq[:], y[:, s * TC:(s + 1) * TC], AF.Square))
                    sc.wait_ge(VE.h, vec + NE + NSTR + 5)
                    SE.inc(sc.activation(lnrow[:], varrow[:], AF.Ln))
                    sc.wait_ge(SE.h, SE.n)
                    SE.inc(sc.activation(rrow[:], lnrow[:], AF.Exp, scale=-0.5))
                    sc.wait_ge(TE.h, tec + NE + NSTR + NSTR + 1)
                    SE.inc(sc.copy(rbv[:], pr[:]))
                    sc.wait_ge(TE.h, tec + NE + NSTR + NSTR + 2)
                    SE.inc(sc.copy(shb[:], psh[:]))

            @b.vector
            def _(v):
                for i in range(nck):
                    cdw(v, i)
                    tec = bTE + i * TEC
                    sec = bSE + i * SEC
                    vec = bVE + i * VEC
                    for e in range(NE):
                        v.wait_ge(SE.h, sec + e + 1)
                        VE.inc(v.tensor_mul(waob[:, e * TC:(e + 1) * TC],
                                            aoT[:, i * TC:(i + 1) * TC],
                                            wbb[(i * NE + e) % 2][:]))
                    v.wait_ge(LD.h, bLD + 16 * (i + 1))
                    for s in range(NSTR):
                        v.wait_ge(SE.h, sec + NE + s + 1)
                        VE.inc(v.tensor_add(y[:, s * TC:(s + 1) * TC],
                                            y[:, s * TC:(s + 1) * TC],
                                            sb3(xf, NSTR)[:, s, :]))
                    v.wait_ge(TE.h, tec + NE + NSTR + NSTR)
                    VE.inc(v.tensor_scalar(mrow[:], ps1[:], 1.0 / D, None, ALU.mult))
                    VE.inc(v.tensor_scalar(qrow[:], ps2[:], 1.0 / D, None, ALU.mult))
                    v.wait_ge(VE.h, VE.n)
                    VE.inc(v.tensor_mul(m2row[:], mrow[:], mrow[:]))
                    v.wait_ge(VE.h, VE.n)
                    VE.inc(v.tensor_sub(varrow[:], qrow[:], m2row[:]))
                    v.wait_ge(VE.h, VE.n)
                    VE.inc(v.tensor_scalar(varrow[:], varrow[:], LN_EPS, None, ALU.add))
                    v.wait_ge(SE.h, sec + NE + NSTR + NSTR + 2)
                    v.wait_ge(VE.h, VE.n)
                    VE.inc(v.tensor_mul(srow[:], mrow[:], rrow[:]))
                    v.wait_ge(SE.h, sec + SEC)
                    for s in range(NSTR):
                        sl = slice(s * TC, (s + 1) * TC)
                        VE.inc(v.tensor_mul(tmp[:], y[:, sl], rbv[:]))
                        v.wait_ge(VE.h, VE.n)
                        VE.inc(v.tensor_sub(y[:, sl], tmp[:], shb[:]))
                        v.wait_ge(VE.h, VE.n)

            @b.sync
            def _(s):
                for i in range(nck):
                    s.wait_ge(VE.h, bVE + (i + 1) * VEC)
                    ST.inc(s.dma_start(xchunk(dram["outT"], i * TC, TC), sb3(y, NSTR)), 16)
                    s.wait_ge(ST.h, bST + 16 * (i + 1))
                    CD.bump(s)


def _shapes(T):
    return {
        "xT": ((D, T), F32),
        "Wq": ((D, D), F32), "bq2": ((128, NSTR), F32),
        "Wk": ((D, HD), F32), "bk": ((128, 1), F32),
        "Wv": ((D, HD), F32), "bv": ((1, HD), F32),
        "Wo": ((D, D), F32), "bo2": ((128, NSTR), F32),
        "w1T": ((D, H2), F32), "w2T": ((D, H2), F32), "w3T": ((H2, D), F32),
        "cosF": ((128, T), F32), "sinF": ((128, T), F32),
        "KR_bf": ((HD, T), BF16), "Vn_bf": ((T, HD), BF16),
        "ctxT_bf": ((D, T), BF16), "den": ((NH, T), F32),
        "x1": ((D, T), F32),
        "thr": ((128, 8), F32),
        "hid_bf": ((H2, T), BF16),
        "y2": ((D, T), F32),
        "x2": ((D, T), F32),
        "Wdt": ((D, D), F32), "Wb": ((D, D), F32), "Wc": ((D, D), F32),
        "biadbc": ((128, 3 * NSTR), F32),
        "a_rawT": ((128, NE * NE), F32), "BbT": ((128, NE * NE), F32),
        "CbT": ((128, NE * NE), F32), "Dt2": ((128, NE), F32),
        "Asb": ((128, NE * NE), F32),
        "dtc": ((D, T), F32), "bco": ((D, T), BF16), "cco": ((D, T), BF16),
        "y3": ((D, T), F32), "x3": ((D, T), F32),
        "sim": ((D, NE), F32), "gates": ((1, NE), F32),
        "QP": ((NE * D, HD), F32), "KP": ((NE * D, HD), F32), "VP": ((NE * D, HD), F32),
        "OP": ((NE * HD, D), F32),
        "rsn_t": ((1, NE), F32), "sg_t": ((1, NE), F32),
        "wTr": ((NE, T), F32),
        "qT_bf": ((HD, T), BF16), "kT_bf": ((HD, T), BF16),
        "Vn2_bf": ((T, HD), BF16), "aoT_bf": ((HD, T), BF16),
        "y4": ((D, T), F32), "outT": ((D, T), F32),
    }


PHASES = [
    ("p0", phase_tables, [], ["cosF", "sinF"]),
    ("p2a", phase_kv, ["xT", "Wk", "Wv", "bk", "bv", "cosF", "sinF"],
     ["KR_bf", "Vn_bf"]),
    ("p2b", phase_qctx, ["xT", "Wq", "bq2", "cosF", "sinF", "KR_bf", "Vn_bf"],
     ["ctxT_bf", "den"]),
    ("p2c", phase_attnout, ["xT", "ctxT_bf", "den", "Wo", "bo2"], ["x1"]),
    ("p3a", phase_gamma, ["w1T", "w2T", "w3T"], ["thr"]),
    ("p3b", phase_ffn1, ["x1", "w1T", "w2T", "thr"], ["hid_bf"]),
    ("p3c", phase_ffn2, ["x1", "w3T", "thr", "hid_bf"], ["y2"]),
    ("p3d", make_phase_ln("y2", "x2", "p3d"), ["y2"], ["x2"]),
    ("p4a", phase_ssmproj, ["x2", "Wdt", "Wb", "Wc", "biadbc", "a_rawT"],
     ["Asb", "dtc", "bco", "cco"]),
    ("p4b", phase_ssmscan, ["x2", "dtc", "bco", "cco", "Asb", "BbT", "CbT", "Dt2"],
     ["x3"]),
    ("p5a", phase_moeroute, ["x3", "sim", "gates"], ["wTr", "rsn_t", "sg_t"]),
    ("p5b", phase_moemix, ["x3", "wTr", "QP", "KP", "VP"],
     ["qT_bf", "kT_bf", "Vn2_bf"]),
    ("p5c", phase_moeattn, ["qT_bf", "kT_bf", "Vn2_bf"], ["aoT_bf"]),
    ("p5d", phase_moeout, ["x3", "wTr", "OP", "aoT_bf"], ["outT"]),
]


def np_dt(dt):
    return {F32: np.float32, BF16: ml_dtypes.bfloat16, I32: np.int32}[dt]


def build_phase_program(name, fn, ins, outs, T):
    nc = bass.Bass("TRN2", target_bir_lowering=False)
    env = Env(nc)
    shp = _shapes(T)
    dram = {}
    for k in ins:
        dram[k] = nc.dram_tensor(k, list(shp[k][0]), shp[k][1], kind="ExternalInput")
    for k in outs:
        dram[k] = nc.dram_tensor(k, list(shp[k][0]), shp[k][1], kind="ExternalOutput")
    fn(env, T, dram)
    nc.finalize()
    return nc


def host_prep(inputs, T):
    def t32(a):
        return np.ascontiguousarray(np.asarray(a, dtype=np.float32))

    x = np.asarray(inputs["L0_attn_x"])
    B = x.shape[0]
    shared = {
        "Wq": t32(inputs["L0_attn_Wq"]),
        "bq2": t32(np.asarray(inputs["L0_attn_bq"]).reshape(NSTR, 128).T),
        "Wk": t32(inputs["L0_attn_Wk"]),
        "bk": t32(np.asarray(inputs["L0_attn_bk"]).reshape(128, 1)),
        "Wv": t32(inputs["L0_attn_Wv"]),
        "bv": t32(np.asarray(inputs["L0_attn_bv"]).reshape(1, 128)),
        "Wo": t32(inputs["L0_attn_Wo"]),
        "bo2": t32(np.asarray(inputs["L0_attn_bo"]).reshape(NSTR, 128).T),
        "w1T": t32(np.asarray(inputs["L0_ffn_w1"]).T),
        "w2T": t32(np.asarray(inputs["L0_ffn_w2"]).T),
        "w3T": t32(np.asarray(inputs["L0_ffn_w3"]).T),
        "Wdt": t32(inputs["L1_attn_Wdt"]),
        "Wb": t32(inputs["L1_attn_Wb"]),
        "Wc": t32(inputs["L1_attn_Wc"]),
        "biadbc": t32(np.concatenate([
            np.asarray(inputs["L1_attn_bdt"]).reshape(NSTR, 128).T,
            np.asarray(inputs["L1_attn_bb"]).reshape(NSTR, 128).T,
            np.asarray(inputs["L1_attn_bc"]).reshape(NSTR, 128).T], axis=1)),
        "a_rawT": t32(np.asarray(inputs["L1_attn_a_raw"]).reshape(NSTR, 128, NE)
                      .transpose(1, 0, 2).reshape(128, NSTR * NE)),
        "BbT": t32(np.asarray(inputs["L1_attn_B_base"]).reshape(NSTR, 128, NE)
                   .transpose(1, 0, 2).reshape(128, NSTR * NE)),
        "CbT": t32(np.asarray(inputs["L1_attn_C_base"]).reshape(NSTR, 128, NE)
                   .transpose(1, 0, 2).reshape(128, NSTR * NE)),
        "Dt2": t32(np.asarray(inputs["L1_attn_D"]).reshape(NSTR, 128).T),
        "sim": t32(inputs["L1_ffn_sim_matrix"]),
        "gates": t32(np.asarray(inputs["L1_ffn_gates"]).reshape(1, NE)),
        "QP": t32(np.asarray(inputs["L1_ffn_q_proj"]).reshape(NE * D, HD)),
        "KP": t32(np.asarray(inputs["L1_ffn_k_proj"]).reshape(NE * D, HD)),
        "VP": t32(np.asarray(inputs["L1_ffn_v_proj"]).reshape(NE * D, HD)),
        "OP": t32(np.asarray(inputs["L1_ffn_o_proj"]).reshape(NE * HD, D)),
    }
    cores = []
    for b in range(B):
        d = dict(shared)
        d["xT"] = t32(x[b].T)
        cores.append(d)
    return cores


def run_phases_multi(state, T, phases=None, core_ids=None):
    phases = phases if phases is not None else PHASES
    core_ids = core_ids if core_ids is not None else list(range(len(state)))
    for (name, fn, ins, outs) in phases:
        nc = build_phase_program(name, fn, ins, outs, T)
        in_maps = [{k: np.ascontiguousarray(state[c][k]) for k in ins}
                   for c in range(len(state))]
        res = run_bass_kernel_spmd(nc, in_maps, core_ids=core_ids)
        for c in range(len(state)):
            for k in outs:
                state[c][k] = res.results[c][k]
    return state


EXT_IN = ["xT", "Wq", "bq2", "Wk", "bk", "Wv", "bv", "Wo", "bo2",
          "w1T", "w2T", "w3T",
          "Wdt", "Wb", "Wc", "biadbc", "a_rawT", "BbT", "CbT", "Dt2",
          "sim", "gates", "QP", "KP", "VP", "OP"]


def build_full_program(T):
    nc = bass.Bass("TRN2", target_bir_lowering=False)
    env = Env(nc)
    shp = _shapes(T)
    dram = {}
    for k, (shape, dt) in shp.items():
        if k in EXT_IN:
            kind = "ExternalInput"
        elif k == "outT":
            kind = "ExternalOutput"
        else:
            kind = "Internal"
        dram[k] = nc.dram_tensor(k, list(shape), dt, kind=kind)
    for (name, fn, ins, outs) in PHASES:
        fn(env, T, dram)
    nc.finalize()
    return nc


def kernel(**inputs):
    x = np.asarray(inputs["L0_attn_x"])
    B, T, _ = x.shape
    state = host_prep(inputs, T)
    if MODE == "multi":
        run_phases_multi(state, T)
        out = np.stack([np.asarray(s["outT"], dtype=np.float32).T for s in state])
        return out
    nc = build_full_program(T)
    in_maps = [{k: np.ascontiguousarray(state[c][k]) for k in EXT_IN}
               for c in range(B)]
    res = run_bass_kernel_spmd(nc, in_maps, core_ids=list(range(B)))
    out = np.stack([np.asarray(res.results[c]["outT"], dtype=np.float32).T
                    for c in range(B)])
    return out

